# version 10
# baseline (speedup 1.0000x reference)
from contextlib import ExitStack

import numpy as np

import concourse.bass as bass
import concourse.mybir as mybir
from concourse.bass_utils import run_bass_kernel_spmd

F32 = mybir.dt.float32
BF16 = mybir.dt.bfloat16
AF = mybir.ActivationFunctionType
ALU = mybir.AluOpType

NCORE = 8
TOK = 2048
D = 1024
H = 16
HD = 64
CH = 128
NT = TOK // CH
PROJ = 2576
NEG = -1.0e30
_DETECT_RACES = False
_DEBUG_DUMP = False


def build_nc():
    nc = bass.Bass("TRN2", target_bir_lowering=False,
                   detect_race_conditions=_DETECT_RACES)

    dx = nc.dram_tensor("x", [TOK, D], F32, kind="ExternalInput").ap()
    dwin = nc.dram_tensor("w_inT", [D, PROJ], F32, kind="ExternalInput").ap()
    dwout = nc.dram_tensor("w_outT", [D, D], F32, kind="ExternalInput").ap()
    ddtw = nc.dram_tensor("dt_WT", [16, 16], F32, kind="ExternalInput").ap()
    ddtb = nc.dram_tensor("dt_b", [16, 1], F32, kind="ExternalInput").ap()
    dalog = nc.dram_tensor("A_log", [16, 1], F32, kind="ExternalInput").ap()
    ddcol = nc.dram_tensor("Dcols", [128, 8], F32, kind="ExternalInput").ap()
    dnwc = nc.dram_tensor("nwcols", [128, 8], F32, kind="ExternalInput").ap()
    dgbc = nc.dram_tensor("gbc", [128, D], F32, kind="ExternalInput").ap()
    dbbc = nc.dram_tensor("bbc", [128, D], F32, kind="ExternalInput").ap()
    drst = nc.dram_tensor("resetm", [16, TOK], F32, kind="ExternalInput").ap()
    dm0 = nc.dram_tensor("m0", [128, 128], F32, kind="ExternalInput").ap()
    dtind = nc.dram_tensor("tileind", [16, TOK], F32, kind="ExternalInput").ap()
    dgrp = nc.dram_tensor("grp4rep", [128, 128], F32, kind="ExternalInput").ap()
    dselA = nc.dram_tensor("selA", [128, 8], F32, kind="ExternalInput").ap()
    dseld = nc.dram_tensor("seld", [128, 8], F32, kind="ExternalInput").ap()
    dinvsd = nc.dram_tensor("invseld", [128, 8], F32,
                            kind="ExternalInput").ap()
    dout = nc.dram_tensor("out", [TOK, D], F32, kind="ExternalOutput").ap()

    ccA_in = nc.dram_tensor("ccA_in", [128, 4 * HD], F32)
    ccA_out = nc.dram_tensor("ccA_out", [NCORE * 128, 4 * HD], F32)
    ccd_in = nc.dram_tensor("ccd_in", [16, 8], F32)
    if _DEBUG_DUMP:
        dbgY = nc.dram_tensor("dbgY", [128, 8 * TOK], BF16)
        dbgS = nc.dram_tensor("dbgS", [128, TOK], BF16)
        dbgE = nc.dram_tensor("dbgE", [128, TOK], BF16)
        dbgR = nc.dram_tensor("dbgR", [128, TOK], F32)
    ccd_out = nc.dram_tensor("ccd_out", [NCORE * 16, 8], F32)

    es = ExitStack()
    sb = lambda n, s, d: es.enter_context(nc.sbuf_tensor(n, s, d))
    psa = lambda n, s, d: es.enter_context(nc.psum_tensor(n, s, d))

    w_in = sb("w_in_sb", [128, 8 * PROJ], BF16)
    fmb = sb("fmbuf", [128, 8 * TOK], BF16)
    xptm = sb("xp_tm", [128, NT * D], BF16)
    ztm = sb("siluz_tm", [128, NT * D], BF16)
    zfm = w_in
    bpadB = w_in[:, 8 * TOK:8 * TOK + 2 * TOK]
    cwb = sb("cwbuf", [128, 8 * D], BF16)
    bpad = sb("bpad2", [128, 2 * TOK], BF16)
    cfm = sb("cfm", [128, 2 * TOK], BF16)
    btm = sb("btm", [128, 2 * 256], BF16)
    packA = sb("packA", [128, TOK], F32)
    scr32 = sb("scr32", [128, TOK], F32)
    snaps = sb("snaps", [128, 16 * 4 * HD], BF16)
    srun = sb("srun", [128, 4 * HD], F32)
    wxb = cwb[:, 5 * D:6 * D]
    scobE = ztm[:, 0:TOK]
    scobO = ztm[:, TOK:2 * TOK]
    onerow = ztm[0:1, TOK:TOK + 128]
    expb = ztm[:, 2 * TOK:3 * TOK]
    eag = ztm[:, 3 * TOK:4 * TOK]
    ub = ztm[:, 4 * TOK:4 * TOK + D]
    vb = ztm[:, 4 * TOK + D:5 * TOK]
    gb = ztm[:, 5 * TOK:5 * TOK + D]
    bb = ztm[:, 5 * TOK + D:6 * TOK]
    fbfc = ztm[:, 6 * TOK:7 * TOK]
    postbuf = sb("postbuf", [128, TOK], F32)
    mob = postbuf[:, 0:D]
    x1b = postbuf[:, D:TOK]
    rhsA = postbuf[0:32, :]
    fbf = fbfc[0:16, :]
    lhsA = sb("lhsA", [32, 128], F32)
    m0b = sb("m0_sb", [128, 128], BF16)
    grpb = sb("grp_sb", [128, 128], F32)
    dtwb = sb("dtw_sb", [16, 16], F32)
    dtbb = sb("dtb_sb", [16, 1], F32)
    alogb = sb("alog_sb", [16, 1], F32)
    acolb = sb("acol_sb", [16, 1], F32)
    dcolb = sb("dcol_sb", [128, 8], F32)
    nwcb = sb("nwc_sb", [128, 8], F32)
    selAb = sb("selA_sb", [128, 8], F32)
    seldb = sb("seld_sb", [128, 8], F32)
    invsdb = sb("invsd_sb", [128, 8], F32)
    rcols = sb("rcols", [128, NT], F32)
    mrcols = sb("mrcols", [128, NT], F32)
    ftm = sb("ftm", [128, NT * 16], BF16)
    e16 = sb("e16", [128, 64], F32)
    sp4 = sb("sp4", [4, 64], F32)
    dcolsb = sb("dcolsb", [128, 64], F32)
    pcolsb = sb("pcolsb", [128, 64], F32)
    dcoreb = sb("dcoreb", [16, 8], F32)
    ddall = sb("ddall", [128, 1], F32)
    drcols = sb("drcols", [128, 32], F32)
    dvb = sb("dvb", [128, 32], F32)
    avb = sb("avb", [128, 4 * HD], F32)
    hbuf = sb("hbuf", [128, 4 * HD], BF16)
    colc = sb("colc", [128, 16], F32)
    onecol = sb("onecol", [128, 1], BF16)
    epsc = sb("epsc", [128, 3], F32)


    xnrot = cwb[:, 4 * D:5 * D]
    xrotL = cwb[:, 6 * D:8 * D].bitcast(F32)
    xrotT = ztm[:, 7 * TOK:8 * TOK].bitcast(F32)

    pfm = psa("pfm", [128, 1024], F32)
    pfm2 = psa("pfm2", [128, 1024], F32)
    pbig = psa("pbig", [128, 2048], F32)

    names = ["SD", "GD", "CD", "TT", "VV", "CC", "OD", "CS", "CT"]
    sems = {n: es.enter_context(nc.semaphore(n)) for n in names}
    cnt = {n: 0 for n in names}

    with es, nc.Block() as blk:
        def inc(ins, name, k=1):
            ins.then_inc(sems[name], k)
            cnt[name] += k

        def w(e, name, val=None):
            e.wait_ge(sems[name], cnt[name] if val is None else val)

        def barrier(e):
            for n in names:
                if cnt[n]:
                    e.wait_ge(sems[n], cnt[n])

        sd_kinds = set()

        def sdk(e, kind):
            if kind in ("tr", "s2s") and (sd_kinds - {kind, "hbm"}):
                e.wait_ge(sems["SD"], cnt["SD"])
                sd_kinds.clear()
            sd_kinds.add(kind)

        def _(e):
            for k in range(8):
                inc(e.dma_start(out=w_in[:, k * PROJ:(k + 1) * PROJ],
                                in_=dwin[k * 128:(k + 1) * 128, :]), "GD", 16)
            inc(e.dma_start(out=m0b[:], in_=dm0), "GD", 16)
        blk.gpsimd(_)
        gd_w = cnt["GD"]

        def _(e):
            sdk(e, "hbm")
            for t_, d_ in ((grpb, dgrp), (dtwb, ddtw),
                           (dtbb, ddtb), (alogb, dalog), (dcolb, ddcol),
                           (nwcb, dnwc), (selAb, dselA), (seldb, dseld),
                           (invsdb, dinvsd)):
                inc(e.dma_start(out=t_[:], in_=d_), "SD", 16)
            inc(e.dma_start(out=rhsA[1:17, :], in_=dtind), "SD", 16)
        blk.sync(_)
        sd_w = cnt["SD"]

        def _(e):
            e.memset(onecol[:], 1.0)
            e.memset(epsc[:, 0:1], 1e-5)
            e.memset(dcoreb[:], 0.0)
            e.memset(e16[32:48, 16:32], 1.0)
            e.memset(epsc[:, 1:2], 1.0)
            e.memset(epsc[:, 2:3], 1e-6)
            e.memset(lhsA[:], 0.0)
            e.memset(lhsA[0:1, :], 1.0)
            e.memset(srun[:], 0.0)
            e.memset(hbuf[:], 0.0)
            inc(e.memset(pbig[:], 0.0), "VV")
        blk.vector(_)
        vv_init = cnt["VV"]

        for t in range(NT):
            def _(e, t=t):
                sdk(e, "hbm")
                if t >= 1:
                    w(e, "VV")
                inc(e.dma_start(out=xrotL[:],
                                in_=dx[t * 128:(t + 1) * 128, :]), "SD", 16)
            blk.sync(_)
            sd_x = cnt["SD"]

            def _(e, t=t, sd_x=sd_x):
                w(e, "SD", sd_x)
                e.tensor_scalar(scr32[:, 0:D], xrotL[:], 1.0, 0.0, ALU.mult,
                                ALU.add, accum_out=colc[:, 0:1])
                e.scalar_tensor_tensor(scr32[:, 0:D], xrotL[:], 1.0, xrotL[:],
                                       ALU.mult, ALU.mult,
                                       accum_out=colc[:, 1:2])
                e.tensor_scalar(colc[:, 2:3], colc[:, 0:1], 1.0 / D, None,
                                ALU.mult)
                e.tensor_scalar(colc[:, 3:4], colc[:, 1:2], 1.0 / D, None,
                                ALU.mult)
                inc(e.scalar_tensor_tensor(colc[:, 4:5], colc[:, 2:3],
                                           colc[:, 2:3], colc[:, 3:4],
                                           ALU.mult, ALU.subtract), "VV")
            blk.vector(_)
            vv_st = cnt["VV"]

            def _(e, t=t, vv_st=vv_st):
                w(e, "VV", vv_st)
                e.activation(colc[:, 5:6], colc[:, 4:5], AF.Ln, bias=epsc[:, 0:1],
                             scale=-1.0)
                inc(e.activation(rcols[:, t:t + 1], colc[:, 5:6], AF.Exp,
                                 bias=0.0, scale=-0.5), "CC")
            blk.scalar(_)
            cc_r = cnt["CC"]

            def _(e, t=t, cc_r=cc_r):
                w(e, "CC", cc_r)
                inc(e.tensor_scalar(mrcols[:, t:t + 1], colc[:, 2:3],
                                    rcols[:, t:t + 1], -1.0, ALU.mult,
                                    ALU.mult), "VV")
            blk.vector(_)

        xn_sd = []
        for t in range(NT):
            def _(e, t=t):
                sdk(e, "hbm")
                if t >= 1:
                    w(e, "CC")
                else:
                    w(e, "VV")
                inc(e.dma_start(out=xrotL[:],
                                in_=dx[t * 128:(t + 1) * 128, :]), "SD", 16)
            blk.sync(_)
            sd_x = cnt["SD"]

            def _(e, t=t, sd_x=sd_x):
                w(e, "SD", sd_x)
                if t >= 2:
                    w(e, "SD")
                inc(e.activation(xnrot[:], xrotL[:], AF.Identity,
                                 bias=mrcols[:, t:t + 1],
                                 scale=rcols[:, t:t + 1]), "CC")
            blk.scalar(_)
            cc_xn = cnt["CC"]

            def _(e, t=t, cc_xn=cc_xn):
                sdk(e, "tr")
                w(e, "CC", cc_xn)
                for k in range(8):
                    inc(e.dma_start(
                        out=fmb[:, k * TOK + t * 128:k * TOK + (t + 1) * 128],
                        in_=xnrot[:, k * 128:(k + 1) * 128], transpose=True),
                        "SD", 16)
            blk.sync(_)
            xn_sd.append(cnt["SD"])

        fm_tt = []
        for mt in range(5):
            for half in range(2):
                pp = pfm if (mt * 2 + half) % 2 == 0 else pfm2
                M = 16 if mt == 4 else 128

                def _(e, mt=mt, half=half, pp=pp, M=M):
                    if mt == 0 and half == 0:
                        w(e, "SD", xn_sd[-1])
                        w(e, "GD", gd_w)
                    if mt * 2 + half >= 2:
                        w(e, "VV")
                        w(e, "CC")
                    for k in range(8):
                        lhs = w_in[:, k * PROJ + 2048 + mt * 128:
                                   k * PROJ + 2048 + mt * 128 + M]
                        for q in range(2):
                            ins = e.matmul(
                                pp[0:M, q * 512:(q + 1) * 512], lhs,
                                fmb[:, k * TOK + half * 1024 + q * 512:
                                    k * TOK + half * 1024 + (q + 1) * 512],
                                start=(k == 0), stop=(k == 7))
                    inc(ins, "TT")
                blk.tensor(_)
                fm_tt.append(cnt["TT"])

                if mt < 2:
                    def _(e, mt=mt, half=half, pp=pp):
                        w(e, "TT", fm_tt[-1])
                        inc(e.tensor_copy(
                            cwb[:, mt * 2048 + half * 1024:
                                mt * 2048 + (half + 1) * 1024], pp[:]), "VV")
                    blk.vector(_)
                elif mt < 4:
                    def _(e, mt=mt, half=half, pp=pp):
                        w(e, "TT", fm_tt[-1])
                        inc(e.copy(
                            cfm[:, (mt - 2) * 2048 + half * 1024:
                                (mt - 2) * 2048 + (half + 1) * 1024],
                            pp[:]), "CC")
                    blk.scalar(_)
                else:
                    def _(e, half=half, pp=pp):
                        w(e, "TT", fm_tt[-1])
                        inc(e.tensor_copy(
                            packA[0:16, half * 1024:(half + 1) * 1024],
                            pp[0:16, :]), "VV")
                    blk.vector(_)
        fm_vv = cnt["VV"]
        fm_cc = cnt["CC"]

        def _(e):
            inc(e.memset(bpad[:], 0.0), "VV")
        blk.vector(_)
        vv_bm = cnt["VV"]

        def _(e):
            sdk(e, "s2s")
            w(e, "VV", vv_bm)
            w(e, "VV", fm_vv)
            for h in range(0, H, 2):
                gp = h // 2
                dst_t = gp // 4
                base = 32 * (gp % 4)
                inc(e.dma_start(
                    out=bpad[base:base + 16, dst_t * TOK:(dst_t + 1) * TOK],
                    in_=cwb[16 * (h % 8):16 * (h % 8) + 16,
                            (h // 8) * 2048:(h // 8 + 1) * 2048]), "SD", 16)
        blk.sync(_)
        sd_f2 = cnt["SD"]


        def _(e):
            w(e, "VV", fm_vv)
            w(e, "SD", sd_w)
            w(e, "VV")
            for q in range(4):
                ins = e.matmul(pbig[0:16, q * 512:(q + 1) * 512], dtwb[:],
                               packA[0:16, q * 512:(q + 1) * 512],
                               start=True, stop=True)
            inc(ins, "TT")
        blk.tensor(_)
        tt_dt = cnt["TT"]

        def _(e):
            w(e, "TT", tt_dt)
            e.activation(scr32[32:48, :], pbig[0:16, :], AF.Abs,
                         bias=dtbb[:], scale=1.0)
            e.activation(scr32[64:80, :], scr32[32:48, :], AF.Exp,
                         bias=0.0, scale=-1.0)
            e.activation(scr32[96:112, :], scr32[64:80, :], AF.Ln,
                         bias=epsc[0:16, 1:2], scale=1.0)
            e.activation(packA[96:112, :], pbig[0:16, :], AF.Relu,
                         bias=dtbb[:], scale=1.0)
            inc(e.activation(acolb[:], alogb[:], AF.Exp, bias=0.0,
                             scale=1.0), "CC")
        blk.scalar(_)
        cc_d1 = cnt["CC"]

        def _(e):
            w(e, "CC", cc_d1)
            e.tensor_add(packA[0:16, :], scr32[96:112, :], packA[96:112, :])
            e.tensor_scalar(acolb[:], acolb[:], -1.0, None, ALU.mult)
            inc(e.tensor_scalar(scr32[32:48, :], packA[0:16, :], acolb[:],
                                None, ALU.mult), "VV")
        blk.vector(_)
        vv_da = cnt["VV"]

        def _(e):
            sdk(e, "hbm")
            w(e, "VV", vv_da)
            inc(e.dma_start(out=packA[96:112, :], in_=drst), "SD", 16)
        blk.sync(_)
        sd_rst = cnt["SD"]

        def _(e):
            w(e, "VV", vv_da)
            inc(e.activation(scr32[64:80, :], packA[0:16, :], AF.Ln,
                             bias=0.0, scale=1.0), "CC")
        blk.scalar(_)
        cc_lndt = cnt["CC"]

        def _(e):
            w(e, "CC", cc_lndt)
            w(e, "SD", sd_rst)
            e.tensor_copy(scr32[96:112, :], scr32[32:48, :])
            inc(e.tensor_tensor_scan(packA[32:48, :], packA[96:112, :],
                                     scr32[96:112, :], 0.0, ALU.mult,
                                     ALU.add), "VV")
        blk.vector(_)
        vv_a = cnt["VV"]

        def _(e):
            w(e, "VV", vv_a)
            e.tensor_copy(scr32[32:48, :], scr32[64:80, :])
            e.tensor_sub(packA[64:80, :], scr32[32:48, :], packA[32:48, :])
            av3 = packA[32:48, :].rearrange("h (c i) -> h c i", c=NT)
            e.tensor_copy(e16[32:48, 0:16].unsqueeze(2), av3[:, :, 127:128])
            bcv = e16[32:48, 0:16].unsqueeze(2).broadcast_to([16, NT, 128])
            e.tensor_tensor(
                scr32[96:112, :].rearrange("h (c i) -> h c i", c=NT), bcv,
                av3, ALU.subtract)
            inc(e.tensor_tensor_scan(e16[32:48, 32:48], e16[32:48, 16:32],
                                     e16[32:48, 0:16], 0.0, ALU.mult,
                                     ALU.add), "VV")
        blk.vector(_)
        vv_pib = cnt["VV"]

        def _(e):
            w(e, "VV", vv_pib)
            inc(e.tensor_sub(e16[32:48, 48:64], e16[32:48, 32:48],
                             e16[32:48, 0:16]), "VV")
        blk.vector(_)
        vv_d2 = cnt["VV"]

        def _(e):
            w(e, "VV", vv_d2)
            e.activation(fbf[:], scr32[96:112, :], AF.Exp, bias=0.0,
                         scale=1.0)
            inc(e.activation(dcoreb[:, 0:1], e16[32:48, 47:48], AF.Exp,
                             bias=0.0, scale=1.0), "CC")
        blk.scalar(_)
        cc_d3 = cnt["CC"]

        def _(e):
            w(e, "CC", cc_d3)
            inc(e.tensor_tensor(fbf[:], packA[0:16, :], fbf[:], ALU.mult),
                "VV")
        blk.vector(_)
        vv_f = cnt["VV"]

        def _(e):
            sdk(e, "tr")
            w(e, "VV", vv_f)
            for t in range(NT):
                inc(e.dma_start(out=ftm[:, t * 16:(t + 1) * 16],
                                in_=fbf[:, t * 128:(t + 1) * 128],
                                transpose=True), "SD", 16)
        blk.sync(_)
        sd_ftm = cnt["SD"]

        def _(e):
            sdk(e, "s2s")
            w(e, "VV", vv_d2)
            for gi in range(4):
                inc(e.dma_start(out=sp4[0:4, 16 * gi:16 * (gi + 1)],
                                in_=e16[32 + 4 * gi:32 + 4 * gi + 4, 0:16]),
                    "SD", 16)
        blk.sync(_)
        sd_ae = cnt["SD"]

        def _(e):
            w(e, "SD", sd_ae)
            w(e, "SD", sd_w)
            for gi in range(4):
                ins = e.matmul(pfm2[:, gi * 16:(gi + 1) * 16],
                               grpb[0:4, :],
                               sp4[0:4, 16 * gi:16 * (gi + 1)],
                               start=True, stop=True)
            inc(ins, "TT")
        blk.tensor(_)
        tt_dc = cnt["TT"]

        def _(e):
            w(e, "TT", tt_dc)
            inc(e.activation(dcolsb[:], pfm2[:, 0:64], AF.Exp, bias=0.0,
                             scale=1.0), "CC")
        blk.scalar(_)
        cc_dcol = cnt["CC"]

        def _(e):
            sdk(e, "s2s")
            w(e, "CC", cc_dcol)
            w(e, "VV", vv_d2)
            for gi in range(4):
                inc(e.dma_start(out=sp4[0:4, 16 * gi:16 * (gi + 1)],
                                in_=e16[32 + 4 * gi:32 + 4 * gi + 4, 48:64]),
                    "SD", 16)
        blk.sync(_)
        sd_pe = cnt["SD"]

        def _(e):
            w(e, "SD", sd_pe)
            for gi in range(4):
                ins = e.matmul(pfm2[:, gi * 16:(gi + 1) * 16],
                               grpb[0:4, :],
                               sp4[0:4, 16 * gi:16 * (gi + 1)],
                               start=True, stop=True)
            inc(ins, "TT")
        blk.tensor(_)
        tt_pc = cnt["TT"]

        def _(e):
            w(e, "TT", tt_pc)
            inc(e.activation(pcolsb[:], pfm2[:, 0:64], AF.Exp, bias=0.0,
                             scale=1.0), "CC")
        blk.scalar(_)
        cc_pcol = cnt["CC"]

        tt_projtm = []
        sd_xpfm = []
        for t in range(NT):
            def _(e, t=t):
                w(e, "SD", xn_sd[t])
                if t == 0:
                    w(e, "GD", gd_w)
                    w(e, "CC", cc_pcol)
                    w(e, "TT", fm_tt[-1])
                    w(e, "SD", sd_ftm)
                else:
                    w(e, "CC")
                for k in range(8):
                    st = fmb[:, k * TOK + t * 128:k * TOK + (t + 1) * 128]
                    for q in range(2):
                        e.matmul(pfm[:, q * 512:(q + 1) * 512], st,
                                 w_in[:, k * PROJ + q * 512:
                                      k * PROJ + (q + 1) * 512],
                                 start=(k == 0), stop=(k == 7))
                    for q in range(2):
                        ins = e.matmul(
                            pfm2[:, q * 512:(q + 1) * 512], st,
                            w_in[:, k * PROJ + 1024 + q * 512:
                                 k * PROJ + 1024 + (q + 1) * 512],
                            start=(k == 0), stop=(k == 7))
                inc(ins, "TT")
            blk.tensor(_)
            tt_projtm.append(cnt["TT"])
            tt_p = cnt["TT"]

            def _(e, t=t, tt_p=tt_p):
                w(e, "TT", tt_p)
                e.copy(xptm[:, t * D:(t + 1) * D], pfm[:])
                inc(e.activation(ztm[:, t * D:(t + 1) * D], pfm2[:], AF.Silu,
                                 bias=0.0, scale=1.0), "CC")
            blk.scalar(_)
            cc_ev = cnt["CC"]

            def _(e, t=t, cc_ev=cc_ev):
                sdk(e, "tr")
                w(e, "CC", cc_ev)
                for k in range(8):
                    inc(e.dma_start(
                        out=fmb[:, k * TOK + t * 128:k * TOK + (t + 1) * 128],
                        in_=xptm[:, t * D + k * 128:t * D + (k + 1) * 128],
                        transpose=True), "SD", 16)
                if t >= 2:
                    w(e, "TT")
                for r in range(2):
                    inc(e.dma_start(
                        out=btm[:, (t % 2) * 256 + r * 128:
                                (t % 2) * 256 + (r + 1) * 128],
                        in_=cwb[:, r * 2048 + t * 128:
                                r * 2048 + (t + 1) * 128],
                        transpose=True), "SD", 16)
            blk.sync(_)
            sd_xpfm.append(cnt["SD"])

            def _(e, t=t, cc_ev=cc_ev):
                w(e, "CC", cc_ev)
                w(e, "SD", sd_ftm)
                fv = ftm[:, t * 16:(t + 1) * 16].unsqueeze(2).broadcast_to(
                    [128, 16, HD])
                xv = xptm[:, t * D:(t + 1) * D].rearrange(
                    "p (h d) -> p h d", h=16)
                inc(e.tensor_tensor(wxb[:].rearrange("p (h d) -> p h d",
                                                     h=16),
                                    xv, fv, ALU.mult), "VV")
            blk.vector(_)
            vv_wx = cnt["VV"]

            def _(e, t=t, vv_wx=vv_wx):
                w(e, "VV", vv_wx)
                w(e, "SD", sd_xpfm[-1])
                if t == 0:
                    w(e, "SD", sd_f2)
                    w(e, "VV", vv_init)
                else:
                    w(e, "VV")
                for h in range(H):
                    gi, m = h // 4, h % 4
                    ins = e.matmul(
                        pbig[32 * m:32 * m + 16, gi * 512:gi * 512 + 64],
                        btm[:, (t % 2) * 256 + (h // 8) * 128 + 16 * (h % 8):
                            (t % 2) * 256 + (h // 8) * 128 + 16 * (h % 8)
                            + 16],
                        wxb[:, h * HD:(h + 1) * HD],
                        start=True, stop=True,
                        tile_position=(0, 32 * m))
                inc(ins, "TT")
            blk.tensor(_)
            tt_cs = cnt["TT"]

            def _(e, t=t, tt_cs=tt_cs):
                w(e, "TT", tt_cs)
                if t == 0:
                    w(e, "CC", cc_dcol)
                for gi in range(4):
                    e.tensor_copy(
                        snaps[:, (t * 4 + gi) * HD:(t * 4 + gi + 1) * HD],
                        srun[:, gi * HD:(gi + 1) * HD])
                    ins = e.scalar_tensor_tensor(
                        srun[:, gi * HD:(gi + 1) * HD],
                        srun[:, gi * HD:(gi + 1) * HD],
                        dcolsb[:, gi * 16 + t:gi * 16 + t + 1],
                        pbig[:, gi * 512:gi * 512 + 64],
                        ALU.mult, ALU.add)
                inc(ins, "VV")
            blk.vector(_)

        def _(e):
            w(e, "VV")
            w(e, "CC", cc_d3)
            inc(e.dma_start(out=ccA_in.ap(), in_=srun[:]), "GD", 16)
            inc(e.dma_start(out=ccd_in.ap(), in_=dcoreb[:]), "GD", 16)
            w(e, "GD")
            inc(e.collective_compute(
                "AllGather", ALU.bypass,
                replica_groups=[list(range(NCORE))],
                ins=[ccA_in.ap().opt()], outs=[ccA_out.ap().opt()]), "CS")
            inc(e.collective_compute(
                "AllGather", ALU.bypass,
                replica_groups=[list(range(NCORE))],
                ins=[ccd_in.ap().opt()], outs=[ccd_out.ap().opt()]), "CT")
            e.wait_ge(sems["CS"], 1)
            e.wait_ge(sems["CT"], 1)
            inc(e.dma_start(out=ddall[:], in_=ccd_out.ap()[:, 0:1],
                            allow_slow_non_contiguous=True), "GD", 16)
        blk.gpsimd(_)
        gd_cc = cnt["GD"]

        def _(e):
            sdk(e, "s2s")
            w(e, "GD", gd_cc)
            for gi in range(4):
                for r in range(NCORE):
                    inc(e.dma_start(
                        out=sp4[0:4, 8 * gi + r:8 * gi + r + 1],
                        in_=ddall[16 * r + 4 * gi:16 * r + 4 * gi + 4, :],
                        allow_slow_non_contiguous=True), "SD", 16)
        blk.sync(_)
        sd_dd = cnt["SD"]

        def _(e):
            w(e, "SD", sd_dd)
            w(e, "CC")
            for gi in range(4):
                ins = e.matmul(pfm2[:, gi * 8:(gi + 1) * 8],
                               grpb[0:4, :],
                               sp4[0:4, 8 * gi:8 * (gi + 1)],
                               start=True, stop=True)
            inc(ins, "TT")
        blk.tensor(_)
        tt_dr = cnt["TT"]

        def _(e):
            w(e, "TT", tt_dr)
            for gi in range(4):
                ins = e.tensor_copy(drcols[:, gi * 8:(gi + 1) * 8],
                                    pfm2[:, gi * 8:(gi + 1) * 8])
            inc(ins, "VV")
        blk.vector(_)
        vv_dr = cnt["VV"]

        def _(e):
            w(e, "VV", vv_dr)
            w(e, "SD", sd_w)
            dv3 = dvb[:].rearrange("p (g r) -> p g r", g=4)
            dr3 = drcols[:].rearrange("p (g r) -> p g r", g=4)
            for r in range(NCORE):
                ins = e.scalar_tensor_tensor(
                    dv3[:, :, r], dr3[:, :, r], seldb[:, r:r + 1],
                    invsdb[:, r:r + 1].broadcast_to([128, 4]),
                    ALU.mult, ALU.add)
            inc(ins, "VV")
        blk.vector(_)
        vv_dv = cnt["VV"]

        for r in range(NCORE):
            def _(e, r=r):
                w(e, "GD", gd_cc)
                if r >= 1:
                    w(e, "VV")
                inc(e.dma_start(
                    out=avb[:],
                    in_=ccA_out.ap()[r * 128:(r + 1) * 128, :]), "GD", 16)
            blk.gpsimd(_)
            gd_av = cnt["GD"]

            def _(e, r=r, gd_av=gd_av):
                w(e, "GD", gd_av)
                if r == 0:
                    w(e, "VV", vv_dv)
                for gi in range(4):
                    e.tensor_scalar(avb[:, gi * HD:(gi + 1) * HD],
                                    avb[:, gi * HD:(gi + 1) * HD],
                                    selAb[:, r:r + 1], None, ALU.mult)
                for gi in range(4):
                    ins = e.scalar_tensor_tensor(
                        hbuf[:, gi * HD:(gi + 1) * HD],
                        hbuf[:, gi * HD:(gi + 1) * HD],
                        dvb[:, gi * 8 + r:gi * 8 + r + 1],
                        avb[:, gi * HD:(gi + 1) * HD],
                        ALU.mult, ALU.add)
                inc(ins, "VV")
            blk.vector(_)
        vv_h = cnt["VV"]

        def _(e):
            w(e, "VV", vv_h)
            w(e, "CC", cc_pcol)
            for t in range(NT):
                for gi in range(4):
                    e.tensor_scalar(avb[:, gi * HD:(gi + 1) * HD],
                                    hbuf[:, gi * HD:(gi + 1) * HD],
                                    pcolsb[:, gi * 16 + t:gi * 16 + t + 1],
                                    None, ALU.mult)
                for gi in range(4):
                    ins = e.tensor_add(
                        snaps[:, (t * 4 + gi) * HD:(t * 4 + gi + 1) * HD],
                        snaps[:, (t * 4 + gi) * HD:(t * 4 + gi + 1) * HD],
                        avb[:, gi * HD:(gi + 1) * HD])
            inc(ins, "VV")
        blk.vector(_)
        vv_snapc = cnt["VV"]

        def _(e):
            sdk(e, "tr")
            w(e, "CC")
            w(e, "TT", tt_projtm[-1])
            for t in range(NT):
                for k in range(8):
                    inc(e.dma_start(
                        out=zfm[:, k * TOK + t * 128:k * TOK + (t + 1) * 128],
                        in_=ztm[:, t * D + k * 128:t * D + (k + 1) * 128],
                        transpose=True), "SD", 16)
        blk.sync(_)
        sd_zfm = cnt["SD"]

        def _(e):
            w(e, "TT", tt_projtm[-1])
            inc(e.memset(bpadB[:], 0.0), "VV")
        blk.vector(_)
        vv_bpB = cnt["VV"]

        def _(e):
            sdk(e, "s2s")
            w(e, "VV", vv_bpB)
            for h in range(1, H, 2):
                gp = h // 2
                dst_t = gp // 4
                base = 32 * (gp % 4) + 16
                inc(e.dma_start(
                    out=bpadB[base:base + 16, dst_t * TOK:(dst_t + 1) * TOK],
                    in_=cwb[16 * (h % 8):16 * (h % 8) + 16,
                            (h // 8) * 2048:(h // 8 + 1) * 2048]), "SD", 16)
        blk.sync(_)
        sd_bpB = cnt["SD"]

        def _(e):
            w(e, "SD", sd_f2)
            w(e, "SD", sd_bpB)
            w(e, "SD", xn_sd[-1])
            w(e, "CC")
            w(e, "VV", fm_vv)
            w(e, "VV")
            inc(e.memset(cwb[:], 0.0), "VV")
        blk.vector(_)
        vv_cm = cnt["VV"]

        def _(e):
            sdk(e, "s2s")
            w(e, "VV", vv_cm)
            w(e, "CC", fm_cc)
            for h in range(H):
                inc(e.dma_start(
                    out=cwb[32 * (h % 4):32 * (h % 4) + 16,
                            (h // 4) * 2048:(h // 4 + 1) * 2048],
                    in_=cfm[16 * (h % 8):16 * (h % 8) + 16,
                            (h // 8) * 2048:(h // 8 + 1) * 2048]), "SD", 16)
            w(e, "CC")
            for gi in range(4):
                inc(e.dma_start(out=scr32[32 * gi:32 * gi + 4, :],
                                in_=packA[32 + 4 * gi:32 + 4 * gi + 4, :]),
                    "SD", 16)
        blk.sync(_)
        sd_cp = cnt["SD"]

        for gi in range(4):
            def _(e, gi=gi):
                w(e, "SD", sd_cp)
                w(e, "SD", sd_zfm)
                if gi == 0:
                    w(e, "VV", vv_dr)
                    w(e, "CC")
                else:
                    w(e, "CC")
                for q in range(4):
                    pp = pfm2 if q < 2 else pfm
                    ins = e.matmul(pp[:, (q % 2) * 512:(q % 2 + 1) * 512],
                                   grpb[32 * gi:32 * gi + 4, :],
                                   scr32[32 * gi:32 * gi + 4,
                                         q * 512:(q + 1) * 512],
                                   start=True, stop=True,
                                   tile_position=(32 * gi, 0))
                inc(ins, "TT")
            blk.tensor(_)
            tt_ea = cnt["TT"]

            def _(e, gi=gi, tt_ea=tt_ea):
                w(e, "TT", tt_ea)
                e.activation(eag[:, 0:512], pfm2[:, 0:512], AF.Exp,
                             bias=0.0, scale=1.0)
                e.activation(eag[:, 512:1024], pfm2[:, 512:1024], AF.Exp,
                             bias=0.0, scale=1.0)
                e.activation(eag[:, 1024:1536], pfm[:, 0:512], AF.Exp,
                             bias=0.0, scale=1.0)
                inc(e.activation(eag[:, 1536:2048], pfm[:, 512:1024], AF.Exp,
                                 bias=0.0, scale=1.0), "CC")
            blk.scalar(_)
            cc_ea = cnt["CC"]

            def _(e, gi=gi, cc_ea=cc_ea):
                w(e, "CC", cc_ea)
                w(e, "SD", sd_cp)
                inc(e.tensor_tensor(cwb[:, gi * 2048:(gi + 1) * 2048],
                                    cwb[:, gi * 2048:(gi + 1) * 2048],
                                    eag[:], ALU.mult), "VV")
            blk.vector(_)
        vv_cpr = cnt["VV"]

        tt_lastA = [0]
        for h in range(H):
            pair, odd = h // 2, h % 2
            scob = scobO if odd else scobE

            def _(e, h=h):
                sdk(e, "s2s")
                if h >= 1:
                    w(e, "TT", tt_lastA[0])
                else:
                    w(e, "VV")
                inc(e.dma_start(out=lhsA[1:17, :],
                                in_=packA[64 + h:65 + h, :]), "SD", 16)
                inc(e.dma_start(out=rhsA[0:1, :],
                                in_=packA[32 + h:33 + h, :]), "SD", 16)
            blk.sync(_)
            sd_ab = cnt["SD"]

            for q in range(4):
                def _(e, h=h, q=q, sd_ab=sd_ab):
                    w(e, "SD", sd_ab)
                    w(e, "VV")
                    inc(e.matmul(pfm[:, (q % 2) * 512:(q % 2 + 1) * 512],
                                 lhsA[0:17, :],
                                 rhsA[0:17, q * 512:(q + 1) * 512],
                                 start=True, stop=True), "TT")
                blk.tensor(_)
                tt_a = cnt["TT"]
                if q == 3:
                    tt_lastA[0] = tt_a

                def _(e, h=h, q=q, tt_a=tt_a):
                    w(e, "TT", tt_a)
                    if h == 0 and q == 0:
                        w(e, "GD", gd_w)
                    m0v = m0b[:].unsqueeze(1).broadcast_to([128, 4, 128])
                    inc(e.tensor_tensor(
                        scr32[:, q * 512:(q + 1) * 512].rearrange(
                            "p (a b) -> p a b", a=4),
                        pfm[:, (q % 2) * 512:(q % 2 + 1) * 512].rearrange(
                            "p (a b) -> p a b", a=4),
                        m0v, ALU.add), "VV")
                blk.vector(_)
            vv_arg = cnt["VV"]

            def _(e, h=h, vv_arg=vv_arg):
                w(e, "VV", vv_arg)
                e.activation(expb[:, 0:1024], scr32[:, 0:1024], AF.Exp,
                             bias=0.0, scale=1.0)
                inc(e.activation(expb[:, 1024:2048], scr32[:, 1024:2048],
                                 AF.Exp, bias=0.0, scale=1.0), "CC")
            blk.scalar(_)
            cc_exp = cnt["CC"]

            def _(e, h=h):
                if h == 0:
                    w(e, "SD", sd_cp)
                    w(e, "VV", vv_cpr)
                w(e, "VV")
                gp = h // 2
                base = 32 * (gp % 4)
                dst_t = gp // 4
                bp = bpad if h % 2 == 0 else bpadB
                for T in range(NT):
                    ins = e.matmul(
                        pbig[:, T * 128:(T + 1) * 128],
                        bp[base:base + 32,
                           dst_t * TOK + T * 128:dst_t * TOK +
                           (T + 1) * 128],
                        cfm[base:base + 32,
                            dst_t * TOK + T * 128:dst_t * TOK +
                            (T + 1) * 128],
                        start=True, stop=True, tile_position=(base, 0))
                inc(ins, "TT")
            blk.tensor(_)
            tt_raw = cnt["TT"]

            def _(e, h=h, tt_raw=tt_raw, cc_exp=cc_exp, scob=scob):
                w(e, "TT", tt_raw)
                w(e, "CC", cc_exp)
                e.tensor_tensor(scob[:, 0:1024], pbig[:, 0:1024],
                                expb[:, 0:1024], ALU.mult)
                inc(e.tensor_tensor(scob[:, 1024:2048], pbig[:, 1024:2048],
                                    expb[:, 1024:2048], ALU.mult), "VV")
            blk.vector(_)
            vv_sco = cnt["VV"]

            if _DEBUG_DUMP and h == 0:
                def _(e, vv_sco=vv_sco):
                    w(e, "VV", vv_sco)
                    inc(e.dma_start(out=dbgS.ap(), in_=scobE[:]), "SD", 16)
                    inc(e.dma_start(out=dbgE.ap(), in_=expb[:]), "SD", 16)
                blk.sync(_)
                def _(e, vv_sco=vv_sco):
                    w(e, "VV", vv_sco)
                    inc(e.tensor_copy(scr32[:], pbig[:]), "VV")
                blk.vector(_)
                vv_dbg = cnt["VV"]
                def _(e, vv_dbg=vv_dbg):
                    w(e, "VV", vv_dbg)
                    inc(e.dma_start(out=dbgR.ap(), in_=scr32[:]), "SD", 16)
                blk.sync(_)

            if odd:
                def _(e, pair=pair, vv_sco=vv_sco):
                    w(e, "VV", vv_sco)
                    w(e, "VV", vv_snapc)
                    for hh in (2 * pair, 2 * pair + 1):
                        par = hh % 2
                        sc = scobO if par else scobE
                        base = 32 * (hh % 4)
                        gtile = hh // 4
                        for T in range(NT):
                            ins = e.matmul(
                                pbig[64 * par:64 * par + 64,
                                     T * 128:(T + 1) * 128],
                                xptm[:, T * D + hh * HD:
                                     T * D + (hh + 1) * HD],
                                sc[:, T * 128:(T + 1) * 128],
                                start=(T % 4 == 0), stop=False,
                                tile_position=(0, 64 * par))
                        inc(ins, "TT")
                        w(e, "TT")
                        for T in range(NT):
                            ins = e.matmul(
                                pbig[64 * par:64 * par + 64,
                                     T * 128:(T + 1) * 128],
                                snaps[base:base + 32,
                                      (T * 4 + gtile) * HD:
                                      (T * 4 + gtile + 1) * HD],
                                cwb[base:base + 32,
                                    gtile * 2048 + T * 128:
                                    gtile * 2048 + (T + 1) * 128],
                                start=False, stop=(T % 4 == 3),
                                tile_position=(32 * (hh % 4), 64 * par))
                        inc(ins, "TT")
                        w(e, "TT")
                blk.tensor(_)
                tt_y = cnt["TT"]

                def _(e, pair=pair, tt_y=tt_y):
                    w(e, "TT", tt_y)
                    if pair == 0:
                        w(e, "SD", sd_xpfm[-1])
                    ins = e.scalar_tensor_tensor(
                        fmb[:, pair * TOK:(pair + 1) * TOK],
                        fmb[:, pair * TOK:(pair + 1) * TOK],
                        dcolb[:, pair:pair + 1], pbig[:],
                        ALU.mult, ALU.add)
                    inc(ins, "VV")
                blk.vector(_)
        vv_y = cnt["VV"]

        if _DEBUG_DUMP:
            def _(e):
                w(e, "VV")
                inc(e.dma_start(out=dbgY.ap(), in_=fmb[:]), "SD", 16)
            blk.sync(_)

        for gi in range(8):
            def _(e, gi=gi):
                if gi == 0:
                    w(e, "VV", vv_y)
                else:
                    w(e, "TT")
                inc(e.activation(expb[:], fmb[:, gi * TOK:(gi + 1) * TOK],
                                 AF.Square, bias=0.0, scale=1.0), "CC")
            blk.scalar(_)
            cc_sq = cnt["CC"]

            def _(e, gi=gi, cc_sq=cc_sq):
                w(e, "CC", cc_sq)
                if gi == 0:
                    w(e, "VV")
                for q in range(4):
                    ins = e.matmul(pbig[0:1, q * 512:(q + 1) * 512],
                                   onecol[:], expb[:, q * 512:(q + 1) * 512],
                                   start=(gi == 0), stop=(gi == 7))
                inc(ins, "TT")
            blk.tensor(_)
        tt_rms = cnt["TT"]

        def _(e):
            w(e, "TT", tt_rms)
            e.activation(scr32[0:1, 0:TOK], pbig[0:1, :], AF.Ln,
                         bias=epsc[0:1, 2:3], scale=1.0 / D)
            inc(e.activation(expb[0:1, 0:TOK], scr32[0:1, 0:TOK], AF.Exp,
                             bias=0.0, scale=-0.5), "CC")
        blk.scalar(_)
        cc_rms = cnt["CC"]

        def _(e):
            w(e, "CC", cc_rms)
            inc(e.memset(onerow[:], 1.0), "VV")
        blk.vector(_)
        vv_or = cnt["VV"]

        for half in range(2):
            def _(e, half=half):
                w(e, "CC", cc_rms)
                w(e, "VV", vv_or)
                if half == 1:
                    w(e, "VV")
                for q in range(2):
                    ins = e.matmul(
                        pfm2[:, q * 512:(q + 1) * 512], onerow[:],
                        expb[0:1, half * 1024 + q * 512:
                             half * 1024 + (q + 1) * 512],
                        start=True, stop=True)
                inc(ins, "TT")
            blk.tensor(_)
            tt_rb = cnt["TT"]

            def _(e, half=half, tt_rb=tt_rb):
                w(e, "TT", tt_rb)
                inc(e.tensor_copy(scobE[:, half * 1024:(half + 1) * 1024],
                                  pfm2[:]), "VV")
            blk.vector(_)
        vv_rbc = cnt["VV"]

        for gi in range(8):
            def _(e, gi=gi):
                w(e, "VV", vv_rbc)
                w(e, "CC")
                if gi == 0:
                    w(e, "SD", sd_zfm)
                e.scalar_tensor_tensor(
                    fmb[:, gi * TOK:(gi + 1) * TOK],
                    fmb[:, gi * TOK:(gi + 1) * TOK],
                    nwcb[:, gi:gi + 1],
                    zfm[:, gi * TOK:(gi + 1) * TOK], ALU.mult, ALU.mult)
                inc(e.tensor_tensor(fmb[:, gi * TOK:(gi + 1) * TOK],
                                    fmb[:, gi * TOK:(gi + 1) * TOK],
                                    scobE[:], ALU.mult), "VV")
            blk.vector(_)
        vv_gate = cnt["VV"]

        def _(e):
            w(e, "TT")
            w(e, "SD", sd_zfm)
            for k in range(8):
                inc(e.dma_start(out=cwb[:, k * D:(k + 1) * D],
                                in_=dwout[k * 128:(k + 1) * 128, :]),
                    "GD", 16)
            inc(e.dma_start(out=gb[:], in_=dgbc), "GD", 16)
            inc(e.dma_start(out=bb[:], in_=dbbc), "GD", 16)
        blk.gpsimd(_)
        gd_wo = cnt["GD"]

        for t in range(NT):
            pmo = pfm if t % 2 == 0 else pfm2

            def _(e, t=t):
                if t == 0:
                    w(e, "CC")
                else:
                    w(e, "VV")
                inc(e.dma_start(out=xrotT[:],
                                in_=dx[t * 128:(t + 1) * 128, :]), "CD", 16)
            blk.scalar(_)
            cd_x = cnt["CD"]

            def _(e, t=t, pmo=pmo):
                w(e, "GD", gd_wo)
                w(e, "VV", vv_gate)
                if t >= 2:
                    w(e, "VV")
                for k in range(8):
                    st = fmb[:, k * TOK + t * 128:k * TOK + (t + 1) * 128]
                    for q in range(2):
                        ins = e.matmul(pmo[:, q * 512:(q + 1) * 512], st,
                                       cwb[:, k * D + q * 512:
                                           k * D + (q + 1) * 512],
                                       start=(k == 0), stop=(k == 7))
                inc(ins, "TT")
            blk.tensor(_)
            tt_mo = cnt["TT"]

            def _(e, t=t, tt_mo=tt_mo, pmo=pmo):
                w(e, "TT", tt_mo)
                if t >= 1:
                    w(e, "CC")
                inc(e.tensor_scalar(mob[:], pmo[:], 1.0, 0.0, ALU.mult,
                                    ALU.add, accum_out=colc[:, 0:1]), "VV")
            blk.vector(_)
            vv_mo = cnt["VV"]

            def _(e, t=t, vv_mo=vv_mo):
                w(e, "VV", vv_mo)
                inc(e.activation(scr32[:, 0:D], mob[:], AF.Square, bias=0.0,
                                 scale=1.0, accum_out=colc[:, 1:2]), "CC")
            blk.scalar(_)
            cc_s2 = cnt["CC"]

            def _(e, t=t, cc_s2=cc_s2):
                w(e, "CC", cc_s2)
                e.tensor_scalar(colc[:, 2:3], colc[:, 0:1], 1.0 / D, None,
                                ALU.mult)
                e.tensor_scalar(colc[:, 3:4], colc[:, 1:2], 1.0 / D, None,
                                ALU.mult)
                inc(e.scalar_tensor_tensor(colc[:, 4:5], colc[:, 2:3],
                                           colc[:, 2:3], colc[:, 3:4],
                                           ALU.mult, ALU.subtract), "VV")
            blk.vector(_)
            vv_c1 = cnt["VV"]

            def _(e, t=t, vv_c1=vv_c1, cd_x=cd_x):
                w(e, "VV", vv_c1)
                e.activation(colc[:, 5:6], colc[:, 4:5], AF.Ln, bias=epsc[:, 0:1],
                             scale=-1.0)
                inc(e.activation(colc[:, 6:7], colc[:, 5:6], AF.Exp,
                                 bias=0.0, scale=-0.5), "CC")
            blk.scalar(_)
            cc_r2 = cnt["CC"]

            def _(e, t=t, cc_r2=cc_r2):
                w(e, "CC", cc_r2)
                inc(e.tensor_scalar(colc[:, 7:8], colc[:, 2:3],
                                    colc[:, 6:7], -1.0, ALU.mult, ALU.mult),
                    "VV")
            blk.vector(_)
            vv_mr2 = cnt["VV"]

            def _(e, t=t, vv_mr2=vv_mr2):
                w(e, "VV", vv_mr2)
                inc(e.activation(scr32[:, D:2 * D], mob[:], AF.Identity,
                                 bias=colc[:, 7:8], scale=colc[:, 6:7]),
                    "CC")
            blk.scalar(_)
            cc_mon = cnt["CC"]

            def _(e, t=t, cc_mon=cc_mon, cd_x=cd_x):
                w(e, "CC", cc_mon)
                w(e, "CD", cd_x)
                w(e, "OD")
                inc(e.scalar_tensor_tensor(x1b[:], scr32[:, D:2 * D], 1.0,
                                           xrotT[:], ALU.mult, ALU.add,
                                           accum_out=colc[:, 0:1]), "VV")
            blk.vector(_)
            vv_x1 = cnt["VV"]

            def _(e, t=t, vv_x1=vv_x1):
                w(e, "VV", vv_x1)
                inc(e.activation(scr32[:, 0:D], x1b[:], AF.Square, bias=0.0,
                                 scale=1.0, accum_out=colc[:, 1:2]), "CC")
            blk.scalar(_)
            cc_sx2 = cnt["CC"]

            def _(e, t=t, cc_sx2=cc_sx2):
                w(e, "CC", cc_sx2)
                e.tensor_scalar(colc[:, 2:3], colc[:, 0:1], 1.0 / D, None,
                                ALU.mult)
                e.tensor_scalar(colc[:, 3:4], colc[:, 1:2], 1.0 / D, None,
                                ALU.mult)
                inc(e.scalar_tensor_tensor(colc[:, 4:5], colc[:, 2:3],
                                           colc[:, 2:3], colc[:, 3:4],
                                           ALU.mult, ALU.subtract), "VV")
            blk.vector(_)
            vv_c3 = cnt["VV"]

            def _(e, t=t, vv_c3=vv_c3):
                w(e, "VV", vv_c3)
                e.activation(colc[:, 5:6], colc[:, 4:5], AF.Ln, bias=epsc[:, 0:1],
                             scale=-1.0)
                e.activation(colc[:, 6:7], colc[:, 5:6], AF.Exp,
                             bias=0.0, scale=-0.5)
                inc(e.activation(colc[:, 8:9], colc[:, 5:6], AF.Exp,
                                 bias=0.0, scale=-1.0), "CC")
            blk.scalar(_)
            cc_r3 = cnt["CC"]

            def _(e, t=t, cc_r3=cc_r3):
                w(e, "CC", cc_r3)
                e.tensor_scalar(colc[:, 7:8], colc[:, 2:3], colc[:, 6:7],
                                -1.0, ALU.mult, ALU.mult)
                inc(e.tensor_scalar(colc[:, 9:10], colc[:, 4:5],
                                    colc[:, 8:9], -1.0, ALU.mult, ALU.mult),
                    "VV")
            blk.vector(_)
            vv_vu = cnt["VV"]

            def _(e, t=t, vv_vu=vv_vu):
                w(e, "VV", vv_vu)
                e.activation(ub[:], x1b[:], AF.Identity, bias=colc[:, 7:8],
                             scale=colc[:, 6:7])
                e.activation(colc[:, 10:11], colc[:, 9:10], AF.Ln,
                             bias=epsc[:, 0:1], scale=1.0)
                inc(e.activation(colc[:, 11:12], colc[:, 10:11], AF.Exp,
                                 bias=0.0, scale=-0.5), "CC")
            blk.scalar(_)
            cc_u = cnt["CC"]

            def _(e, t=t, cc_u=cc_u):
                w(e, "CC", cc_u)
                w(e, "GD", gd_wo)
                e.scalar_tensor_tensor(vb[:], ub[:], colc[:, 11:12], gb[:],
                                       ALU.mult, ALU.mult)
                inc(e.scalar_tensor_tensor(vb[:], vb[:], 1.0, bb[:],
                                           ALU.mult, ALU.add,
                                           accum_out=colc[:, 0:1]), "VV")
            blk.vector(_)
            vv_v = cnt["VV"]

            def _(e, t=t, vv_v=vv_v):
                w(e, "VV", vv_v)
                inc(e.activation(scr32[:, 0:D], vb[:], AF.Square, bias=0.0,
                                 scale=1.0, accum_out=colc[:, 1:2]), "CC")
            blk.scalar(_)
            cc_sv2 = cnt["CC"]

            def _(e, t=t, cc_sv2=cc_sv2):
                w(e, "CC", cc_sv2)
                e.tensor_scalar(colc[:, 2:3], colc[:, 0:1], 1.0 / D, None,
                                ALU.mult)
                e.tensor_scalar(colc[:, 3:4], colc[:, 1:2], 1.0 / D, None,
                                ALU.mult)
                inc(e.scalar_tensor_tensor(colc[:, 4:5], colc[:, 2:3],
                                           colc[:, 2:3], colc[:, 3:4],
                                           ALU.mult, ALU.subtract), "VV")
            blk.vector(_)
            vv_c4 = cnt["VV"]

            def _(e, t=t, vv_c4=vv_c4):
                w(e, "VV", vv_c4)
                e.activation(colc[:, 5:6], colc[:, 4:5], AF.Ln, bias=epsc[:, 0:1],
                             scale=-1.0)
                inc(e.activation(colc[:, 6:7], colc[:, 5:6], AF.Exp,
                                 bias=0.0, scale=-0.5), "CC")
            blk.scalar(_)
            cc_r4 = cnt["CC"]

            def _(e, t=t, cc_r4=cc_r4):
                w(e, "CC", cc_r4)
                inc(e.tensor_scalar(colc[:, 7:8], colc[:, 2:3],
                                    colc[:, 6:7], -1.0, ALU.mult, ALU.mult),
                    "VV")
            blk.vector(_)
            vv_mr4 = cnt["VV"]

            def _(e, t=t, vv_mr4=vv_mr4):
                w(e, "VV", vv_mr4)
                inc(e.activation(scr32[:, D:2 * D], vb[:], AF.Identity,
                                 bias=colc[:, 7:8], scale=colc[:, 6:7]),
                    "CC")
            blk.scalar(_)
            cc_vn = cnt["CC"]

            def _(e, t=t, cc_vn=cc_vn):
                w(e, "CC", cc_vn)
                inc(e.tensor_add(x1b[:], x1b[:], scr32[:, D:2 * D]), "VV")
            blk.vector(_)
            vv_out = cnt["VV"]

            def _(e, t=t, vv_out=vv_out):
                sdk(e, "hbm")
                w(e, "VV", vv_out)
                inc(e.dma_start(out=dout[t * 128:(t + 1) * 128, :],
                                in_=x1b[:]), "OD", 16)
            blk.sync(_)

        def _(e):
            w(e, "OD")
        blk.sync(_)

    return nc


_CACHE = {}


def _host_prep(inputs):
    x = np.asarray(inputs["L0_attn_x"], dtype=np.float32).reshape(16384, D)
    w_inT = np.ascontiguousarray(
        np.asarray(inputs["L0_attn_W_in"], dtype=np.float32).T)
    w_outT = np.ascontiguousarray(
        np.asarray(inputs["L0_attn_W_out"], dtype=np.float32).T)
    dt_WT = np.ascontiguousarray(
        np.asarray(inputs["L0_attn_dt_W"], dtype=np.float32).T)
    dt_b = np.asarray(inputs["L0_attn_dt_b"],
                      dtype=np.float32).reshape(16, 1)
    A_log = np.asarray(inputs["L0_attn_A_log"],
                       dtype=np.float32).reshape(16, 1)
    Dp = np.asarray(inputs["L0_attn_D_param"], dtype=np.float32)
    nw = np.asarray(inputs["L0_attn_norm_w"], dtype=np.float32)
    ln_g = np.asarray(inputs["L0_ffn_ln_g"], dtype=np.float32)
    ln_b = np.asarray(inputs["L0_ffn_ln_b"], dtype=np.float32)

    Dcols = np.empty((128, 8), np.float32)
    for g in range(8):
        Dcols[0:64, g] = Dp[2 * g]
        Dcols[64:128, g] = Dp[2 * g + 1]
    nwcols = nw.reshape(8, 128).T.copy()
    gbc = np.broadcast_to(ln_g, (128, D)).copy()
    bbc = np.broadcast_to(ln_b, (128, D)).copy()

    resetm = np.ones((16, TOK), np.float32)
    resetm[:, ::CH] = 0.0
    jj, ii = np.meshgrid(np.arange(128), np.arange(128), indexing="ij")
    m0 = np.where(ii >= jj, 0.0, NEG).astype(np.float32)
    tileind = np.zeros((16, TOK), np.float32)
    for t in range(16):
        tileind[t, t * 128:(t + 1) * 128] = 1.0
    grp4rep = np.zeros((128, 128), np.float32)
    for g in range(4):
        for m in range(4):
            grp4rep[32 * g + m, 32 * m:32 * (m + 1)] = 1.0

    common = dict(w_inT=w_inT, w_outT=w_outT, dt_WT=dt_WT, dt_b=dt_b,
                  A_log=A_log, Dcols=Dcols, nwcols=nwcols, gbc=gbc, bbc=bbc,
                  resetm=resetm, m0=m0, tileind=tileind, grp4rep=grp4rep)
    in_maps = []
    for c in range(NCORE):
        selA = np.zeros((128, 8), np.float32)
        selA[:, :c] = 1.0
        seld = np.zeros((128, 8), np.float32)
        seld[:, :c] = 1.0
        invseld = 1.0 - seld
        m = dict(common)
        m["x"] = np.ascontiguousarray(x[c * TOK:(c + 1) * TOK])
        m["selA"] = selA
        m["seld"] = seld
        m["invseld"] = invseld
        in_maps.append(m)
    return in_maps


def kernel(**inputs):
    if "nc" not in _CACHE:
        _CACHE["nc"] = build_nc()
    nc = _CACHE["nc"]
    in_maps = _host_prep(inputs)
    res = run_bass_kernel_spmd(nc, in_maps, core_ids=list(range(NCORE)))
    out = np.concatenate([res.results[c]["out"] for c in range(NCORE)],
                         axis=0)
    return out.reshape(1, 16384, D).astype(np.float32)


if __name__ == "__main__":
    build_nc()
    print("build ok")


# version 12
# speedup vs baseline: 1.1578x; 1.1578x over previous
from contextlib import ExitStack

import numpy as np

import concourse.bass as bass
import concourse.mybir as mybir
from concourse.bass_utils import run_bass_kernel_spmd

F32 = mybir.dt.float32
BF16 = mybir.dt.bfloat16
AF = mybir.ActivationFunctionType
ALU = mybir.AluOpType

NCORE = 8
TOK = 2048
D = 1024
H = 16
HD = 64
CH = 128
NT = TOK // CH
PROJ = 2576
NEG = -1.0e30
_DETECT_RACES = False
_DEBUG_DUMP = False


def build_nc():
    nc = bass.Bass("TRN2", target_bir_lowering=False,
                   detect_race_conditions=_DETECT_RACES)

    dx = nc.dram_tensor("x", [TOK, D], F32, kind="ExternalInput").ap()
    dwin = nc.dram_tensor("w_inT", [D, PROJ], F32, kind="ExternalInput").ap()
    dwout = nc.dram_tensor("w_outT", [D, D], F32, kind="ExternalInput").ap()
    ddtw = nc.dram_tensor("dt_WT", [16, 16], F32, kind="ExternalInput").ap()
    ddtb = nc.dram_tensor("dt_b", [16, 1], F32, kind="ExternalInput").ap()
    dalog = nc.dram_tensor("A_log", [16, 1], F32, kind="ExternalInput").ap()
    ddcol = nc.dram_tensor("Dcols", [128, 8], F32, kind="ExternalInput").ap()
    dnwc = nc.dram_tensor("nwcols", [128, 8], F32, kind="ExternalInput").ap()
    dgbc = nc.dram_tensor("gbc", [128, D], F32, kind="ExternalInput").ap()
    dbbc = nc.dram_tensor("bbc", [128, D], F32, kind="ExternalInput").ap()
    drst = nc.dram_tensor("resetm", [16, TOK], F32, kind="ExternalInput").ap()
    dm0 = nc.dram_tensor("m0", [128, 128], F32, kind="ExternalInput").ap()
    dtind = nc.dram_tensor("tileind", [16, TOK], F32, kind="ExternalInput").ap()
    dgrp = nc.dram_tensor("grp4rep", [128, 128], F32, kind="ExternalInput").ap()
    dselA = nc.dram_tensor("selA", [128, 8], F32, kind="ExternalInput").ap()
    dseld = nc.dram_tensor("seld", [128, 8], F32, kind="ExternalInput").ap()
    dinvsd = nc.dram_tensor("invseld", [128, 8], F32,
                            kind="ExternalInput").ap()
    dout = nc.dram_tensor("out", [TOK, D], F32, kind="ExternalOutput").ap()

    ccA_in = nc.dram_tensor("ccA_in", [128, 4 * HD], F32)
    ccA_out = nc.dram_tensor("ccA_out", [NCORE * 128, 4 * HD], F32)
    ccd_in = nc.dram_tensor("ccd_in", [16, 8], F32)
    if _DEBUG_DUMP:
        dbgY = nc.dram_tensor("dbgY", [128, 8 * TOK], BF16)
        dbgS = nc.dram_tensor("dbgS", [128, TOK], BF16)
        dbgE = nc.dram_tensor("dbgE", [128, TOK], BF16)
        dbgR = nc.dram_tensor("dbgR", [128, TOK], F32)
    ccd_out = nc.dram_tensor("ccd_out", [NCORE * 16, 8], F32)

    es = ExitStack()
    sb = lambda n, s, d: es.enter_context(nc.sbuf_tensor(n, s, d))
    psa = lambda n, s, d: es.enter_context(nc.psum_tensor(n, s, d))

    w_in = sb("w_in_sb", [128, 8 * PROJ], BF16)
    fmb = sb("fmbuf", [128, 8 * TOK], BF16)
    xptm = sb("xp_tm", [128, NT * D], BF16)
    ztm = sb("siluz_tm", [128, NT * D], BF16)
    zfm = w_in
    bpadB = w_in[:, 8 * TOK:8 * TOK + 2 * TOK]
    cwb = sb("cwbuf", [128, 8 * D], BF16)
    bpad = sb("bpad2", [128, 2 * TOK], BF16)
    cfm = sb("cfm", [128, 2 * TOK], BF16)
    btm = sb("btm", [128, 2 * 256], BF16)
    packA = sb("packA", [128, TOK], F32)
    scr32 = sb("scr32", [128, TOK], F32)
    snaps = sb("snaps", [128, 16 * 4 * HD], BF16)
    srun = sb("srun", [128, 4 * HD], F32)
    wxb = cwb[:, 5 * D:6 * D]
    scobE = ztm[:, 0:TOK]
    scobO = ztm[:, TOK:2 * TOK]
    onerow = ztm[0:1, TOK:TOK + 128]
    expb = ztm[:, 2 * TOK:3 * TOK]
    eag = ztm[:, 3 * TOK:4 * TOK]
    ub = ztm[:, 4 * TOK:4 * TOK + D]
    vb = ztm[:, 4 * TOK + D:5 * TOK]
    gb = ztm[:, 5 * TOK:5 * TOK + D]
    bb = ztm[:, 5 * TOK + D:6 * TOK]
    fbfc = ztm[:, 6 * TOK:7 * TOK]
    postbuf = sb("postbuf", [128, TOK], F32)
    mob = postbuf[:, 0:D]
    x1b = postbuf[:, D:TOK]
    rhsA = postbuf[0:32, :]
    fbf = fbfc[0:16, :]
    lhsA = sb("lhsA", [32, 128], F32)
    m0b = sb("m0_sb", [128, 128], BF16)
    grpb = sb("grp_sb", [128, 128], F32)
    dtwb = sb("dtw_sb", [16, 16], F32)
    dtbb = sb("dtb_sb", [16, 1], F32)
    alogb = sb("alog_sb", [16, 1], F32)
    acolb = sb("acol_sb", [16, 1], F32)
    dcolb = sb("dcol_sb", [128, 8], F32)
    nwcb = sb("nwc_sb", [128, 8], F32)
    selAb = sb("selA_sb", [128, 8], F32)
    seldb = sb("seld_sb", [128, 8], F32)
    invsdb = sb("invsd_sb", [128, 8], F32)
    rcols = sb("rcols", [128, NT], F32)
    mrcols = sb("mrcols", [128, NT], F32)
    ftm = sb("ftm", [128, NT * 16], BF16)
    e16 = sb("e16", [128, 64], F32)
    sp4 = sb("sp4", [4, 64], F32)
    dcolsb = sb("dcolsb", [128, 64], F32)
    pcolsb = sb("pcolsb", [128, 64], F32)
    dcoreb = sb("dcoreb", [16, 8], F32)
    ddall = sb("ddall", [128, 1], F32)
    drcols = sb("drcols", [128, 32], F32)
    dvb = sb("dvb", [128, 32], F32)
    avb = sb("avb", [128, 4 * HD], F32)
    hbuf = sb("hbuf", [128, 4 * HD], BF16)
    colc = sb("colc", [128, 32], F32)
    onecol = sb("onecol", [128, 1], BF16)
    epsc = sb("epsc", [128, 3], F32)


    xnrot = cwb[:, 4 * D:5 * D]
    xrotL = cwb[:, 6 * D:8 * D].bitcast(F32)
    xrotT = ztm[:, 7 * TOK:8 * TOK].bitcast(F32)

    pfm = psa("pfm", [128, 1024], F32)
    pfm2 = psa("pfm2", [128, 1024], F32)
    pbig = psa("pbig", [128, 2048], F32)

    names = ["SD", "GD", "CD", "TT", "VV", "CC", "OD", "CS", "CT"]
    sems = {n: es.enter_context(nc.semaphore(n)) for n in names}
    cnt = {n: 0 for n in names}

    with es, nc.Block() as blk:
        def inc(ins, name, k=1):
            ins.then_inc(sems[name], k)
            cnt[name] += k

        def w(e, name, val=None):
            e.wait_ge(sems[name], cnt[name] if val is None else val)

        def barrier(e):
            for n in names:
                if cnt[n]:
                    e.wait_ge(sems[n], cnt[n])

        sd_kinds = set()

        def sdk(e, kind):
            if kind in ("tr", "s2s") and (sd_kinds - {kind, "hbm"}):
                e.wait_ge(sems["SD"], cnt["SD"])
                sd_kinds.clear()
            sd_kinds.add(kind)

        def _(e):
            for k in range(8):
                inc(e.dma_start(out=w_in[:, k * PROJ:(k + 1) * PROJ],
                                in_=dwin[k * 128:(k + 1) * 128, :]), "GD", 16)
            inc(e.dma_start(out=m0b[:], in_=dm0), "GD", 16)
        blk.gpsimd(_)
        gd_w = cnt["GD"]

        def _(e):
            sdk(e, "hbm")
            for t_, d_ in ((grpb, dgrp), (dtwb, ddtw),
                           (dtbb, ddtb), (alogb, dalog), (dcolb, ddcol),
                           (nwcb, dnwc), (selAb, dselA), (seldb, dseld),
                           (invsdb, dinvsd)):
                inc(e.dma_start(out=t_[:], in_=d_), "SD", 16)
            inc(e.dma_start(out=rhsA[1:17, :], in_=dtind), "SD", 16)
        blk.sync(_)
        sd_w = cnt["SD"]

        def _(e):
            e.memset(onecol[:], 1.0)
            e.memset(epsc[:, 0:1], 1e-5)
            e.memset(dcoreb[:], 0.0)
            e.memset(e16[32:48, 16:32], 1.0)
            e.memset(epsc[:, 1:2], 1.0)
            e.memset(epsc[:, 2:3], 1e-6)
            e.memset(lhsA[:], 0.0)
            e.memset(lhsA[0:1, :], 1.0)
            e.memset(srun[:], 0.0)
            e.memset(hbuf[:], 0.0)
            inc(e.memset(pbig[:], 0.0), "VV")
        blk.vector(_)
        vv_init = cnt["VV"]

        for t in range(NT):
            def _(e, t=t):
                sdk(e, "hbm")
                if t >= 1:
                    w(e, "VV")
                inc(e.dma_start(out=xrotL[:],
                                in_=dx[t * 128:(t + 1) * 128, :]), "SD", 16)
            blk.sync(_)
            sd_x = cnt["SD"]

            def _(e, t=t, sd_x=sd_x):
                w(e, "SD", sd_x)
                e.tensor_scalar(scr32[:, 0:D], xrotL[:], 1.0, 0.0, ALU.mult,
                                ALU.add, accum_out=colc[:, 0:1])
                e.scalar_tensor_tensor(scr32[:, 0:D], xrotL[:], 1.0, xrotL[:],
                                       ALU.mult, ALU.mult,
                                       accum_out=colc[:, 1:2])
                e.tensor_scalar(colc[:, 2:3], colc[:, 0:1], 1.0 / D, None,
                                ALU.mult)
                e.tensor_scalar(colc[:, 3:4], colc[:, 1:2], 1.0 / D, None,
                                ALU.mult)
                inc(e.scalar_tensor_tensor(colc[:, 4:5], colc[:, 2:3],
                                           colc[:, 2:3], colc[:, 3:4],
                                           ALU.mult, ALU.subtract), "VV")
            blk.vector(_)
            vv_st = cnt["VV"]

            def _(e, t=t, vv_st=vv_st):
                w(e, "VV", vv_st)
                e.activation(colc[:, 5:6], colc[:, 4:5], AF.Ln, bias=epsc[:, 0:1],
                             scale=-1.0)
                inc(e.activation(rcols[:, t:t + 1], colc[:, 5:6], AF.Exp,
                                 bias=0.0, scale=-0.5), "CC")
            blk.scalar(_)
            cc_r = cnt["CC"]

            def _(e, t=t, cc_r=cc_r):
                w(e, "CC", cc_r)
                inc(e.tensor_scalar(mrcols[:, t:t + 1], colc[:, 2:3],
                                    rcols[:, t:t + 1], -1.0, ALU.mult,
                                    ALU.mult), "VV")
            blk.vector(_)

        xn_sd = []
        for t in range(NT):
            def _(e, t=t):
                sdk(e, "hbm")
                if t >= 1:
                    w(e, "CC")
                else:
                    w(e, "VV")
                inc(e.dma_start(out=xrotL[:],
                                in_=dx[t * 128:(t + 1) * 128, :]), "SD", 16)
            blk.sync(_)
            sd_x = cnt["SD"]

            def _(e, t=t, sd_x=sd_x):
                w(e, "SD", sd_x)
                if t >= 2:
                    w(e, "SD")
                inc(e.activation(xnrot[:], xrotL[:], AF.Identity,
                                 bias=mrcols[:, t:t + 1],
                                 scale=rcols[:, t:t + 1]), "CC")
            blk.scalar(_)
            cc_xn = cnt["CC"]

            def _(e, t=t, cc_xn=cc_xn):
                sdk(e, "tr")
                w(e, "CC", cc_xn)
                for k in range(8):
                    inc(e.dma_start(
                        out=fmb[:, k * TOK + t * 128:k * TOK + (t + 1) * 128],
                        in_=xnrot[:, k * 128:(k + 1) * 128], transpose=True),
                        "SD", 16)
            blk.sync(_)
            xn_sd.append(cnt["SD"])

        fm_tt = []
        for mt in range(5):
            for half in range(2):
                pp = pfm if (mt * 2 + half) % 2 == 0 else pfm2
                M = 16 if mt == 4 else 128

                def _(e, mt=mt, half=half, pp=pp, M=M):
                    if mt == 0 and half == 0:
                        w(e, "SD", xn_sd[-1])
                        w(e, "GD", gd_w)
                    if mt * 2 + half >= 2:
                        w(e, "VV")
                        w(e, "CC")
                    for k in range(8):
                        lhs = w_in[:, k * PROJ + 2048 + mt * 128:
                                   k * PROJ + 2048 + mt * 128 + M]
                        for q in range(2):
                            ins = e.matmul(
                                pp[0:M, q * 512:(q + 1) * 512], lhs,
                                fmb[:, k * TOK + half * 1024 + q * 512:
                                    k * TOK + half * 1024 + (q + 1) * 512],
                                start=(k == 0), stop=(k == 7))
                    inc(ins, "TT")
                blk.tensor(_)
                fm_tt.append(cnt["TT"])

                if mt < 2:
                    def _(e, mt=mt, half=half, pp=pp):
                        w(e, "TT", fm_tt[-1])
                        inc(e.tensor_copy(
                            cwb[:, mt * 2048 + half * 1024:
                                mt * 2048 + (half + 1) * 1024], pp[:]), "VV")
                    blk.vector(_)
                elif mt < 4:
                    def _(e, mt=mt, half=half, pp=pp):
                        w(e, "TT", fm_tt[-1])
                        inc(e.copy(
                            cfm[:, (mt - 2) * 2048 + half * 1024:
                                (mt - 2) * 2048 + (half + 1) * 1024],
                            pp[:]), "CC")
                    blk.scalar(_)
                else:
                    def _(e, half=half, pp=pp):
                        w(e, "TT", fm_tt[-1])
                        inc(e.tensor_copy(
                            packA[0:16, half * 1024:(half + 1) * 1024],
                            pp[0:16, :]), "VV")
                    blk.vector(_)
        fm_vv = cnt["VV"]
        fm_cc = cnt["CC"]

        def _(e):
            inc(e.memset(bpad[:], 0.0), "VV")
        blk.vector(_)
        vv_bm = cnt["VV"]

        def _(e):
            sdk(e, "s2s")
            w(e, "VV", vv_bm)
            w(e, "VV", fm_vv)
            for h in range(0, H, 2):
                gp = h // 2
                dst_t = gp // 4
                base = 32 * (gp % 4)
                inc(e.dma_start(
                    out=bpad[base:base + 16, dst_t * TOK:(dst_t + 1) * TOK],
                    in_=cwb[16 * (h % 8):16 * (h % 8) + 16,
                            (h // 8) * 2048:(h // 8 + 1) * 2048]), "SD", 16)
        blk.sync(_)
        sd_f2 = cnt["SD"]


        def _(e):
            w(e, "VV", fm_vv)
            w(e, "SD", sd_w)
            w(e, "VV")
            for q in range(4):
                ins = e.matmul(pbig[0:16, q * 512:(q + 1) * 512], dtwb[:],
                               packA[0:16, q * 512:(q + 1) * 512],
                               start=True, stop=True)
            inc(ins, "TT")
        blk.tensor(_)
        tt_dt = cnt["TT"]

        def _(e):
            w(e, "TT", tt_dt)
            e.activation(scr32[32:48, :], pbig[0:16, :], AF.Abs,
                         bias=dtbb[:], scale=1.0)
            e.activation(scr32[64:80, :], scr32[32:48, :], AF.Exp,
                         bias=0.0, scale=-1.0)
            e.activation(scr32[96:112, :], scr32[64:80, :], AF.Ln,
                         bias=epsc[0:16, 1:2], scale=1.0)
            e.activation(packA[96:112, :], pbig[0:16, :], AF.Relu,
                         bias=dtbb[:], scale=1.0)
            inc(e.activation(acolb[:], alogb[:], AF.Exp, bias=0.0,
                             scale=1.0), "CC")
        blk.scalar(_)
        cc_d1 = cnt["CC"]

        def _(e):
            w(e, "CC", cc_d1)
            e.tensor_add(packA[0:16, :], scr32[96:112, :], packA[96:112, :])
            e.tensor_scalar(acolb[:], acolb[:], -1.0, None, ALU.mult)
            inc(e.tensor_scalar(scr32[32:48, :], packA[0:16, :], acolb[:],
                                None, ALU.mult), "VV")
        blk.vector(_)
        vv_da = cnt["VV"]

        def _(e):
            sdk(e, "hbm")
            w(e, "VV", vv_da)
            inc(e.dma_start(out=packA[96:112, :], in_=drst), "SD", 16)
        blk.sync(_)
        sd_rst = cnt["SD"]

        def _(e):
            w(e, "VV", vv_da)
            inc(e.activation(scr32[64:80, :], packA[0:16, :], AF.Ln,
                             bias=0.0, scale=1.0), "CC")
        blk.scalar(_)
        cc_lndt = cnt["CC"]

        def _(e):
            w(e, "CC", cc_lndt)
            w(e, "SD", sd_rst)
            e.tensor_copy(scr32[96:112, :], scr32[32:48, :])
            inc(e.tensor_tensor_scan(packA[32:48, :], packA[96:112, :],
                                     scr32[96:112, :], 0.0, ALU.mult,
                                     ALU.add), "VV")
        blk.vector(_)
        vv_a = cnt["VV"]

        def _(e):
            w(e, "VV", vv_a)
            e.tensor_copy(scr32[32:48, :], scr32[64:80, :])
            e.tensor_sub(packA[64:80, :], scr32[32:48, :], packA[32:48, :])
            av3 = packA[32:48, :].rearrange("h (c i) -> h c i", c=NT)
            e.tensor_copy(e16[32:48, 0:16].unsqueeze(2), av3[:, :, 127:128])
            bcv = e16[32:48, 0:16].unsqueeze(2).broadcast_to([16, NT, 128])
            e.tensor_tensor(
                scr32[96:112, :].rearrange("h (c i) -> h c i", c=NT), bcv,
                av3, ALU.subtract)
            inc(e.tensor_tensor_scan(e16[32:48, 32:48], e16[32:48, 16:32],
                                     e16[32:48, 0:16], 0.0, ALU.mult,
                                     ALU.add), "VV")
        blk.vector(_)
        vv_pib = cnt["VV"]

        def _(e):
            w(e, "VV", vv_pib)
            inc(e.tensor_sub(e16[32:48, 48:64], e16[32:48, 32:48],
                             e16[32:48, 0:16]), "VV")
        blk.vector(_)
        vv_d2 = cnt["VV"]

        def _(e):
            w(e, "VV", vv_d2)
            e.activation(fbf[:], scr32[96:112, :], AF.Exp, bias=0.0,
                         scale=1.0)
            inc(e.activation(dcoreb[:, 0:1], e16[32:48, 47:48], AF.Exp,
                             bias=0.0, scale=1.0), "CC")
        blk.scalar(_)
        cc_d3 = cnt["CC"]

        def _(e):
            w(e, "CC", cc_d3)
            inc(e.tensor_tensor(fbf[:], packA[0:16, :], fbf[:], ALU.mult),
                "VV")
        blk.vector(_)
        vv_f = cnt["VV"]

        def _(e):
            sdk(e, "tr")
            w(e, "VV", vv_f)
            for t in range(NT):
                inc(e.dma_start(out=ftm[:, t * 16:(t + 1) * 16],
                                in_=fbf[:, t * 128:(t + 1) * 128],
                                transpose=True), "SD", 16)
        blk.sync(_)
        sd_ftm = cnt["SD"]

        def _(e):
            sdk(e, "s2s")
            w(e, "VV", vv_d2)
            for gi in range(4):
                inc(e.dma_start(out=sp4[0:4, 16 * gi:16 * (gi + 1)],
                                in_=e16[32 + 4 * gi:32 + 4 * gi + 4, 0:16]),
                    "SD", 16)
        blk.sync(_)
        sd_ae = cnt["SD"]

        def _(e):
            w(e, "SD", sd_ae)
            w(e, "SD", sd_w)
            for gi in range(4):
                ins = e.matmul(pfm2[:, gi * 16:(gi + 1) * 16],
                               grpb[0:4, :],
                               sp4[0:4, 16 * gi:16 * (gi + 1)],
                               start=True, stop=True)
            inc(ins, "TT")
        blk.tensor(_)
        tt_dc = cnt["TT"]

        def _(e):
            w(e, "TT", tt_dc)
            inc(e.activation(dcolsb[:], pfm2[:, 0:64], AF.Exp, bias=0.0,
                             scale=1.0), "CC")
        blk.scalar(_)
        cc_dcol = cnt["CC"]

        def _(e):
            sdk(e, "s2s")
            w(e, "CC", cc_dcol)
            w(e, "VV", vv_d2)
            for gi in range(4):
                inc(e.dma_start(out=sp4[0:4, 16 * gi:16 * (gi + 1)],
                                in_=e16[32 + 4 * gi:32 + 4 * gi + 4, 48:64]),
                    "SD", 16)
        blk.sync(_)
        sd_pe = cnt["SD"]

        def _(e):
            w(e, "SD", sd_pe)
            for gi in range(4):
                ins = e.matmul(pfm2[:, gi * 16:(gi + 1) * 16],
                               grpb[0:4, :],
                               sp4[0:4, 16 * gi:16 * (gi + 1)],
                               start=True, stop=True)
            inc(ins, "TT")
        blk.tensor(_)
        tt_pc = cnt["TT"]

        def _(e):
            w(e, "TT", tt_pc)
            inc(e.activation(pcolsb[:], pfm2[:, 0:64], AF.Exp, bias=0.0,
                             scale=1.0), "CC")
        blk.scalar(_)
        cc_pcol = cnt["CC"]

        tt_projtm = []
        sd_xpfm = []
        for t in range(NT):
            def _(e, t=t):
                w(e, "SD", xn_sd[t])
                if t == 0:
                    w(e, "GD", gd_w)
                    w(e, "CC", cc_pcol)
                    w(e, "TT", fm_tt[-1])
                    w(e, "SD", sd_ftm)
                else:
                    w(e, "CC")
                for k in range(8):
                    st = fmb[:, k * TOK + t * 128:k * TOK + (t + 1) * 128]
                    for q in range(2):
                        e.matmul(pfm[:, q * 512:(q + 1) * 512], st,
                                 w_in[:, k * PROJ + q * 512:
                                      k * PROJ + (q + 1) * 512],
                                 start=(k == 0), stop=(k == 7))
                    for q in range(2):
                        ins = e.matmul(
                            pfm2[:, q * 512:(q + 1) * 512], st,
                            w_in[:, k * PROJ + 1024 + q * 512:
                                 k * PROJ + 1024 + (q + 1) * 512],
                            start=(k == 0), stop=(k == 7))
                inc(ins, "TT")
            blk.tensor(_)
            tt_projtm.append(cnt["TT"])
            tt_p = cnt["TT"]

            def _(e, t=t, tt_p=tt_p):
                w(e, "TT", tt_p)
                e.copy(xptm[:, t * D:(t + 1) * D], pfm[:])
                inc(e.activation(ztm[:, t * D:(t + 1) * D], pfm2[:], AF.Silu,
                                 bias=0.0, scale=1.0), "CC")
            blk.scalar(_)
            cc_ev = cnt["CC"]

            def _(e, t=t, cc_ev=cc_ev):
                sdk(e, "tr")
                w(e, "CC", cc_ev)
                for k in range(8):
                    inc(e.dma_start(
                        out=fmb[:, k * TOK + t * 128:k * TOK + (t + 1) * 128],
                        in_=xptm[:, t * D + k * 128:t * D + (k + 1) * 128],
                        transpose=True), "SD", 16)
                if t >= 2:
                    w(e, "TT")
                for r in range(2):
                    inc(e.dma_start(
                        out=btm[:, (t % 2) * 256 + r * 128:
                                (t % 2) * 256 + (r + 1) * 128],
                        in_=cwb[:, r * 2048 + t * 128:
                                r * 2048 + (t + 1) * 128],
                        transpose=True), "SD", 16)
            blk.sync(_)
            sd_xpfm.append(cnt["SD"])

            def _(e, t=t, cc_ev=cc_ev):
                w(e, "CC", cc_ev)
                w(e, "SD", sd_ftm)
                fv = ftm[:, t * 16:(t + 1) * 16].unsqueeze(2).broadcast_to(
                    [128, 16, HD])
                xv = xptm[:, t * D:(t + 1) * D].rearrange(
                    "p (h d) -> p h d", h=16)
                inc(e.tensor_tensor(wxb[:].rearrange("p (h d) -> p h d",
                                                     h=16),
                                    xv, fv, ALU.mult), "VV")
            blk.vector(_)
            vv_wx = cnt["VV"]

            def _(e, t=t, vv_wx=vv_wx):
                w(e, "VV", vv_wx)
                w(e, "SD", sd_xpfm[-1])
                if t == 0:
                    w(e, "SD", sd_f2)
                    w(e, "VV", vv_init)
                else:
                    w(e, "VV")
                for h in range(H):
                    gi, m = h // 4, h % 4
                    ins = e.matmul(
                        pbig[32 * m:32 * m + 16, gi * 512:gi * 512 + 64],
                        btm[:, (t % 2) * 256 + (h // 8) * 128 + 16 * (h % 8):
                            (t % 2) * 256 + (h // 8) * 128 + 16 * (h % 8)
                            + 16],
                        wxb[:, h * HD:(h + 1) * HD],
                        start=True, stop=True,
                        tile_position=(0, 32 * m))
                inc(ins, "TT")
            blk.tensor(_)
            tt_cs = cnt["TT"]

            def _(e, t=t, tt_cs=tt_cs):
                w(e, "TT", tt_cs)
                if t == 0:
                    w(e, "CC", cc_dcol)
                for gi in range(4):
                    e.tensor_copy(
                        snaps[:, (t * 4 + gi) * HD:(t * 4 + gi + 1) * HD],
                        srun[:, gi * HD:(gi + 1) * HD])
                    ins = e.scalar_tensor_tensor(
                        srun[:, gi * HD:(gi + 1) * HD],
                        srun[:, gi * HD:(gi + 1) * HD],
                        dcolsb[:, gi * 16 + t:gi * 16 + t + 1],
                        pbig[:, gi * 512:gi * 512 + 64],
                        ALU.mult, ALU.add)
                inc(ins, "VV")
            blk.vector(_)

        def _(e):
            w(e, "VV")
            w(e, "CC", cc_d3)
            inc(e.dma_start(out=ccA_in.ap(), in_=srun[:]), "GD", 16)
            inc(e.dma_start(out=ccd_in.ap(), in_=dcoreb[:]), "GD", 16)
            w(e, "GD")
            inc(e.collective_compute(
                "AllGather", ALU.bypass,
                replica_groups=[list(range(NCORE))],
                ins=[ccA_in.ap().opt()], outs=[ccA_out.ap().opt()]), "CS")
            inc(e.collective_compute(
                "AllGather", ALU.bypass,
                replica_groups=[list(range(NCORE))],
                ins=[ccd_in.ap().opt()], outs=[ccd_out.ap().opt()]), "CT")
            e.wait_ge(sems["CS"], 1)
            e.wait_ge(sems["CT"], 1)
            inc(e.dma_start(out=ddall[:], in_=ccd_out.ap()[:, 0:1],
                            allow_slow_non_contiguous=True), "GD", 16)
        blk.gpsimd(_)
        gd_cc = cnt["GD"]

        def _(e):
            sdk(e, "s2s")
            w(e, "GD", gd_cc)
            for gi in range(4):
                for r in range(NCORE):
                    inc(e.dma_start(
                        out=sp4[0:4, 8 * gi + r:8 * gi + r + 1],
                        in_=ddall[16 * r + 4 * gi:16 * r + 4 * gi + 4, :],
                        allow_slow_non_contiguous=True), "SD", 16)
        blk.sync(_)
        sd_dd = cnt["SD"]

        def _(e):
            w(e, "SD", sd_dd)
            w(e, "CC")
            for gi in range(4):
                ins = e.matmul(pfm2[:, gi * 8:(gi + 1) * 8],
                               grpb[0:4, :],
                               sp4[0:4, 8 * gi:8 * (gi + 1)],
                               start=True, stop=True)
            inc(ins, "TT")
        blk.tensor(_)
        tt_dr = cnt["TT"]

        def _(e):
            w(e, "TT", tt_dr)
            for gi in range(4):
                ins = e.tensor_copy(drcols[:, gi * 8:(gi + 1) * 8],
                                    pfm2[:, gi * 8:(gi + 1) * 8])
            inc(ins, "VV")
        blk.vector(_)
        vv_dr = cnt["VV"]

        def _(e):
            w(e, "VV", vv_dr)
            w(e, "SD", sd_w)
            dv3 = dvb[:].rearrange("p (g r) -> p g r", g=4)
            dr3 = drcols[:].rearrange("p (g r) -> p g r", g=4)
            for r in range(NCORE):
                ins = e.scalar_tensor_tensor(
                    dv3[:, :, r], dr3[:, :, r], seldb[:, r:r + 1],
                    invsdb[:, r:r + 1].broadcast_to([128, 4]),
                    ALU.mult, ALU.add)
            inc(ins, "VV")
        blk.vector(_)
        vv_dv = cnt["VV"]

        for r in range(NCORE):
            def _(e, r=r):
                w(e, "GD", gd_cc)
                if r >= 1:
                    w(e, "VV")
                inc(e.dma_start(
                    out=avb[:],
                    in_=ccA_out.ap()[r * 128:(r + 1) * 128, :]), "GD", 16)
            blk.gpsimd(_)
            gd_av = cnt["GD"]

            def _(e, r=r, gd_av=gd_av):
                w(e, "GD", gd_av)
                if r == 0:
                    w(e, "VV", vv_dv)
                for gi in range(4):
                    e.tensor_scalar(avb[:, gi * HD:(gi + 1) * HD],
                                    avb[:, gi * HD:(gi + 1) * HD],
                                    selAb[:, r:r + 1], None, ALU.mult)
                for gi in range(4):
                    ins = e.scalar_tensor_tensor(
                        hbuf[:, gi * HD:(gi + 1) * HD],
                        hbuf[:, gi * HD:(gi + 1) * HD],
                        dvb[:, gi * 8 + r:gi * 8 + r + 1],
                        avb[:, gi * HD:(gi + 1) * HD],
                        ALU.mult, ALU.add)
                inc(ins, "VV")
            blk.vector(_)
        vv_h = cnt["VV"]

        def _(e):
            w(e, "VV", vv_h)
            w(e, "CC", cc_pcol)
            for t in range(NT):
                for gi in range(4):
                    e.tensor_scalar(avb[:, gi * HD:(gi + 1) * HD],
                                    hbuf[:, gi * HD:(gi + 1) * HD],
                                    pcolsb[:, gi * 16 + t:gi * 16 + t + 1],
                                    None, ALU.mult)
                for gi in range(4):
                    ins = e.tensor_add(
                        snaps[:, (t * 4 + gi) * HD:(t * 4 + gi + 1) * HD],
                        snaps[:, (t * 4 + gi) * HD:(t * 4 + gi + 1) * HD],
                        avb[:, gi * HD:(gi + 1) * HD])
            inc(ins, "VV")
        blk.vector(_)
        vv_snapc = cnt["VV"]

        def _(e):
            sdk(e, "tr")
            w(e, "CC")
            w(e, "TT", tt_projtm[-1])
            for t in range(NT):
                for k in range(8):
                    inc(e.dma_start(
                        out=zfm[:, k * TOK + t * 128:k * TOK + (t + 1) * 128],
                        in_=ztm[:, t * D + k * 128:t * D + (k + 1) * 128],
                        transpose=True), "SD", 16)
        blk.sync(_)
        sd_zfm = cnt["SD"]

        def _(e):
            w(e, "TT", tt_projtm[-1])
            inc(e.memset(bpadB[:], 0.0), "VV")
        blk.vector(_)
        vv_bpB = cnt["VV"]

        def _(e):
            sdk(e, "s2s")
            w(e, "VV", vv_bpB)
            for h in range(1, H, 2):
                gp = h // 2
                dst_t = gp // 4
                base = 32 * (gp % 4) + 16
                inc(e.dma_start(
                    out=bpadB[base:base + 16, dst_t * TOK:(dst_t + 1) * TOK],
                    in_=cwb[16 * (h % 8):16 * (h % 8) + 16,
                            (h // 8) * 2048:(h // 8 + 1) * 2048]), "SD", 16)
        blk.sync(_)
        sd_bpB = cnt["SD"]

        def _(e):
            w(e, "SD", sd_f2)
            w(e, "SD", sd_bpB)
            w(e, "SD", xn_sd[-1])
            w(e, "CC")
            w(e, "VV", fm_vv)
            w(e, "VV")
            inc(e.memset(cwb[:], 0.0), "VV")
        blk.vector(_)
        vv_cm = cnt["VV"]

        def _(e):
            sdk(e, "s2s")
            w(e, "VV", vv_cm)
            w(e, "CC", fm_cc)
            for h in range(H):
                inc(e.dma_start(
                    out=cwb[32 * (h % 4):32 * (h % 4) + 16,
                            (h // 4) * 2048:(h // 4 + 1) * 2048],
                    in_=cfm[16 * (h % 8):16 * (h % 8) + 16,
                            (h // 8) * 2048:(h // 8 + 1) * 2048]), "SD", 16)
            w(e, "CC")
            for gi in range(4):
                inc(e.dma_start(out=scr32[32 * gi:32 * gi + 4, :],
                                in_=packA[32 + 4 * gi:32 + 4 * gi + 4, :]),
                    "SD", 16)
        blk.sync(_)
        sd_cp = cnt["SD"]

        for gi in range(4):
            def _(e, gi=gi):
                w(e, "SD", sd_cp)
                w(e, "SD", sd_zfm)
                if gi == 0:
                    w(e, "VV", vv_dr)
                    w(e, "CC")
                else:
                    w(e, "CC")
                for q in range(4):
                    pp = pfm2 if q < 2 else pfm
                    ins = e.matmul(pp[:, (q % 2) * 512:(q % 2 + 1) * 512],
                                   grpb[32 * gi:32 * gi + 4, :],
                                   scr32[32 * gi:32 * gi + 4,
                                         q * 512:(q + 1) * 512],
                                   start=True, stop=True,
                                   tile_position=(32 * gi, 0))
                inc(ins, "TT")
            blk.tensor(_)
            tt_ea = cnt["TT"]

            def _(e, gi=gi, tt_ea=tt_ea):
                w(e, "TT", tt_ea)
                e.activation(eag[:, 0:512], pfm2[:, 0:512], AF.Exp,
                             bias=0.0, scale=1.0)
                e.activation(eag[:, 512:1024], pfm2[:, 512:1024], AF.Exp,
                             bias=0.0, scale=1.0)
                e.activation(eag[:, 1024:1536], pfm[:, 0:512], AF.Exp,
                             bias=0.0, scale=1.0)
                inc(e.activation(eag[:, 1536:2048], pfm[:, 512:1024], AF.Exp,
                                 bias=0.0, scale=1.0), "CC")
            blk.scalar(_)
            cc_ea = cnt["CC"]

            def _(e, gi=gi, cc_ea=cc_ea):
                w(e, "CC", cc_ea)
                w(e, "SD", sd_cp)
                inc(e.tensor_tensor(cwb[:, gi * 2048:(gi + 1) * 2048],
                                    cwb[:, gi * 2048:(gi + 1) * 2048],
                                    eag[:], ALU.mult), "VV")
            blk.vector(_)
        vv_cpr = cnt["VV"]

        tt_lastA = [0]
        for h in range(H):
            pair, odd = h // 2, h % 2
            scob = scobO if odd else scobE

            def _(e, h=h):
                sdk(e, "s2s")
                if h >= 1:
                    w(e, "TT", tt_lastA[0])
                else:
                    w(e, "VV")
                inc(e.dma_start(out=lhsA[1:17, :],
                                in_=packA[64 + h:65 + h, :]), "SD", 16)
                inc(e.dma_start(out=rhsA[0:1, :],
                                in_=packA[32 + h:33 + h, :]), "SD", 16)
            blk.sync(_)
            sd_ab = cnt["SD"]

            for q in range(4):
                def _(e, h=h, q=q, sd_ab=sd_ab):
                    w(e, "SD", sd_ab)
                    w(e, "VV")
                    inc(e.matmul(pfm[:, (q % 2) * 512:(q % 2 + 1) * 512],
                                 lhsA[0:17, :],
                                 rhsA[0:17, q * 512:(q + 1) * 512],
                                 start=True, stop=True), "TT")
                blk.tensor(_)
                tt_a = cnt["TT"]
                if q == 3:
                    tt_lastA[0] = tt_a

                def _(e, h=h, q=q, tt_a=tt_a):
                    w(e, "TT", tt_a)
                    if h == 0 and q == 0:
                        w(e, "GD", gd_w)
                    m0v = m0b[:].unsqueeze(1).broadcast_to([128, 4, 128])
                    inc(e.tensor_tensor(
                        scr32[:, q * 512:(q + 1) * 512].rearrange(
                            "p (a b) -> p a b", a=4),
                        pfm[:, (q % 2) * 512:(q % 2 + 1) * 512].rearrange(
                            "p (a b) -> p a b", a=4),
                        m0v, ALU.add), "VV")
                blk.vector(_)
            vv_arg = cnt["VV"]

            def _(e, h=h, vv_arg=vv_arg):
                w(e, "VV", vv_arg)
                e.activation(expb[:, 0:1024], scr32[:, 0:1024], AF.Exp,
                             bias=0.0, scale=1.0)
                inc(e.activation(expb[:, 1024:2048], scr32[:, 1024:2048],
                                 AF.Exp, bias=0.0, scale=1.0), "CC")
            blk.scalar(_)
            cc_exp = cnt["CC"]

            def _(e, h=h):
                if h == 0:
                    w(e, "SD", sd_cp)
                    w(e, "VV", vv_cpr)
                w(e, "VV")
                gp = h // 2
                base = 32 * (gp % 4)
                dst_t = gp // 4
                bp = bpad if h % 2 == 0 else bpadB
                for T in range(NT):
                    ins = e.matmul(
                        pbig[:, T * 128:(T + 1) * 128],
                        bp[base:base + 32,
                           dst_t * TOK + T * 128:dst_t * TOK +
                           (T + 1) * 128],
                        cfm[base:base + 32,
                            dst_t * TOK + T * 128:dst_t * TOK +
                            (T + 1) * 128],
                        start=True, stop=True, tile_position=(base, 0))
                inc(ins, "TT")
            blk.tensor(_)
            tt_raw = cnt["TT"]

            def _(e, h=h, tt_raw=tt_raw, cc_exp=cc_exp, scob=scob):
                w(e, "TT", tt_raw)
                w(e, "CC", cc_exp)
                e.tensor_tensor(scob[:, 0:1024], pbig[:, 0:1024],
                                expb[:, 0:1024], ALU.mult)
                inc(e.tensor_tensor(scob[:, 1024:2048], pbig[:, 1024:2048],
                                    expb[:, 1024:2048], ALU.mult), "VV")
            blk.vector(_)
            vv_sco = cnt["VV"]

            if _DEBUG_DUMP and h == 0:
                def _(e, vv_sco=vv_sco):
                    w(e, "VV", vv_sco)
                    inc(e.dma_start(out=dbgS.ap(), in_=scobE[:]), "SD", 16)
                    inc(e.dma_start(out=dbgE.ap(), in_=expb[:]), "SD", 16)
                blk.sync(_)
                def _(e, vv_sco=vv_sco):
                    w(e, "VV", vv_sco)
                    inc(e.tensor_copy(scr32[:], pbig[:]), "VV")
                blk.vector(_)
                vv_dbg = cnt["VV"]
                def _(e, vv_dbg=vv_dbg):
                    w(e, "VV", vv_dbg)
                    inc(e.dma_start(out=dbgR.ap(), in_=scr32[:]), "SD", 16)
                blk.sync(_)

            if odd:
                def _(e, pair=pair, vv_sco=vv_sco):
                    w(e, "VV", vv_sco)
                    w(e, "VV", vv_snapc)
                    for hh in (2 * pair, 2 * pair + 1):
                        par = hh % 2
                        sc = scobO if par else scobE
                        base = 32 * (hh % 4)
                        gtile = hh // 4
                        for T in range(NT):
                            ins = e.matmul(
                                pbig[64 * par:64 * par + 64,
                                     T * 128:(T + 1) * 128],
                                xptm[:, T * D + hh * HD:
                                     T * D + (hh + 1) * HD],
                                sc[:, T * 128:(T + 1) * 128],
                                start=(T % 4 == 0), stop=False,
                                tile_position=(0, 64 * par))
                        inc(ins, "TT")
                        w(e, "TT")
                        for T in range(NT):
                            ins = e.matmul(
                                pbig[64 * par:64 * par + 64,
                                     T * 128:(T + 1) * 128],
                                snaps[base:base + 32,
                                      (T * 4 + gtile) * HD:
                                      (T * 4 + gtile + 1) * HD],
                                cwb[base:base + 32,
                                    gtile * 2048 + T * 128:
                                    gtile * 2048 + (T + 1) * 128],
                                start=False, stop=(T % 4 == 3),
                                tile_position=(32 * (hh % 4), 64 * par))
                        inc(ins, "TT")
                        w(e, "TT")
                blk.tensor(_)
                tt_y = cnt["TT"]

                def _(e, pair=pair, tt_y=tt_y):
                    w(e, "TT", tt_y)
                    if pair == 0:
                        w(e, "SD", sd_xpfm[-1])
                    ins = e.scalar_tensor_tensor(
                        fmb[:, pair * TOK:(pair + 1) * TOK],
                        fmb[:, pair * TOK:(pair + 1) * TOK],
                        dcolb[:, pair:pair + 1], pbig[:],
                        ALU.mult, ALU.add)
                    inc(ins, "VV")
                blk.vector(_)
        vv_y = cnt["VV"]

        if _DEBUG_DUMP:
            def _(e):
                w(e, "VV")
                inc(e.dma_start(out=dbgY.ap(), in_=fmb[:]), "SD", 16)
            blk.sync(_)

        for gi in range(8):
            def _(e, gi=gi):
                if gi == 0:
                    w(e, "VV")
                    w(e, "SD", sd_zfm)
                inc(e.scalar_tensor_tensor(
                    fmb[:, gi * TOK:(gi + 1) * TOK],
                    fmb[:, gi * TOK:(gi + 1) * TOK],
                    nwcb[:, gi:gi + 1],
                    zfm[:, gi * TOK:(gi + 1) * TOK], ALU.mult, ALU.mult),
                    "VV")
            blk.vector(_)
        vv_gate = cnt["VV"]

        def _(e):
            w(e, "TT")
            w(e, "SD", sd_zfm)
            for k in range(8):
                inc(e.dma_start(out=cwb[:, k * D:(k + 1) * D],
                                in_=dwout[k * 128:(k + 1) * 128, :]),
                    "GD", 16)
            inc(e.dma_start(out=gb[:], in_=dgbc), "GD", 16)
            inc(e.dma_start(out=bb[:], in_=dbbc), "GD", 16)
        blk.gpsimd(_)
        gd_wo = cnt["GD"]

        xrot2 = ztm[:, 6 * TOK:7 * TOK].bitcast(F32)
        xrots = [xrotT, xrot2]
        mobs = [postbuf[:, 0:D], scr32[:, 0:D]]
        x1bs = [postbuf[:, D:TOK], scr32[:, D:TOK]]
        tAs = [ztm[:, 0:D], ztm[:, 2 * D:3 * D]]
        vbs = [ztm[:, D:2 * D], ztm[:, 3 * D:4 * D]]
        txs = [packA[:, 0:D], packA[:, D:2 * D]]
        CB = [0, 16]

        cd_t = {}
        tt_mo_t = {}
        s0_vv = {}
        od_t = {}
        for p in range(NT // 2):
            pair = (2 * p, 2 * p + 1)
            for t in pair:
                def _(e, t=t, p=p):
                    if p == 0:
                        w(e, "SD", sd_zfm)
                    else:
                        w(e, "VV")
                    inc(e.dma_start(out=xrots[t % 2][:],
                                    in_=dx[t * 128:(t + 1) * 128, :]),
                        "CD", 16)
                blk.scalar(_)
                cd_t[t] = cnt["CD"]

            for t in pair:
                pmo = pfm if t % 2 == 0 else pfm2

                def _(e, t=t, pmo=pmo):
                    w(e, "GD", gd_wo)
                    w(e, "VV", vv_gate)
                    if t >= 2:
                        w(e, "VV", s0_vv[t - 2])
                    for k in range(8):
                        st = fmb[:, k * TOK + t * 128:k * TOK + (t + 1) * 128]
                        for q in range(2):
                            ins = e.matmul(pmo[:, q * 512:(q + 1) * 512], st,
                                           cwb[:, k * D + q * 512:
                                               k * D + (q + 1) * 512],
                                           start=(k == 0), stop=(k == 7))
                    inc(ins, "TT")
                blk.tensor(_)
                tt_mo_t[t] = cnt["TT"]

            for t in pair:
                pmo = pfm if t % 2 == 0 else pfm2

                def _(e, t=t, pmo=pmo):
                    w(e, "TT", tt_mo_t[t])
                    B = CB[t % 2]
                    inc(e.tensor_scalar(mobs[t % 2][:], pmo[:], 1.0, 0.0,
                                        ALU.mult, ALU.add,
                                        accum_out=colc[:, B:B + 1]), "VV")
                blk.vector(_)
                s0_vv[t] = cnt["VV"]

            def _(e, p=p):
                for t in pair:
                    B = CB[t % 2]
                    e.scalar_tensor_tensor(
                        tAs[t % 2][:], mobs[t % 2][:], 1.0, mobs[t % 2][:],
                        ALU.mult, ALU.mult,
                        accum_out=colc[:, B + 1:B + 2])
                for t in pair:
                    B = CB[t % 2]
                    e.tensor_scalar(colc[:, B + 2:B + 3],
                                    colc[:, B:B + 1], 1.0 / D, None, ALU.mult)
                    e.tensor_scalar(colc[:, B + 3:B + 4],
                                    colc[:, B + 1:B + 2], 1.0 / D, None,
                                    ALU.mult)
                for t in pair:
                    B = CB[t % 2]
                    ins = e.scalar_tensor_tensor(
                        colc[:, B + 4:B + 5], colc[:, B + 2:B + 3],
                        colc[:, B + 2:B + 3], colc[:, B + 3:B + 4],
                        ALU.mult, ALU.subtract)
                inc(ins, "VV")
            blk.vector(_)
            vv_st1 = cnt["VV"]

            def _(e, p=p, vv_st1=vv_st1):
                w(e, "VV", vv_st1)
                for t in pair:
                    B = CB[t % 2]
                    e.activation(colc[:, B + 5:B + 6], colc[:, B + 4:B + 5],
                                 AF.Ln, bias=epsc[:, 0:1], scale=-1.0)
                    ins = e.activation(colc[:, B + 6:B + 7],
                                       colc[:, B + 5:B + 6], AF.Exp,
                                       bias=0.0, scale=-0.5)
                inc(ins, "CC")
            blk.scalar(_)
            cc_r2 = cnt["CC"]

            def _(e, p=p, cc_r2=cc_r2):
                w(e, "CC", cc_r2)
                w(e, "CD", cd_t[pair[1]])
                if p >= 1:
                    w(e, "OD", od_t[pair[0] - 2])
                    w(e, "OD", od_t[pair[1] - 2])
                for t in pair:
                    B = CB[t % 2]
                    e.tensor_scalar(colc[:, B + 7:B + 8],
                                    colc[:, B + 2:B + 3],
                                    colc[:, B + 6:B + 7], -1.0, ALU.mult,
                                    ALU.mult)
                for t in pair:
                    B = CB[t % 2]
                    e.tensor_scalar(txs[t % 2][:], xrots[t % 2][:],
                                    colc[:, B + 7:B + 8], None, ALU.add)
                for t in pair:
                    B = CB[t % 2]
                    e.scalar_tensor_tensor(
                        x1bs[t % 2][:], mobs[t % 2][:],
                        colc[:, B + 6:B + 7], txs[t % 2][:], ALU.mult,
                        ALU.add, accum_out=colc[:, B:B + 1])
                for t in pair:
                    B = CB[t % 2]
                    e.scalar_tensor_tensor(
                        tAs[t % 2][:], x1bs[t % 2][:], 1.0, x1bs[t % 2][:],
                        ALU.mult, ALU.mult,
                        accum_out=colc[:, B + 1:B + 2])
                for t in pair:
                    B = CB[t % 2]
                    e.tensor_scalar(colc[:, B + 2:B + 3],
                                    colc[:, B:B + 1], 1.0 / D, None, ALU.mult)
                    e.tensor_scalar(colc[:, B + 3:B + 4],
                                    colc[:, B + 1:B + 2], 1.0 / D, None,
                                    ALU.mult)
                for t in pair:
                    B = CB[t % 2]
                    ins = e.scalar_tensor_tensor(
                        colc[:, B + 4:B + 5], colc[:, B + 2:B + 3],
                        colc[:, B + 2:B + 3], colc[:, B + 3:B + 4],
                        ALU.mult, ALU.subtract)
                inc(ins, "VV")
            blk.vector(_)
            vv_st2 = cnt["VV"]

            def _(e, p=p, vv_st2=vv_st2):
                w(e, "VV", vv_st2)
                for t in pair:
                    B = CB[t % 2]
                    e.activation(colc[:, B + 5:B + 6], colc[:, B + 4:B + 5],
                                 AF.Ln, bias=epsc[:, 0:1], scale=-1.0)
                    e.activation(colc[:, B + 6:B + 7], colc[:, B + 5:B + 6],
                                 AF.Exp, bias=0.0, scale=-0.5)
                    ins = e.activation(colc[:, B + 8:B + 9],
                                       colc[:, B + 5:B + 6], AF.Exp,
                                       bias=0.0, scale=-1.0)
                inc(ins, "CC")
            blk.scalar(_)
            cc_r3 = cnt["CC"]

            def _(e, p=p, cc_r3=cc_r3):
                w(e, "CC", cc_r3)
                for t in pair:
                    B = CB[t % 2]
                    e.tensor_scalar(colc[:, B + 7:B + 8],
                                    colc[:, B + 2:B + 3],
                                    colc[:, B + 6:B + 7], -1.0, ALU.mult,
                                    ALU.mult)
                for t in pair:
                    B = CB[t % 2]
                    ins = e.tensor_scalar(colc[:, B + 9:B + 10],
                                          colc[:, B + 4:B + 5],
                                          colc[:, B + 8:B + 9], -1.0,
                                          ALU.mult, ALU.mult)
                inc(ins, "VV")
            blk.vector(_)
            vv_vu = cnt["VV"]

            def _(e, p=p, vv_vu=vv_vu):
                w(e, "VV", vv_vu)
                for t in pair:
                    B = CB[t % 2]
                    e.activation(colc[:, B + 10:B + 11],
                                 colc[:, B + 9:B + 10], AF.Ln,
                                 bias=epsc[:, 0:1], scale=1.0)
                    ins = e.activation(colc[:, B + 11:B + 12],
                                       colc[:, B + 10:B + 11], AF.Exp,
                                       bias=0.0, scale=-0.5)
                inc(ins, "CC")
            blk.scalar(_)
            cc_rs = cnt["CC"]

            def _(e, p=p, cc_rs=cc_rs):
                w(e, "CC", cc_rs)
                w(e, "GD", gd_wo)
                for t in pair:
                    B = CB[t % 2]
                    e.tensor_scalar(colc[:, B + 13:B + 14],
                                    colc[:, B + 7:B + 8],
                                    colc[:, B + 11:B + 12], None, ALU.mult)
                for t in pair:
                    B = CB[t % 2]
                    e.tensor_scalar(colc[:, B + 12:B + 13],
                                    colc[:, B + 6:B + 7],
                                    colc[:, B + 11:B + 12], None, ALU.mult)
                for t in pair:
                    B = CB[t % 2]
                    e.scalar_tensor_tensor(
                        tAs[t % 2][:], x1bs[t % 2][:],
                        colc[:, B + 12:B + 13],
                        colc[:, B + 13:B + 14].broadcast_to([128, D]),
                        ALU.mult, ALU.add)
                for t in pair:
                    e.tensor_tensor(vbs[t % 2][:], tAs[t % 2][:], gb[:],
                                    ALU.mult)
                for t in pair:
                    B = CB[t % 2]
                    e.scalar_tensor_tensor(
                        vbs[t % 2][:], vbs[t % 2][:], 1.0, bb[:], ALU.mult,
                        ALU.add, accum_out=colc[:, B:B + 1])
                for t in pair:
                    B = CB[t % 2]
                    e.scalar_tensor_tensor(
                        tAs[t % 2][:], vbs[t % 2][:], 1.0, vbs[t % 2][:],
                        ALU.mult, ALU.mult,
                        accum_out=colc[:, B + 1:B + 2])
                for t in pair:
                    B = CB[t % 2]
                    e.tensor_scalar(colc[:, B + 2:B + 3],
                                    colc[:, B:B + 1], 1.0 / D, None, ALU.mult)
                    e.tensor_scalar(colc[:, B + 3:B + 4],
                                    colc[:, B + 1:B + 2], 1.0 / D, None,
                                    ALU.mult)
                for t in pair:
                    B = CB[t % 2]
                    ins = e.scalar_tensor_tensor(
                        colc[:, B + 4:B + 5], colc[:, B + 2:B + 3],
                        colc[:, B + 2:B + 3], colc[:, B + 3:B + 4],
                        ALU.mult, ALU.subtract)
                inc(ins, "VV")
            blk.vector(_)
            vv_st3 = cnt["VV"]

            def _(e, p=p, vv_st3=vv_st3):
                w(e, "VV", vv_st3)
                for t in pair:
                    B = CB[t % 2]
                    e.activation(colc[:, B + 5:B + 6], colc[:, B + 4:B + 5],
                                 AF.Ln, bias=epsc[:, 0:1], scale=-1.0)
                    ins = e.activation(colc[:, B + 6:B + 7],
                                       colc[:, B + 5:B + 6], AF.Exp,
                                       bias=0.0, scale=-0.5)
                inc(ins, "CC")
            blk.scalar(_)
            cc_r4 = cnt["CC"]

            def _(e, p=p, cc_r4=cc_r4):
                w(e, "CC", cc_r4)
                for t in pair:
                    B = CB[t % 2]
                    e.tensor_scalar(colc[:, B + 7:B + 8],
                                    colc[:, B + 2:B + 3],
                                    colc[:, B + 6:B + 7], -1.0, ALU.mult,
                                    ALU.mult)
                for t in pair:
                    B = CB[t % 2]
                    e.scalar_tensor_tensor(
                        txs[t % 2][:], vbs[t % 2][:], colc[:, B + 6:B + 7],
                        colc[:, B + 7:B + 8].broadcast_to([128, D]),
                        ALU.mult, ALU.add)
                for t in pair:
                    inc(e.tensor_tensor(x1bs[t % 2][:], x1bs[t % 2][:],
                                        txs[t % 2][:], ALU.add), "VV")
            blk.vector(_)
            vv_o1 = cnt["VV"] - 1
            vv_o2 = cnt["VV"]

            for t, vvn in zip(pair, (vv_o1, vv_o2)):
                def _(e, t=t, vvn=vvn):
                    sdk(e, "hbm")
                    w(e, "VV", vvn)
                    inc(e.dma_start(out=dout[t * 128:(t + 1) * 128, :],
                                    in_=x1bs[t % 2][:]), "OD", 16)
                blk.sync(_)
                od_t[t] = cnt["OD"]

        def _(e):
            w(e, "OD")
        blk.sync(_)

    return nc


_CACHE = {}


def _host_prep(inputs):
    x = np.asarray(inputs["L0_attn_x"], dtype=np.float32).reshape(16384, D)
    w_inT = np.ascontiguousarray(
        np.asarray(inputs["L0_attn_W_in"], dtype=np.float32).T)
    w_outT = np.ascontiguousarray(
        np.asarray(inputs["L0_attn_W_out"], dtype=np.float32).T)
    dt_WT = np.ascontiguousarray(
        np.asarray(inputs["L0_attn_dt_W"], dtype=np.float32).T)
    dt_b = np.asarray(inputs["L0_attn_dt_b"],
                      dtype=np.float32).reshape(16, 1)
    A_log = np.asarray(inputs["L0_attn_A_log"],
                       dtype=np.float32).reshape(16, 1)
    Dp = np.asarray(inputs["L0_attn_D_param"], dtype=np.float32)
    nw = np.asarray(inputs["L0_attn_norm_w"], dtype=np.float32)
    ln_g = np.asarray(inputs["L0_ffn_ln_g"], dtype=np.float32)
    ln_b = np.asarray(inputs["L0_ffn_ln_b"], dtype=np.float32)

    Dcols = np.empty((128, 8), np.float32)
    for g in range(8):
        Dcols[0:64, g] = Dp[2 * g]
        Dcols[64:128, g] = Dp[2 * g + 1]
    nwcols = nw.reshape(8, 128).T.copy()
    gbc = np.broadcast_to(ln_g, (128, D)).copy()
    bbc = np.broadcast_to(ln_b, (128, D)).copy()

    resetm = np.ones((16, TOK), np.float32)
    resetm[:, ::CH] = 0.0
    jj, ii = np.meshgrid(np.arange(128), np.arange(128), indexing="ij")
    m0 = np.where(ii >= jj, 0.0, NEG).astype(np.float32)
    tileind = np.zeros((16, TOK), np.float32)
    for t in range(16):
        tileind[t, t * 128:(t + 1) * 128] = 1.0
    grp4rep = np.zeros((128, 128), np.float32)
    for g in range(4):
        for m in range(4):
            grp4rep[32 * g + m, 32 * m:32 * (m + 1)] = 1.0

    common = dict(w_inT=w_inT, w_outT=w_outT, dt_WT=dt_WT, dt_b=dt_b,
                  A_log=A_log, Dcols=Dcols, nwcols=nwcols, gbc=gbc, bbc=bbc,
                  resetm=resetm, m0=m0, tileind=tileind, grp4rep=grp4rep)
    in_maps = []
    for c in range(NCORE):
        selA = np.zeros((128, 8), np.float32)
        selA[:, :c] = 1.0
        seld = np.zeros((128, 8), np.float32)
        seld[:, :c] = 1.0
        invseld = 1.0 - seld
        m = dict(common)
        m["x"] = np.ascontiguousarray(x[c * TOK:(c + 1) * TOK])
        m["selA"] = selA
        m["seld"] = seld
        m["invseld"] = invseld
        in_maps.append(m)
    return in_maps


def kernel(**inputs):
    if "nc" not in _CACHE:
        _CACHE["nc"] = build_nc()
    nc = _CACHE["nc"]
    in_maps = _host_prep(inputs)
    res = run_bass_kernel_spmd(nc, in_maps, core_ids=list(range(NCORE)))
    out = np.concatenate([res.results[c]["out"] for c in range(NCORE)],
                         axis=0)
    return out.reshape(1, 16384, D).astype(np.float32)


if __name__ == "__main__":
    build_nc()
    print("build ok")


# version 13
# speedup vs baseline: 1.3202x; 1.1403x over previous
from contextlib import ExitStack

import numpy as np

import concourse.bass as bass
import concourse.mybir as mybir
from concourse.bass_utils import run_bass_kernel_spmd

F32 = mybir.dt.float32
BF16 = mybir.dt.bfloat16
AF = mybir.ActivationFunctionType
ALU = mybir.AluOpType

NCORE = 8
TOK = 2048
D = 1024
H = 16
HD = 64
CH = 128
NT = TOK // CH
PROJ = 2576
NEG = -1.0e30
_DETECT_RACES = False
_DEBUG_DUMP = False


def build_nc():
    nc = bass.Bass("TRN2", target_bir_lowering=False,
                   detect_race_conditions=_DETECT_RACES)

    dx = nc.dram_tensor("x", [TOK, D], F32, kind="ExternalInput").ap()
    dwin = nc.dram_tensor("w_inT", [D, PROJ], F32, kind="ExternalInput").ap()
    dwout = nc.dram_tensor("w_outT", [D, D], F32, kind="ExternalInput").ap()
    ddtw = nc.dram_tensor("dt_WT", [16, 16], F32, kind="ExternalInput").ap()
    ddtb = nc.dram_tensor("dt_b", [16, 1], F32, kind="ExternalInput").ap()
    dalog = nc.dram_tensor("A_log", [16, 1], F32, kind="ExternalInput").ap()
    ddcol = nc.dram_tensor("Dcols", [128, 8], F32, kind="ExternalInput").ap()
    dnwc = nc.dram_tensor("nwcols", [128, 8], F32, kind="ExternalInput").ap()
    dgbc = nc.dram_tensor("gbc", [128, D], F32, kind="ExternalInput").ap()
    dbbc = nc.dram_tensor("bbc", [128, D], F32, kind="ExternalInput").ap()
    drst = nc.dram_tensor("resetm", [16, TOK], F32, kind="ExternalInput").ap()
    dm0 = nc.dram_tensor("m0", [128, 128], F32, kind="ExternalInput").ap()
    dtind = nc.dram_tensor("tileind", [16, TOK], F32, kind="ExternalInput").ap()
    dgrp = nc.dram_tensor("grp4rep", [128, 128], F32, kind="ExternalInput").ap()
    dselA = nc.dram_tensor("selA", [128, 8], F32, kind="ExternalInput").ap()
    dseld = nc.dram_tensor("seld", [128, 8], F32, kind="ExternalInput").ap()
    dinvsd = nc.dram_tensor("invseld", [128, 8], F32,
                            kind="ExternalInput").ap()
    dout = nc.dram_tensor("out", [TOK, D], F32, kind="ExternalOutput").ap()

    ccA_in = nc.dram_tensor("ccA_in", [128, 4 * HD], F32)
    ccA_out = nc.dram_tensor("ccA_out", [NCORE * 128, 4 * HD], F32)
    ccd_in = nc.dram_tensor("ccd_in", [16, 8], F32)
    if _DEBUG_DUMP:
        dbgY = nc.dram_tensor("dbgY", [128, 8 * TOK], BF16)
        dbgS = nc.dram_tensor("dbgS", [128, TOK], BF16)
        dbgE = nc.dram_tensor("dbgE", [128, TOK], BF16)
        dbgR = nc.dram_tensor("dbgR", [128, TOK], F32)
    ccd_out = nc.dram_tensor("ccd_out", [NCORE * 16, 8], F32)

    es = ExitStack()
    sb = lambda n, s, d: es.enter_context(nc.sbuf_tensor(n, s, d))
    psa = lambda n, s, d: es.enter_context(nc.psum_tensor(n, s, d))

    w_in = sb("w_in_sb", [128, 8 * PROJ], BF16)
    fmb = sb("fmbuf", [128, 8 * TOK], BF16)
    xptm = sb("xp_tm", [128, NT * D], BF16)
    ztm = sb("siluz_tm", [128, NT * D], BF16)
    zfm = w_in
    bpadB = w_in[:, 8 * TOK:8 * TOK + 2 * TOK]
    cwb = sb("cwbuf", [128, 8 * D], BF16)
    bpad = sb("bpad2", [128, 2 * TOK], BF16)
    cfm = sb("cfm", [128, 2 * TOK], BF16)
    btm = sb("btm", [128, 2 * 256], BF16)
    packA = sb("packA", [128, TOK], F32)
    scr32 = sb("scr32", [128, TOK], F32)
    snaps = sb("snaps", [128, 16 * 4 * HD], BF16)
    srun = sb("srun", [128, 4 * HD], F32)
    wxb = cwb[:, 5 * D:6 * D]
    scobE = ztm[:, 0:TOK]
    scobO = ztm[:, TOK:2 * TOK]
    onerow = ztm[0:1, TOK:TOK + 128]
    expb = ztm[:, 2 * TOK:3 * TOK]
    eag = ztm[:, 3 * TOK:4 * TOK]
    ub = ztm[:, 4 * TOK:4 * TOK + D]
    vb = ztm[:, 4 * TOK + D:5 * TOK]
    gb = ztm[:, 5 * TOK:5 * TOK + D]
    bb = ztm[:, 5 * TOK + D:6 * TOK]
    fbfc = ztm[:, 6 * TOK:7 * TOK]
    postbuf = sb("postbuf", [128, TOK], F32)
    mob = postbuf[:, 0:D]
    x1b = postbuf[:, D:TOK]
    rhsA = postbuf[0:32, :]
    fbf = fbfc[0:16, :]
    lhsA = sb("lhsA", [32, 128], F32)
    m0b = sb("m0_sb", [128, 128], BF16)
    grpb = sb("grp_sb", [128, 128], F32)
    dtwb = sb("dtw_sb", [16, 16], F32)
    dtbb = sb("dtb_sb", [16, 1], F32)
    alogb = sb("alog_sb", [16, 1], F32)
    acolb = sb("acol_sb", [16, 1], F32)
    dcolb = sb("dcol_sb", [128, 8], F32)
    nwcb = sb("nwc_sb", [128, 8], F32)
    selAb = sb("selA_sb", [128, 8], F32)
    seldb = sb("seld_sb", [128, 8], F32)
    invsdb = sb("invsd_sb", [128, 8], F32)
    rcols = sb("rcols", [128, NT], F32)
    mrcols = sb("mrcols", [128, NT], F32)
    ftm = sb("ftm", [128, NT * 16], BF16)
    e16 = sb("e16", [128, 64], F32)
    sp4 = sb("sp4", [4, 64], F32)
    dcolsb = sb("dcolsb", [128, 64], F32)
    pcolsb = sb("pcolsb", [128, 64], F32)
    dcoreb = sb("dcoreb", [16, 8], F32)
    ddall = sb("ddall", [128, 1], F32)
    drcols = sb("drcols", [128, 32], F32)
    dvb = sb("dvb", [128, 32], F32)
    avb = sb("avb", [128, 4 * HD], F32)
    hbuf = sb("hbuf", [128, 4 * HD], BF16)
    colc = sb("colc", [128, 32], F32)
    onecol = sb("onecol", [128, 1], BF16)
    epsc = sb("epsc", [128, 3], F32)


    xnrot = cwb[:, 4 * D:5 * D]
    xrotL = cwb[:, 6 * D:8 * D].bitcast(F32)
    xrotT = ztm[:, 7 * TOK:8 * TOK].bitcast(F32)

    pfm = psa("pfm", [128, 1024], F32)
    pfm2 = psa("pfm2", [128, 1024], F32)
    pbig = psa("pbig", [128, 2048], F32)

    names = ["SD", "GD", "CD", "TT", "VV", "CC", "OD", "CS", "CT"]
    sems = {n: es.enter_context(nc.semaphore(n)) for n in names}
    cnt = {n: 0 for n in names}

    with es, nc.Block() as blk:
        def inc(ins, name, k=1):
            ins.then_inc(sems[name], k)
            cnt[name] += k

        def w(e, name, val=None):
            e.wait_ge(sems[name], cnt[name] if val is None else val)

        def barrier(e):
            for n in names:
                if cnt[n]:
                    e.wait_ge(sems[n], cnt[n])

        sd_kinds = set()

        def sdk(e, kind):
            if kind in ("tr", "s2s") and (sd_kinds - {kind, "hbm"}):
                e.wait_ge(sems["SD"], cnt["SD"])
                sd_kinds.clear()
            sd_kinds.add(kind)

        def _(e):
            for k in range(8):
                inc(e.dma_start(out=w_in[:, k * PROJ:(k + 1) * PROJ],
                                in_=dwin[k * 128:(k + 1) * 128, :]), "GD", 16)
            inc(e.dma_start(out=m0b[:], in_=dm0), "GD", 16)
        blk.gpsimd(_)
        gd_w = cnt["GD"]

        def _(e):
            sdk(e, "hbm")
            for t_, d_ in ((grpb, dgrp), (dtwb, ddtw),
                           (dtbb, ddtb), (alogb, dalog), (dcolb, ddcol),
                           (nwcb, dnwc), (selAb, dselA), (seldb, dseld),
                           (invsdb, dinvsd)):
                inc(e.dma_start(out=t_[:], in_=d_), "SD", 16)
            inc(e.dma_start(out=rhsA[1:17, :], in_=dtind), "SD", 16)
        blk.sync(_)
        sd_w = cnt["SD"]

        def _(e):
            e.memset(onecol[:], 1.0)
            e.memset(epsc[:, 0:1], 1e-5)
            e.memset(dcoreb[:], 0.0)
            e.memset(e16[32:48, 16:32], 1.0)
            e.memset(epsc[:, 1:2], 1.0)
            e.memset(epsc[:, 2:3], 1e-6)
            e.memset(lhsA[:], 0.0)
            e.memset(lhsA[0:1, :], 1.0)
            e.memset(srun[:], 0.0)
            e.memset(hbuf[:], 0.0)
            inc(e.memset(pbig[:], 0.0), "VV")
        blk.vector(_)
        vv_init = cnt["VV"]

        xrotL2 = cwb[:, 0:2 * D].bitcast(F32)
        xnrot2 = cwb[:, 2 * D:3 * D]
        xrotsL = [xrotL, xrotL2]
        xnrots = [xnrot, xnrot2]
        CBL = [0, 16]
        xn_sd = []
        vv_xn_t = {}
        sd_tr_t = {}
        for t in range(NT):
            par = t % 2

            def _(e, t=t, par=par):
                sdk(e, "hbm")
                if t >= 2:
                    w(e, "VV", vv_xn_t[t - 2])
                inc(e.dma_start(out=xrotsL[par][:],
                                in_=dx[t * 128:(t + 1) * 128, :]), "SD", 16)
            blk.sync(_)
            sd_x = cnt["SD"]

            def _(e, t=t, par=par, sd_x=sd_x):
                w(e, "SD", sd_x)
                B = CBL[par]
                e.tensor_scalar(scr32[:, 0:D], xrotsL[par][:], 1.0, 0.0,
                                ALU.mult, ALU.add,
                                accum_out=colc[:, B:B + 1])
                e.scalar_tensor_tensor(scr32[:, 0:D], xrotsL[par][:], 1.0,
                                       xrotsL[par][:], ALU.mult, ALU.mult,
                                       accum_out=colc[:, B + 1:B + 2])
                e.tensor_scalar(colc[:, B + 2:B + 3], colc[:, B:B + 1],
                                1.0 / D, None, ALU.mult)
                e.tensor_scalar(colc[:, B + 3:B + 4], colc[:, B + 1:B + 2],
                                1.0 / D, None, ALU.mult)
                inc(e.scalar_tensor_tensor(colc[:, B + 4:B + 5],
                                           colc[:, B + 2:B + 3],
                                           colc[:, B + 2:B + 3],
                                           colc[:, B + 3:B + 4],
                                           ALU.mult, ALU.subtract), "VV")
            blk.vector(_)
            vv_st = cnt["VV"]

            def _(e, t=t, par=par, vv_st=vv_st):
                w(e, "VV", vv_st)
                B = CBL[par]
                e.activation(colc[:, B + 5:B + 6], colc[:, B + 4:B + 5],
                             AF.Ln, bias=epsc[:, 0:1], scale=-1.0)
                inc(e.activation(colc[:, B + 6:B + 7], colc[:, B + 5:B + 6],
                                 AF.Exp, bias=0.0, scale=-0.5), "CC")
            blk.scalar(_)
            cc_r = cnt["CC"]

            def _(e, t=t, par=par, cc_r=cc_r):
                w(e, "CC", cc_r)
                B = CBL[par]
                inc(e.tensor_scalar(colc[:, B + 7:B + 8],
                                    colc[:, B + 2:B + 3],
                                    colc[:, B + 6:B + 7], None, ALU.mult),
                    "VV")
            blk.vector(_)
            vv_mr = cnt["VV"]

            def _(e, t=t, par=par, vv_mr=vv_mr):
                w(e, "VV", vv_mr)
                if t >= 2:
                    w(e, "SD", sd_tr_t[t - 2])
                B = CBL[par]
                inc(e.scalar_tensor_tensor(
                    xnrots[par][:], xrotsL[par][:], colc[:, B + 6:B + 7],
                    colc[:, B + 7:B + 8].broadcast_to([128, D]),
                    ALU.mult, ALU.subtract), "VV")
            blk.vector(_)
            vv_xn_t[t] = cnt["VV"]

            def _(e, t=t, par=par):
                sdk(e, "tr")
                w(e, "VV", vv_xn_t[t])
                for k in range(8):
                    inc(e.dma_start(
                        out=fmb[:, k * TOK + t * 128:k * TOK + (t + 1) * 128],
                        in_=xnrots[par][:, k * 128:(k + 1) * 128],
                        transpose=True), "SD", 16)
            blk.sync(_)
            sd_tr_t[t] = cnt["SD"]
            xn_sd.append(cnt["SD"])

        fm_tt = []
        for mt in range(5):
            for half in range(2):
                pp = pfm if (mt * 2 + half) % 2 == 0 else pfm2
                M = 16 if mt == 4 else 128

                def _(e, mt=mt, half=half, pp=pp, M=M):
                    if mt == 0 and half == 0:
                        w(e, "SD", xn_sd[-1])
                        w(e, "GD", gd_w)
                    if mt * 2 + half >= 2:
                        w(e, "VV")
                        w(e, "CC")
                    for k in range(8):
                        lhs = w_in[:, k * PROJ + 2048 + mt * 128:
                                   k * PROJ + 2048 + mt * 128 + M]
                        for q in range(2):
                            ins = e.matmul(
                                pp[0:M, q * 512:(q + 1) * 512], lhs,
                                fmb[:, k * TOK + half * 1024 + q * 512:
                                    k * TOK + half * 1024 + (q + 1) * 512],
                                start=(k == 0), stop=(k == 7))
                    inc(ins, "TT")
                blk.tensor(_)
                fm_tt.append(cnt["TT"])

                if mt < 2:
                    def _(e, mt=mt, half=half, pp=pp):
                        w(e, "TT", fm_tt[-1])
                        inc(e.tensor_copy(
                            cwb[:, mt * 2048 + half * 1024:
                                mt * 2048 + (half + 1) * 1024], pp[:]), "VV")
                    blk.vector(_)
                elif mt < 4:
                    def _(e, mt=mt, half=half, pp=pp):
                        w(e, "TT", fm_tt[-1])
                        inc(e.copy(
                            cfm[:, (mt - 2) * 2048 + half * 1024:
                                (mt - 2) * 2048 + (half + 1) * 1024],
                            pp[:]), "CC")
                    blk.scalar(_)
                else:
                    def _(e, half=half, pp=pp):
                        w(e, "TT", fm_tt[-1])
                        inc(e.tensor_copy(
                            packA[0:16, half * 1024:(half + 1) * 1024],
                            pp[0:16, :]), "VV")
                    blk.vector(_)
        fm_vv = cnt["VV"]
        fm_cc = cnt["CC"]

        def _(e):
            inc(e.memset(bpad[:], 0.0), "VV")
        blk.vector(_)
        vv_bm = cnt["VV"]

        def _(e):
            sdk(e, "s2s")
            w(e, "VV", vv_bm)
            w(e, "VV", fm_vv)
            for h in range(0, H, 2):
                gp = h // 2
                dst_t = gp // 4
                base = 32 * (gp % 4)
                inc(e.dma_start(
                    out=bpad[base:base + 16, dst_t * TOK:(dst_t + 1) * TOK],
                    in_=cwb[16 * (h % 8):16 * (h % 8) + 16,
                            (h // 8) * 2048:(h // 8 + 1) * 2048]), "SD", 16)
        blk.sync(_)
        sd_f2 = cnt["SD"]


        def _(e):
            w(e, "VV", fm_vv)
            w(e, "SD", sd_w)
            w(e, "VV")
            for q in range(4):
                ins = e.matmul(pbig[0:16, q * 512:(q + 1) * 512], dtwb[:],
                               packA[0:16, q * 512:(q + 1) * 512],
                               start=True, stop=True)
            inc(ins, "TT")
        blk.tensor(_)
        tt_dt = cnt["TT"]

        def _(e):
            w(e, "TT", tt_dt)
            e.activation(scr32[32:48, :], pbig[0:16, :], AF.Abs,
                         bias=dtbb[:], scale=1.0)
            e.activation(scr32[64:80, :], scr32[32:48, :], AF.Exp,
                         bias=0.0, scale=-1.0)
            e.activation(scr32[96:112, :], scr32[64:80, :], AF.Ln,
                         bias=epsc[0:16, 1:2], scale=1.0)
            e.activation(packA[96:112, :], pbig[0:16, :], AF.Relu,
                         bias=dtbb[:], scale=1.0)
            inc(e.activation(acolb[:], alogb[:], AF.Exp, bias=0.0,
                             scale=1.0), "CC")
        blk.scalar(_)
        cc_d1 = cnt["CC"]

        def _(e):
            w(e, "CC", cc_d1)
            e.tensor_add(packA[0:16, :], scr32[96:112, :], packA[96:112, :])
            e.tensor_scalar(acolb[:], acolb[:], -1.0, None, ALU.mult)
            inc(e.tensor_scalar(scr32[32:48, :], packA[0:16, :], acolb[:],
                                None, ALU.mult), "VV")
        blk.vector(_)
        vv_da = cnt["VV"]

        def _(e):
            sdk(e, "hbm")
            w(e, "VV", vv_da)
            inc(e.dma_start(out=packA[96:112, :], in_=drst), "SD", 16)
        blk.sync(_)
        sd_rst = cnt["SD"]

        def _(e):
            w(e, "VV", vv_da)
            inc(e.activation(scr32[64:80, :], packA[0:16, :], AF.Ln,
                             bias=0.0, scale=1.0), "CC")
        blk.scalar(_)
        cc_lndt = cnt["CC"]

        def _(e):
            w(e, "CC", cc_lndt)
            w(e, "SD", sd_rst)
            e.tensor_copy(scr32[96:112, :], scr32[32:48, :])
            inc(e.tensor_tensor_scan(packA[32:48, :], packA[96:112, :],
                                     scr32[96:112, :], 0.0, ALU.mult,
                                     ALU.add), "VV")
        blk.vector(_)
        vv_a = cnt["VV"]

        def _(e):
            w(e, "VV", vv_a)
            e.tensor_copy(scr32[32:48, :], scr32[64:80, :])
            e.tensor_sub(packA[64:80, :], scr32[32:48, :], packA[32:48, :])
            av3 = packA[32:48, :].rearrange("h (c i) -> h c i", c=NT)
            e.tensor_copy(e16[32:48, 0:16].unsqueeze(2), av3[:, :, 127:128])
            bcv = e16[32:48, 0:16].unsqueeze(2).broadcast_to([16, NT, 128])
            e.tensor_tensor(
                scr32[96:112, :].rearrange("h (c i) -> h c i", c=NT), bcv,
                av3, ALU.subtract)
            inc(e.tensor_tensor_scan(e16[32:48, 32:48], e16[32:48, 16:32],
                                     e16[32:48, 0:16], 0.0, ALU.mult,
                                     ALU.add), "VV")
        blk.vector(_)
        vv_pib = cnt["VV"]

        def _(e):
            w(e, "VV", vv_pib)
            inc(e.tensor_sub(e16[32:48, 48:64], e16[32:48, 32:48],
                             e16[32:48, 0:16]), "VV")
        blk.vector(_)
        vv_d2 = cnt["VV"]

        def _(e):
            w(e, "VV", vv_d2)
            e.activation(fbf[:], scr32[96:112, :], AF.Exp, bias=0.0,
                         scale=1.0)
            inc(e.activation(dcoreb[:, 0:1], e16[32:48, 47:48], AF.Exp,
                             bias=0.0, scale=1.0), "CC")
        blk.scalar(_)
        cc_d3 = cnt["CC"]

        def _(e):
            w(e, "CC", cc_d3)
            inc(e.tensor_tensor(fbf[:], packA[0:16, :], fbf[:], ALU.mult),
                "VV")
        blk.vector(_)
        vv_f = cnt["VV"]

        def _(e):
            sdk(e, "tr")
            w(e, "VV", vv_f)
            for t in range(NT):
                inc(e.dma_start(out=ftm[:, t * 16:(t + 1) * 16],
                                in_=fbf[:, t * 128:(t + 1) * 128],
                                transpose=True), "SD", 16)
        blk.sync(_)
        sd_ftm = cnt["SD"]

        def _(e):
            sdk(e, "s2s")
            w(e, "VV", vv_d2)
            for gi in range(4):
                inc(e.dma_start(out=sp4[0:4, 16 * gi:16 * (gi + 1)],
                                in_=e16[32 + 4 * gi:32 + 4 * gi + 4, 0:16]),
                    "SD", 16)
        blk.sync(_)
        sd_ae = cnt["SD"]

        def _(e):
            w(e, "SD", sd_ae)
            w(e, "SD", sd_w)
            for gi in range(4):
                ins = e.matmul(pfm2[:, gi * 16:(gi + 1) * 16],
                               grpb[0:4, :],
                               sp4[0:4, 16 * gi:16 * (gi + 1)],
                               start=True, stop=True)
            inc(ins, "TT")
        blk.tensor(_)
        tt_dc = cnt["TT"]

        def _(e):
            w(e, "TT", tt_dc)
            inc(e.activation(dcolsb[:], pfm2[:, 0:64], AF.Exp, bias=0.0,
                             scale=1.0), "CC")
        blk.scalar(_)
        cc_dcol = cnt["CC"]

        def _(e):
            sdk(e, "s2s")
            w(e, "CC", cc_dcol)
            w(e, "VV", vv_d2)
            for gi in range(4):
                inc(e.dma_start(out=sp4[0:4, 16 * gi:16 * (gi + 1)],
                                in_=e16[32 + 4 * gi:32 + 4 * gi + 4, 48:64]),
                    "SD", 16)
        blk.sync(_)
        sd_pe = cnt["SD"]

        def _(e):
            w(e, "SD", sd_pe)
            for gi in range(4):
                ins = e.matmul(pfm2[:, gi * 16:(gi + 1) * 16],
                               grpb[0:4, :],
                               sp4[0:4, 16 * gi:16 * (gi + 1)],
                               start=True, stop=True)
            inc(ins, "TT")
        blk.tensor(_)
        tt_pc = cnt["TT"]

        def _(e):
            w(e, "TT", tt_pc)
            inc(e.activation(pcolsb[:], pfm2[:, 0:64], AF.Exp, bias=0.0,
                             scale=1.0), "CC")
        blk.scalar(_)
        cc_pcol = cnt["CC"]

        tt_projtm = []
        sd_xpfm = []
        for t in range(NT):
            def _(e, t=t):
                w(e, "SD", xn_sd[t])
                if t == 0:
                    w(e, "GD", gd_w)
                    w(e, "CC", cc_pcol)
                    w(e, "TT", fm_tt[-1])
                    w(e, "SD", sd_ftm)
                else:
                    w(e, "CC")
                for k in range(8):
                    st = fmb[:, k * TOK + t * 128:k * TOK + (t + 1) * 128]
                    for q in range(2):
                        e.matmul(pfm[:, q * 512:(q + 1) * 512], st,
                                 w_in[:, k * PROJ + q * 512:
                                      k * PROJ + (q + 1) * 512],
                                 start=(k == 0), stop=(k == 7))
                    for q in range(2):
                        ins = e.matmul(
                            pfm2[:, q * 512:(q + 1) * 512], st,
                            w_in[:, k * PROJ + 1024 + q * 512:
                                 k * PROJ + 1024 + (q + 1) * 512],
                            start=(k == 0), stop=(k == 7))
                inc(ins, "TT")
            blk.tensor(_)
            tt_projtm.append(cnt["TT"])
            tt_p = cnt["TT"]

            def _(e, t=t, tt_p=tt_p):
                w(e, "TT", tt_p)
                e.copy(xptm[:, t * D:(t + 1) * D], pfm[:])
                inc(e.activation(ztm[:, t * D:(t + 1) * D], pfm2[:], AF.Silu,
                                 bias=0.0, scale=1.0), "CC")
            blk.scalar(_)
            cc_ev = cnt["CC"]

            def _(e, t=t, cc_ev=cc_ev):
                sdk(e, "tr")
                w(e, "CC", cc_ev)
                if t >= 2:
                    w(e, "TT")
                for r in range(2):
                    inc(e.dma_start(
                        out=btm[:, (t % 2) * 256 + r * 128:
                                (t % 2) * 256 + (r + 1) * 128],
                        in_=cwb[:, r * 2048 + t * 128:
                                r * 2048 + (t + 1) * 128],
                        transpose=True), "SD", 16)
            blk.sync(_)
            sd_btm = cnt["SD"]

            def _(e, t=t):
                sdk(e, "tr")
                for k in range(8):
                    inc(e.dma_start(
                        out=fmb[:, k * TOK + t * 128:k * TOK + (t + 1) * 128],
                        in_=xptm[:, t * D + k * 128:t * D + (k + 1) * 128],
                        transpose=True), "SD", 16)
            blk.sync(_)
            sd_xpfm.append(cnt["SD"])

            def _(e, t=t, cc_ev=cc_ev):
                w(e, "CC", cc_ev)
                w(e, "SD", sd_ftm)
                fv = ftm[:, t * 16:(t + 1) * 16].unsqueeze(2).broadcast_to(
                    [128, 16, HD])
                xv = xptm[:, t * D:(t + 1) * D].rearrange(
                    "p (h d) -> p h d", h=16)
                inc(e.tensor_tensor(wxb[:].rearrange("p (h d) -> p h d",
                                                     h=16),
                                    xv, fv, ALU.mult), "VV")
            blk.vector(_)
            vv_wx = cnt["VV"]

            def _(e, t=t, vv_wx=vv_wx, sd_btm=sd_btm):
                w(e, "VV", vv_wx)
                w(e, "SD", sd_btm)
                if t == 0:
                    w(e, "SD", sd_f2)
                    w(e, "VV", vv_init)
                else:
                    w(e, "VV")
                for h in range(H):
                    gi, m = h // 4, h % 4
                    ins = e.matmul(
                        pbig[32 * m:32 * m + 16, gi * 512:gi * 512 + 64],
                        btm[:, (t % 2) * 256 + (h // 8) * 128 + 16 * (h % 8):
                            (t % 2) * 256 + (h // 8) * 128 + 16 * (h % 8)
                            + 16],
                        wxb[:, h * HD:(h + 1) * HD],
                        start=True, stop=True,
                        tile_position=(0, 32 * m))
                inc(ins, "TT")
            blk.tensor(_)
            tt_cs = cnt["TT"]

            def _(e, t=t, tt_cs=tt_cs):
                w(e, "TT", tt_cs)
                if t == 0:
                    w(e, "CC", cc_dcol)
                for gi in range(4):
                    e.tensor_copy(
                        snaps[:, (t * 4 + gi) * HD:(t * 4 + gi + 1) * HD],
                        srun[:, gi * HD:(gi + 1) * HD])
                    ins = e.scalar_tensor_tensor(
                        srun[:, gi * HD:(gi + 1) * HD],
                        srun[:, gi * HD:(gi + 1) * HD],
                        dcolsb[:, gi * 16 + t:gi * 16 + t + 1],
                        pbig[:, gi * 512:gi * 512 + 64],
                        ALU.mult, ALU.add)
                inc(ins, "VV")
            blk.vector(_)

        def _(e):
            w(e, "VV")
            w(e, "CC", cc_d3)
            inc(e.dma_start(out=ccA_in.ap(), in_=srun[:]), "GD", 16)
            inc(e.dma_start(out=ccd_in.ap(), in_=dcoreb[:]), "GD", 16)
            w(e, "GD")
            inc(e.collective_compute(
                "AllGather", ALU.bypass,
                replica_groups=[list(range(NCORE))],
                ins=[ccA_in.ap().opt()], outs=[ccA_out.ap().opt()]), "CS")
            inc(e.collective_compute(
                "AllGather", ALU.bypass,
                replica_groups=[list(range(NCORE))],
                ins=[ccd_in.ap().opt()], outs=[ccd_out.ap().opt()]), "CT")
            e.wait_ge(sems["CS"], 1)
            e.wait_ge(sems["CT"], 1)
            inc(e.dma_start(out=ddall[:], in_=ccd_out.ap()[:, 0:1],
                            allow_slow_non_contiguous=True), "GD", 16)
        blk.gpsimd(_)
        gd_cc = cnt["GD"]

        def _(e):
            sdk(e, "s2s")
            w(e, "GD", gd_cc)
            for gi in range(4):
                for r in range(NCORE):
                    inc(e.dma_start(
                        out=sp4[0:4, 8 * gi + r:8 * gi + r + 1],
                        in_=ddall[16 * r + 4 * gi:16 * r + 4 * gi + 4, :],
                        allow_slow_non_contiguous=True), "SD", 16)
        blk.sync(_)
        sd_dd = cnt["SD"]

        def _(e):
            w(e, "SD", sd_dd)
            w(e, "CC")
            for gi in range(4):
                ins = e.matmul(pfm2[:, gi * 8:(gi + 1) * 8],
                               grpb[0:4, :],
                               sp4[0:4, 8 * gi:8 * (gi + 1)],
                               start=True, stop=True)
            inc(ins, "TT")
        blk.tensor(_)
        tt_dr = cnt["TT"]

        def _(e):
            w(e, "TT", tt_dr)
            for gi in range(4):
                ins = e.tensor_copy(drcols[:, gi * 8:(gi + 1) * 8],
                                    pfm2[:, gi * 8:(gi + 1) * 8])
            inc(ins, "VV")
        blk.vector(_)
        vv_dr = cnt["VV"]

        def _(e):
            w(e, "VV", vv_dr)
            w(e, "SD", sd_w)
            dv3 = dvb[:].rearrange("p (g r) -> p g r", g=4)
            dr3 = drcols[:].rearrange("p (g r) -> p g r", g=4)
            for r in range(NCORE):
                ins = e.scalar_tensor_tensor(
                    dv3[:, :, r], dr3[:, :, r], seldb[:, r:r + 1],
                    invsdb[:, r:r + 1].broadcast_to([128, 4]),
                    ALU.mult, ALU.add)
            inc(ins, "VV")
        blk.vector(_)
        vv_dv = cnt["VV"]

        for r in range(NCORE):
            def _(e, r=r):
                w(e, "GD", gd_cc)
                if r >= 1:
                    w(e, "VV")
                inc(e.dma_start(
                    out=avb[:],
                    in_=ccA_out.ap()[r * 128:(r + 1) * 128, :]), "GD", 16)
            blk.gpsimd(_)
            gd_av = cnt["GD"]

            def _(e, r=r, gd_av=gd_av):
                w(e, "GD", gd_av)
                if r == 0:
                    w(e, "VV", vv_dv)
                for gi in range(4):
                    e.tensor_scalar(avb[:, gi * HD:(gi + 1) * HD],
                                    avb[:, gi * HD:(gi + 1) * HD],
                                    selAb[:, r:r + 1], None, ALU.mult)
                for gi in range(4):
                    ins = e.scalar_tensor_tensor(
                        hbuf[:, gi * HD:(gi + 1) * HD],
                        hbuf[:, gi * HD:(gi + 1) * HD],
                        dvb[:, gi * 8 + r:gi * 8 + r + 1],
                        avb[:, gi * HD:(gi + 1) * HD],
                        ALU.mult, ALU.add)
                inc(ins, "VV")
            blk.vector(_)
        vv_h = cnt["VV"]

        def _(e):
            w(e, "VV", vv_h)
            w(e, "CC", cc_pcol)
            for t in range(NT):
                for gi in range(4):
                    e.tensor_scalar(avb[:, gi * HD:(gi + 1) * HD],
                                    hbuf[:, gi * HD:(gi + 1) * HD],
                                    pcolsb[:, gi * 16 + t:gi * 16 + t + 1],
                                    None, ALU.mult)
                for gi in range(4):
                    ins = e.tensor_add(
                        snaps[:, (t * 4 + gi) * HD:(t * 4 + gi + 1) * HD],
                        snaps[:, (t * 4 + gi) * HD:(t * 4 + gi + 1) * HD],
                        avb[:, gi * HD:(gi + 1) * HD])
            inc(ins, "VV")
        blk.vector(_)
        vv_snapc = cnt["VV"]

        def _(e):
            sdk(e, "tr")
            w(e, "CC")
            w(e, "TT", tt_projtm[-1])
            for t in range(NT):
                for k in range(8):
                    inc(e.dma_start(
                        out=zfm[:, k * TOK + t * 128:k * TOK + (t + 1) * 128],
                        in_=ztm[:, t * D + k * 128:t * D + (k + 1) * 128],
                        transpose=True), "SD", 16)
        blk.sync(_)
        sd_zfm = cnt["SD"]

        def _(e):
            w(e, "TT", tt_projtm[-1])
            inc(e.memset(bpadB[:], 0.0), "VV")
        blk.vector(_)
        vv_bpB = cnt["VV"]

        def _(e):
            sdk(e, "s2s")
            w(e, "VV", vv_bpB)
            for h in range(1, H, 2):
                gp = h // 2
                dst_t = gp // 4
                base = 32 * (gp % 4) + 16
                inc(e.dma_start(
                    out=bpadB[base:base + 16, dst_t * TOK:(dst_t + 1) * TOK],
                    in_=cwb[16 * (h % 8):16 * (h % 8) + 16,
                            (h // 8) * 2048:(h // 8 + 1) * 2048]), "SD", 16)
        blk.sync(_)
        sd_bpB = cnt["SD"]

        def _(e):
            w(e, "SD", sd_f2)
            w(e, "SD", sd_bpB)
            w(e, "SD", xn_sd[-1])
            w(e, "CC")
            w(e, "VV", fm_vv)
            w(e, "VV")
            inc(e.memset(cwb[:], 0.0), "VV")
        blk.vector(_)
        vv_cm = cnt["VV"]

        def _(e):
            sdk(e, "s2s")
            w(e, "VV", vv_cm)
            w(e, "CC", fm_cc)
            for h in range(H):
                inc(e.dma_start(
                    out=cwb[32 * (h % 4):32 * (h % 4) + 16,
                            (h // 4) * 2048:(h // 4 + 1) * 2048],
                    in_=cfm[16 * (h % 8):16 * (h % 8) + 16,
                            (h // 8) * 2048:(h // 8 + 1) * 2048]), "SD", 16)
            w(e, "CC")
            for gi in range(4):
                inc(e.dma_start(out=scr32[32 * gi:32 * gi + 4, :],
                                in_=packA[32 + 4 * gi:32 + 4 * gi + 4, :]),
                    "SD", 16)
        blk.sync(_)
        sd_cp = cnt["SD"]

        for gi in range(4):
            def _(e, gi=gi):
                w(e, "SD", sd_cp)
                w(e, "SD", sd_zfm)
                if gi == 0:
                    w(e, "VV", vv_dr)
                    w(e, "CC")
                else:
                    w(e, "CC")
                for q in range(4):
                    pp = pfm2 if q < 2 else pfm
                    ins = e.matmul(pp[:, (q % 2) * 512:(q % 2 + 1) * 512],
                                   grpb[32 * gi:32 * gi + 4, :],
                                   scr32[32 * gi:32 * gi + 4,
                                         q * 512:(q + 1) * 512],
                                   start=True, stop=True,
                                   tile_position=(32 * gi, 0))
                inc(ins, "TT")
            blk.tensor(_)
            tt_ea = cnt["TT"]

            def _(e, gi=gi, tt_ea=tt_ea):
                w(e, "TT", tt_ea)
                e.activation(eag[:, 0:512], pfm2[:, 0:512], AF.Exp,
                             bias=0.0, scale=1.0)
                e.activation(eag[:, 512:1024], pfm2[:, 512:1024], AF.Exp,
                             bias=0.0, scale=1.0)
                e.activation(eag[:, 1024:1536], pfm[:, 0:512], AF.Exp,
                             bias=0.0, scale=1.0)
                inc(e.activation(eag[:, 1536:2048], pfm[:, 512:1024], AF.Exp,
                                 bias=0.0, scale=1.0), "CC")
            blk.scalar(_)
            cc_ea = cnt["CC"]

            def _(e, gi=gi, cc_ea=cc_ea):
                w(e, "CC", cc_ea)
                w(e, "SD", sd_cp)
                inc(e.tensor_tensor(cwb[:, gi * 2048:(gi + 1) * 2048],
                                    cwb[:, gi * 2048:(gi + 1) * 2048],
                                    eag[:], ALU.mult), "VV")
            blk.vector(_)
        vv_cpr = cnt["VV"]

        tt_lastA = [0]
        for h in range(H):
            pair, odd = h // 2, h % 2
            scob = scobO if odd else scobE

            def _(e, h=h):
                sdk(e, "s2s")
                if h >= 1:
                    w(e, "TT", tt_lastA[0])
                else:
                    w(e, "VV")
                inc(e.dma_start(out=lhsA[1:17, :],
                                in_=packA[64 + h:65 + h, :]), "SD", 16)
                inc(e.dma_start(out=rhsA[0:1, :],
                                in_=packA[32 + h:33 + h, :]), "SD", 16)
            blk.sync(_)
            sd_ab = cnt["SD"]

            for q in range(4):
                def _(e, h=h, q=q, sd_ab=sd_ab):
                    w(e, "SD", sd_ab)
                    w(e, "VV")
                    inc(e.matmul(pfm[:, (q % 2) * 512:(q % 2 + 1) * 512],
                                 lhsA[0:17, :],
                                 rhsA[0:17, q * 512:(q + 1) * 512],
                                 start=True, stop=True), "TT")
                blk.tensor(_)
                tt_a = cnt["TT"]
                if q == 3:
                    tt_lastA[0] = tt_a

                def _(e, h=h, q=q, tt_a=tt_a):
                    w(e, "TT", tt_a)
                    if h == 0 and q == 0:
                        w(e, "GD", gd_w)
                    m0v = m0b[:].unsqueeze(1).broadcast_to([128, 4, 128])
                    inc(e.tensor_tensor(
                        scr32[:, q * 512:(q + 1) * 512].rearrange(
                            "p (a b) -> p a b", a=4),
                        pfm[:, (q % 2) * 512:(q % 2 + 1) * 512].rearrange(
                            "p (a b) -> p a b", a=4),
                        m0v, ALU.add), "VV")
                blk.vector(_)
            vv_arg = cnt["VV"]

            def _(e, h=h, vv_arg=vv_arg):
                w(e, "VV", vv_arg)
                e.activation(expb[:, 0:1024], scr32[:, 0:1024], AF.Exp,
                             bias=0.0, scale=1.0)
                inc(e.activation(expb[:, 1024:2048], scr32[:, 1024:2048],
                                 AF.Exp, bias=0.0, scale=1.0), "CC")
            blk.scalar(_)
            cc_exp = cnt["CC"]

            def _(e, h=h):
                if h == 0:
                    w(e, "SD", sd_cp)
                    w(e, "VV", vv_cpr)
                w(e, "VV")
                gp = h // 2
                base = 32 * (gp % 4)
                dst_t = gp // 4
                bp = bpad if h % 2 == 0 else bpadB
                for T in range(NT):
                    ins = e.matmul(
                        pbig[:, T * 128:(T + 1) * 128],
                        bp[base:base + 32,
                           dst_t * TOK + T * 128:dst_t * TOK +
                           (T + 1) * 128],
                        cfm[base:base + 32,
                            dst_t * TOK + T * 128:dst_t * TOK +
                            (T + 1) * 128],
                        start=True, stop=True, tile_position=(base, 0))
                inc(ins, "TT")
            blk.tensor(_)
            tt_raw = cnt["TT"]

            def _(e, h=h, tt_raw=tt_raw, cc_exp=cc_exp, scob=scob):
                w(e, "TT", tt_raw)
                w(e, "CC", cc_exp)
                e.tensor_tensor(scob[:, 0:1024], pbig[:, 0:1024],
                                expb[:, 0:1024], ALU.mult)
                inc(e.tensor_tensor(scob[:, 1024:2048], pbig[:, 1024:2048],
                                    expb[:, 1024:2048], ALU.mult), "VV")
            blk.vector(_)
            vv_sco = cnt["VV"]

            if _DEBUG_DUMP and h == 0:
                def _(e, vv_sco=vv_sco):
                    w(e, "VV", vv_sco)
                    inc(e.dma_start(out=dbgS.ap(), in_=scobE[:]), "SD", 16)
                    inc(e.dma_start(out=dbgE.ap(), in_=expb[:]), "SD", 16)
                blk.sync(_)
                def _(e, vv_sco=vv_sco):
                    w(e, "VV", vv_sco)
                    inc(e.tensor_copy(scr32[:], pbig[:]), "VV")
                blk.vector(_)
                vv_dbg = cnt["VV"]
                def _(e, vv_dbg=vv_dbg):
                    w(e, "VV", vv_dbg)
                    inc(e.dma_start(out=dbgR.ap(), in_=scr32[:]), "SD", 16)
                blk.sync(_)

            if odd:
                def _(e, pair=pair, vv_sco=vv_sco):
                    w(e, "VV", vv_sco)
                    w(e, "VV", vv_snapc)
                    for hh in (2 * pair, 2 * pair + 1):
                        par = hh % 2
                        sc = scobO if par else scobE
                        base = 32 * (hh % 4)
                        gtile = hh // 4
                        for T in range(NT):
                            ins = e.matmul(
                                pbig[64 * par:64 * par + 64,
                                     T * 128:(T + 1) * 128],
                                xptm[:, T * D + hh * HD:
                                     T * D + (hh + 1) * HD],
                                sc[:, T * 128:(T + 1) * 128],
                                start=(T % 4 == 0), stop=False,
                                tile_position=(0, 64 * par))
                        inc(ins, "TT")
                        w(e, "TT")
                        for T in range(NT):
                            ins = e.matmul(
                                pbig[64 * par:64 * par + 64,
                                     T * 128:(T + 1) * 128],
                                snaps[base:base + 32,
                                      (T * 4 + gtile) * HD:
                                      (T * 4 + gtile + 1) * HD],
                                cwb[base:base + 32,
                                    gtile * 2048 + T * 128:
                                    gtile * 2048 + (T + 1) * 128],
                                start=False, stop=(T % 4 == 3),
                                tile_position=(32 * (hh % 4), 64 * par))
                        inc(ins, "TT")
                        w(e, "TT")
                blk.tensor(_)
                tt_y = cnt["TT"]

                def _(e, pair=pair, tt_y=tt_y):
                    w(e, "TT", tt_y)
                    if pair == 0:
                        w(e, "SD", sd_xpfm[-1])
                    ins = e.scalar_tensor_tensor(
                        fmb[:, pair * TOK:(pair + 1) * TOK],
                        fmb[:, pair * TOK:(pair + 1) * TOK],
                        dcolb[:, pair:pair + 1], pbig[:],
                        ALU.mult, ALU.add)
                    inc(ins, "VV")
                blk.vector(_)
        vv_y = cnt["VV"]

        if _DEBUG_DUMP:
            def _(e):
                w(e, "VV")
                inc(e.dma_start(out=dbgY.ap(), in_=fmb[:]), "SD", 16)
            blk.sync(_)

        for gi in range(8):
            def _(e, gi=gi):
                if gi == 0:
                    w(e, "VV")
                    w(e, "SD", sd_zfm)
                inc(e.scalar_tensor_tensor(
                    fmb[:, gi * TOK:(gi + 1) * TOK],
                    fmb[:, gi * TOK:(gi + 1) * TOK],
                    nwcb[:, gi:gi + 1],
                    zfm[:, gi * TOK:(gi + 1) * TOK], ALU.mult, ALU.mult),
                    "VV")
            blk.vector(_)
        vv_gate = cnt["VV"]

        def _(e):
            w(e, "TT")
            w(e, "SD", sd_zfm)
            for k in range(8):
                inc(e.dma_start(out=cwb[:, k * D:(k + 1) * D],
                                in_=dwout[k * 128:(k + 1) * 128, :]),
                    "GD", 16)
            inc(e.dma_start(out=gb[:], in_=dgbc), "GD", 16)
            inc(e.dma_start(out=bb[:], in_=dbbc), "GD", 16)
        blk.gpsimd(_)
        gd_wo = cnt["GD"]

        xrot2 = ztm[:, 6 * TOK:7 * TOK].bitcast(F32)
        xrots = [xrotT, xrot2]
        mobs = [postbuf[:, 0:D], scr32[:, 0:D]]
        x1bs = [postbuf[:, D:TOK], scr32[:, D:TOK]]
        tAs = [ztm[:, 0:D], ztm[:, 2 * D:3 * D]]
        vbs = [ztm[:, D:2 * D], ztm[:, 3 * D:4 * D]]
        txs = [packA[:, 0:D], packA[:, D:2 * D]]
        CB = [0, 16]

        cd_t = {}
        tt_mo_t = {}
        s0_vv = {}
        od_t = {}
        for p in range(NT // 2):
            pair = (2 * p, 2 * p + 1)
            for t in pair:
                def _(e, t=t, p=p):
                    if p == 0:
                        w(e, "SD", sd_zfm)
                    else:
                        w(e, "VV")
                    inc(e.dma_start(out=xrots[t % 2][:],
                                    in_=dx[t * 128:(t + 1) * 128, :]),
                        "CD", 16)
                blk.scalar(_)
                cd_t[t] = cnt["CD"]

            for t in pair:
                pmo = pfm if t % 2 == 0 else pfm2

                def _(e, t=t, pmo=pmo):
                    w(e, "GD", gd_wo)
                    w(e, "VV", vv_gate)
                    if t >= 2:
                        w(e, "VV", s0_vv[t - 2])
                    for k in range(8):
                        st = fmb[:, k * TOK + t * 128:k * TOK + (t + 1) * 128]
                        for q in range(2):
                            ins = e.matmul(pmo[:, q * 512:(q + 1) * 512], st,
                                           cwb[:, k * D + q * 512:
                                               k * D + (q + 1) * 512],
                                           start=(k == 0), stop=(k == 7))
                    inc(ins, "TT")
                blk.tensor(_)
                tt_mo_t[t] = cnt["TT"]

            for t in pair:
                pmo = pfm if t % 2 == 0 else pfm2

                def _(e, t=t, pmo=pmo):
                    w(e, "TT", tt_mo_t[t])
                    B = CB[t % 2]
                    inc(e.tensor_scalar(mobs[t % 2][:], pmo[:], 1.0, 0.0,
                                        ALU.mult, ALU.add,
                                        accum_out=colc[:, B:B + 1]), "VV")
                blk.vector(_)
                s0_vv[t] = cnt["VV"]

            def _(e, p=p):
                for t in pair:
                    B = CB[t % 2]
                    e.scalar_tensor_tensor(
                        tAs[t % 2][:], mobs[t % 2][:], 1.0, mobs[t % 2][:],
                        ALU.mult, ALU.mult,
                        accum_out=colc[:, B + 1:B + 2])
                for t in pair:
                    B = CB[t % 2]
                    e.tensor_scalar(colc[:, B + 2:B + 3],
                                    colc[:, B:B + 1], 1.0 / D, None, ALU.mult)
                    e.tensor_scalar(colc[:, B + 3:B + 4],
                                    colc[:, B + 1:B + 2], 1.0 / D, None,
                                    ALU.mult)
                for t in pair:
                    B = CB[t % 2]
                    ins = e.scalar_tensor_tensor(
                        colc[:, B + 4:B + 5], colc[:, B + 2:B + 3],
                        colc[:, B + 2:B + 3], colc[:, B + 3:B + 4],
                        ALU.mult, ALU.subtract)
                inc(ins, "VV")
            blk.vector(_)
            vv_st1 = cnt["VV"]

            def _(e, p=p, vv_st1=vv_st1):
                w(e, "VV", vv_st1)
                for t in pair:
                    B = CB[t % 2]
                    e.activation(colc[:, B + 5:B + 6], colc[:, B + 4:B + 5],
                                 AF.Ln, bias=epsc[:, 0:1], scale=-1.0)
                    ins = e.activation(colc[:, B + 6:B + 7],
                                       colc[:, B + 5:B + 6], AF.Exp,
                                       bias=0.0, scale=-0.5)
                inc(ins, "CC")
            blk.scalar(_)
            cc_r2 = cnt["CC"]

            def _(e, p=p, cc_r2=cc_r2):
                w(e, "CC", cc_r2)
                w(e, "CD", cd_t[pair[1]])
                if p >= 1:
                    w(e, "OD", od_t[pair[0] - 2])
                    w(e, "OD", od_t[pair[1] - 2])
                for t in pair:
                    B = CB[t % 2]
                    e.tensor_scalar(colc[:, B + 7:B + 8],
                                    colc[:, B + 2:B + 3],
                                    colc[:, B + 6:B + 7], -1.0, ALU.mult,
                                    ALU.mult)
                for t in pair:
                    B = CB[t % 2]
                    e.tensor_scalar(txs[t % 2][:], xrots[t % 2][:],
                                    colc[:, B + 7:B + 8], None, ALU.add)
                for t in pair:
                    B = CB[t % 2]
                    e.scalar_tensor_tensor(
                        x1bs[t % 2][:], mobs[t % 2][:],
                        colc[:, B + 6:B + 7], txs[t % 2][:], ALU.mult,
                        ALU.add, accum_out=colc[:, B:B + 1])
                for t in pair:
                    B = CB[t % 2]
                    e.scalar_tensor_tensor(
                        tAs[t % 2][:], x1bs[t % 2][:], 1.0, x1bs[t % 2][:],
                        ALU.mult, ALU.mult,
                        accum_out=colc[:, B + 1:B + 2])
                for t in pair:
                    B = CB[t % 2]
                    e.tensor_scalar(colc[:, B + 2:B + 3],
                                    colc[:, B:B + 1], 1.0 / D, None, ALU.mult)
                    e.tensor_scalar(colc[:, B + 3:B + 4],
                                    colc[:, B + 1:B + 2], 1.0 / D, None,
                                    ALU.mult)
                for t in pair:
                    B = CB[t % 2]
                    ins = e.scalar_tensor_tensor(
                        colc[:, B + 4:B + 5], colc[:, B + 2:B + 3],
                        colc[:, B + 2:B + 3], colc[:, B + 3:B + 4],
                        ALU.mult, ALU.subtract)
                inc(ins, "VV")
            blk.vector(_)
            vv_st2 = cnt["VV"]

            def _(e, p=p, vv_st2=vv_st2):
                w(e, "VV", vv_st2)
                for t in pair:
                    B = CB[t % 2]
                    e.activation(colc[:, B + 5:B + 6], colc[:, B + 4:B + 5],
                                 AF.Ln, bias=epsc[:, 0:1], scale=-1.0)
                    e.activation(colc[:, B + 6:B + 7], colc[:, B + 5:B + 6],
                                 AF.Exp, bias=0.0, scale=-0.5)
                    ins = e.activation(colc[:, B + 8:B + 9],
                                       colc[:, B + 5:B + 6], AF.Exp,
                                       bias=0.0, scale=-1.0)
                inc(ins, "CC")
            blk.scalar(_)
            cc_r3 = cnt["CC"]

            def _(e, p=p, cc_r3=cc_r3):
                w(e, "CC", cc_r3)
                for t in pair:
                    B = CB[t % 2]
                    e.tensor_scalar(colc[:, B + 7:B + 8],
                                    colc[:, B + 2:B + 3],
                                    colc[:, B + 6:B + 7], -1.0, ALU.mult,
                                    ALU.mult)
                for t in pair:
                    B = CB[t % 2]
                    ins = e.tensor_scalar(colc[:, B + 9:B + 10],
                                          colc[:, B + 4:B + 5],
                                          colc[:, B + 8:B + 9], -1.0,
                                          ALU.mult, ALU.mult)
                inc(ins, "VV")
            blk.vector(_)
            vv_vu = cnt["VV"]

            def _(e, p=p, vv_vu=vv_vu):
                w(e, "VV", vv_vu)
                for t in pair:
                    B = CB[t % 2]
                    e.activation(colc[:, B + 10:B + 11],
                                 colc[:, B + 9:B + 10], AF.Ln,
                                 bias=epsc[:, 0:1], scale=1.0)
                    ins = e.activation(colc[:, B + 11:B + 12],
                                       colc[:, B + 10:B + 11], AF.Exp,
                                       bias=0.0, scale=-0.5)
                inc(ins, "CC")
            blk.scalar(_)
            cc_rs = cnt["CC"]

            def _(e, p=p, cc_rs=cc_rs):
                w(e, "CC", cc_rs)
                w(e, "GD", gd_wo)
                for t in pair:
                    B = CB[t % 2]
                    e.tensor_scalar(colc[:, B + 13:B + 14],
                                    colc[:, B + 7:B + 8],
                                    colc[:, B + 11:B + 12], None, ALU.mult)
                for t in pair:
                    B = CB[t % 2]
                    e.tensor_scalar(colc[:, B + 12:B + 13],
                                    colc[:, B + 6:B + 7],
                                    colc[:, B + 11:B + 12], None, ALU.mult)
                for t in pair:
                    B = CB[t % 2]
                    e.scalar_tensor_tensor(
                        tAs[t % 2][:], x1bs[t % 2][:],
                        colc[:, B + 12:B + 13],
                        colc[:, B + 13:B + 14].broadcast_to([128, D]),
                        ALU.mult, ALU.add)
                for t in pair:
                    e.tensor_tensor(vbs[t % 2][:], tAs[t % 2][:], gb[:],
                                    ALU.mult)
                for t in pair:
                    B = CB[t % 2]
                    e.scalar_tensor_tensor(
                        vbs[t % 2][:], vbs[t % 2][:], 1.0, bb[:], ALU.mult,
                        ALU.add, accum_out=colc[:, B:B + 1])
                for t in pair:
                    B = CB[t % 2]
                    e.scalar_tensor_tensor(
                        tAs[t % 2][:], vbs[t % 2][:], 1.0, vbs[t % 2][:],
                        ALU.mult, ALU.mult,
                        accum_out=colc[:, B + 1:B + 2])
                for t in pair:
                    B = CB[t % 2]
                    e.tensor_scalar(colc[:, B + 2:B + 3],
                                    colc[:, B:B + 1], 1.0 / D, None, ALU.mult)
                    e.tensor_scalar(colc[:, B + 3:B + 4],
                                    colc[:, B + 1:B + 2], 1.0 / D, None,
                                    ALU.mult)
                for t in pair:
                    B = CB[t % 2]
                    ins = e.scalar_tensor_tensor(
                        colc[:, B + 4:B + 5], colc[:, B + 2:B + 3],
                        colc[:, B + 2:B + 3], colc[:, B + 3:B + 4],
                        ALU.mult, ALU.subtract)
                inc(ins, "VV")
            blk.vector(_)
            vv_st3 = cnt["VV"]

            def _(e, p=p, vv_st3=vv_st3):
                w(e, "VV", vv_st3)
                for t in pair:
                    B = CB[t % 2]
                    e.activation(colc[:, B + 5:B + 6], colc[:, B + 4:B + 5],
                                 AF.Ln, bias=epsc[:, 0:1], scale=-1.0)
                    ins = e.activation(colc[:, B + 6:B + 7],
                                       colc[:, B + 5:B + 6], AF.Exp,
                                       bias=0.0, scale=-0.5)
                inc(ins, "CC")
            blk.scalar(_)
            cc_r4 = cnt["CC"]

            def _(e, p=p, cc_r4=cc_r4):
                w(e, "CC", cc_r4)
                for t in pair:
                    B = CB[t % 2]
                    e.tensor_scalar(colc[:, B + 7:B + 8],
                                    colc[:, B + 2:B + 3],
                                    colc[:, B + 6:B + 7], -1.0, ALU.mult,
                                    ALU.mult)
                for t in pair:
                    B = CB[t % 2]
                    e.scalar_tensor_tensor(
                        txs[t % 2][:], vbs[t % 2][:], colc[:, B + 6:B + 7],
                        colc[:, B + 7:B + 8].broadcast_to([128, D]),
                        ALU.mult, ALU.add)
                for t in pair:
                    inc(e.tensor_tensor(x1bs[t % 2][:], x1bs[t % 2][:],
                                        txs[t % 2][:], ALU.add), "VV")
            blk.vector(_)
            vv_o1 = cnt["VV"] - 1
            vv_o2 = cnt["VV"]

            for t, vvn in zip(pair, (vv_o1, vv_o2)):
                def _(e, t=t, vvn=vvn):
                    sdk(e, "hbm")
                    w(e, "VV", vvn)
                    inc(e.dma_start(out=dout[t * 128:(t + 1) * 128, :],
                                    in_=x1bs[t % 2][:]), "OD", 16)
                blk.sync(_)
                od_t[t] = cnt["OD"]

        def _(e):
            w(e, "OD")
        blk.sync(_)

    return nc


_CACHE = {}


def _host_prep(inputs):
    x = np.asarray(inputs["L0_attn_x"], dtype=np.float32).reshape(16384, D)
    w_inT = np.ascontiguousarray(
        np.asarray(inputs["L0_attn_W_in"], dtype=np.float32).T)
    w_outT = np.ascontiguousarray(
        np.asarray(inputs["L0_attn_W_out"], dtype=np.float32).T)
    dt_WT = np.ascontiguousarray(
        np.asarray(inputs["L0_attn_dt_W"], dtype=np.float32).T)
    dt_b = np.asarray(inputs["L0_attn_dt_b"],
                      dtype=np.float32).reshape(16, 1)
    A_log = np.asarray(inputs["L0_attn_A_log"],
                       dtype=np.float32).reshape(16, 1)
    Dp = np.asarray(inputs["L0_attn_D_param"], dtype=np.float32)
    nw = np.asarray(inputs["L0_attn_norm_w"], dtype=np.float32)
    ln_g = np.asarray(inputs["L0_ffn_ln_g"], dtype=np.float32)
    ln_b = np.asarray(inputs["L0_ffn_ln_b"], dtype=np.float32)

    Dcols = np.empty((128, 8), np.float32)
    for g in range(8):
        Dcols[0:64, g] = Dp[2 * g]
        Dcols[64:128, g] = Dp[2 * g + 1]
    nwcols = nw.reshape(8, 128).T.copy()
    gbc = np.broadcast_to(ln_g, (128, D)).copy()
    bbc = np.broadcast_to(ln_b, (128, D)).copy()

    resetm = np.ones((16, TOK), np.float32)
    resetm[:, ::CH] = 0.0
    jj, ii = np.meshgrid(np.arange(128), np.arange(128), indexing="ij")
    m0 = np.where(ii >= jj, 0.0, NEG).astype(np.float32)
    tileind = np.zeros((16, TOK), np.float32)
    for t in range(16):
        tileind[t, t * 128:(t + 1) * 128] = 1.0
    grp4rep = np.zeros((128, 128), np.float32)
    for g in range(4):
        for m in range(4):
            grp4rep[32 * g + m, 32 * m:32 * (m + 1)] = 1.0

    common = dict(w_inT=w_inT, w_outT=w_outT, dt_WT=dt_WT, dt_b=dt_b,
                  A_log=A_log, Dcols=Dcols, nwcols=nwcols, gbc=gbc, bbc=bbc,
                  resetm=resetm, m0=m0, tileind=tileind, grp4rep=grp4rep)
    in_maps = []
    for c in range(NCORE):
        selA = np.zeros((128, 8), np.float32)
        selA[:, :c] = 1.0
        seld = np.zeros((128, 8), np.float32)
        seld[:, :c] = 1.0
        invseld = 1.0 - seld
        m = dict(common)
        m["x"] = np.ascontiguousarray(x[c * TOK:(c + 1) * TOK])
        m["selA"] = selA
        m["seld"] = seld
        m["invseld"] = invseld
        in_maps.append(m)
    return in_maps


def kernel(**inputs):
    if "nc" not in _CACHE:
        _CACHE["nc"] = build_nc()
    nc = _CACHE["nc"]
    in_maps = _host_prep(inputs)
    res = run_bass_kernel_spmd(nc, in_maps, core_ids=list(range(NCORE)))
    out = np.concatenate([res.results[c]["out"] for c in range(NCORE)],
                         axis=0)
    return out.reshape(1, 16384, D).astype(np.float32)


if __name__ == "__main__":
    build_nc()
    print("build ok")


# version 14
# speedup vs baseline: 1.3852x; 1.0493x over previous
from contextlib import ExitStack

import numpy as np

import concourse.bass as bass
import concourse.mybir as mybir
from concourse.bass_utils import run_bass_kernel_spmd

F32 = mybir.dt.float32
BF16 = mybir.dt.bfloat16
AF = mybir.ActivationFunctionType
ALU = mybir.AluOpType

NCORE = 8
TOK = 2048
D = 1024
H = 16
HD = 64
CH = 128
NT = TOK // CH
PROJ = 2576
NEG = -1.0e30
_DETECT_RACES = False
_DEBUG_DUMP = False


def build_nc():
    nc = bass.Bass("TRN2", target_bir_lowering=False,
                   detect_race_conditions=_DETECT_RACES)

    dx = nc.dram_tensor("x", [TOK, D], F32, kind="ExternalInput").ap()
    dwin = nc.dram_tensor("w_inT", [D, PROJ], F32, kind="ExternalInput").ap()
    dwout = nc.dram_tensor("w_outT", [D, D], F32, kind="ExternalInput").ap()
    ddtw = nc.dram_tensor("dt_WT", [16, 16], F32, kind="ExternalInput").ap()
    ddtb = nc.dram_tensor("dt_b", [16, 1], F32, kind="ExternalInput").ap()
    dalog = nc.dram_tensor("A_log", [16, 1], F32, kind="ExternalInput").ap()
    ddcol = nc.dram_tensor("Dcols", [128, 8], F32, kind="ExternalInput").ap()
    dnwc = nc.dram_tensor("nwcols", [128, 8], F32, kind="ExternalInput").ap()
    dgbc = nc.dram_tensor("gbc", [128, D], F32, kind="ExternalInput").ap()
    dbbc = nc.dram_tensor("bbc", [128, D], F32, kind="ExternalInput").ap()
    drst = nc.dram_tensor("resetm", [16, TOK], F32, kind="ExternalInput").ap()
    dm0 = nc.dram_tensor("m0", [128, 128], F32, kind="ExternalInput").ap()
    dtind = nc.dram_tensor("tileind", [16, TOK], F32, kind="ExternalInput").ap()
    dgrp = nc.dram_tensor("grp4rep", [128, 128], F32, kind="ExternalInput").ap()
    dident = nc.dram_tensor("ident", [128, 128], F32, kind="ExternalInput").ap()
    dselA = nc.dram_tensor("selA", [128, 8], F32, kind="ExternalInput").ap()
    dseld = nc.dram_tensor("seld", [128, 8], F32, kind="ExternalInput").ap()
    dinvsd = nc.dram_tensor("invseld", [128, 8], F32,
                            kind="ExternalInput").ap()
    dout = nc.dram_tensor("out", [TOK, D], F32, kind="ExternalOutput").ap()

    ccA_in = nc.dram_tensor("ccA_in", [128, 4 * HD], F32)
    ccA_out = nc.dram_tensor("ccA_out", [NCORE * 128, 4 * HD], F32)
    ccd_in = nc.dram_tensor("ccd_in", [16, 8], F32)
    if _DEBUG_DUMP:
        dbgY = nc.dram_tensor("dbgY", [128, 8 * TOK], BF16)
        dbgS = nc.dram_tensor("dbgS", [128, TOK], BF16)
        dbgE = nc.dram_tensor("dbgE", [128, TOK], BF16)
        dbgR = nc.dram_tensor("dbgR", [128, TOK], F32)
    ccd_out = nc.dram_tensor("ccd_out", [NCORE * 16, 8], F32)

    es = ExitStack()
    sb = lambda n, s, d: es.enter_context(nc.sbuf_tensor(n, s, d))
    psa = lambda n, s, d: es.enter_context(nc.psum_tensor(n, s, d))

    w_in = sb("w_in_sb", [128, 8 * PROJ], BF16)
    fmb = sb("fmbuf", [128, 8 * TOK], BF16)
    xptm = sb("xp_tm", [128, NT * D], BF16)
    ztm = sb("siluz_tm", [128, NT * D], BF16)
    zfm = w_in
    bpadB = w_in[:, 8 * TOK:8 * TOK + 2 * TOK]
    cwb = sb("cwbuf", [128, 8 * D], BF16)
    bpad = sb("bpad2", [128, 2 * TOK], BF16)
    cfm = sb("cfm", [128, 2 * TOK], BF16)
    btm = sb("btm", [128, 2 * 256], BF16)
    packA = sb("packA", [128, TOK], F32)
    scr32 = sb("scr32", [128, TOK], F32)
    snaps = sb("snaps", [128, 16 * 4 * HD], BF16)
    srun = sb("srun", [128, 4 * HD], F32)
    wxb = cwb[:, 5 * D:6 * D]
    scobE = ztm[:, 0:TOK]
    scobO = ztm[:, TOK:2 * TOK]
    onerow = ztm[0:1, TOK:TOK + 128]
    expb = ztm[:, 2 * TOK:3 * TOK]
    eag = ztm[:, 3 * TOK:4 * TOK]
    ub = ztm[:, 4 * TOK:4 * TOK + D]
    vb = ztm[:, 4 * TOK + D:5 * TOK]
    gb = ztm[:, 5 * TOK:5 * TOK + D]
    bb = ztm[:, 5 * TOK + D:6 * TOK]
    fbfc = ztm[:, 6 * TOK:7 * TOK]
    postbuf = sb("postbuf", [128, TOK], F32)
    mob = postbuf[:, 0:D]
    x1b = postbuf[:, D:TOK]
    rhsA = postbuf[0:32, :]
    fbf = fbfc[0:16, :]
    lhsA = sb("lhsA", [32, 128], F32)
    m0b = sb("m0_sb", [128, 128], BF16)
    identb = sb("ident_sb", [128, 128], BF16)
    grpb = sb("grp_sb", [128, 128], F32)
    dtwb = sb("dtw_sb", [16, 16], F32)
    dtbb = sb("dtb_sb", [16, 1], F32)
    alogb = sb("alog_sb", [16, 1], F32)
    acolb = sb("acol_sb", [16, 1], F32)
    dcolb = sb("dcol_sb", [128, 8], F32)
    nwcb = sb("nwc_sb", [128, 8], F32)
    selAb = sb("selA_sb", [128, 8], F32)
    seldb = sb("seld_sb", [128, 8], F32)
    invsdb = sb("invsd_sb", [128, 8], F32)
    rcols = sb("rcols", [128, NT], F32)
    mrcols = sb("mrcols", [128, NT], F32)
    ftm = sb("ftm", [128, NT * 16], BF16)
    e16 = sb("e16", [128, 64], F32)
    sp4 = sb("sp4", [4, 64], F32)
    dcolsb = sb("dcolsb", [128, 64], F32)
    pcolsb = sb("pcolsb", [128, 64], F32)
    dcoreb = sb("dcoreb", [16, 8], F32)
    ddall = sb("ddall", [128, 1], F32)
    drcols = sb("drcols", [128, 32], F32)
    dvb = sb("dvb", [128, 32], F32)
    avb = sb("avb", [128, 4 * HD], F32)
    hbuf = sb("hbuf", [128, 4 * HD], BF16)
    colc = sb("colc", [128, 32], F32)
    onecol = sb("onecol", [128, 1], BF16)
    epsc = sb("epsc", [128, 3], F32)


    xnrot = cwb[:, 4 * D:5 * D]
    xrotL = cwb[:, 6 * D:8 * D].bitcast(F32)
    xrotT = ztm[:, 7 * TOK:8 * TOK].bitcast(F32)

    pfm = psa("pfm", [128, 1024], F32)
    pfm2 = psa("pfm2", [128, 1024], F32)
    pbig = psa("pbig", [128, 2048], F32)

    names = ["SD", "GD", "CD", "TT", "VV", "CC", "OD", "CS", "CT"]
    sems = {n: es.enter_context(nc.semaphore(n)) for n in names}
    cnt = {n: 0 for n in names}

    with es, nc.Block() as blk:
        def inc(ins, name, k=1):
            ins.then_inc(sems[name], k)
            cnt[name] += k

        def w(e, name, val=None):
            e.wait_ge(sems[name], cnt[name] if val is None else val)

        def barrier(e):
            for n in names:
                if cnt[n]:
                    e.wait_ge(sems[n], cnt[n])

        sd_kinds = set()

        def sdk(e, kind):
            if kind in ("tr", "s2s") and (sd_kinds - {kind, "hbm"}):
                e.wait_ge(sems["SD"], cnt["SD"])
                sd_kinds.clear()
            sd_kinds.add(kind)

        def _(e):
            for k in range(8):
                inc(e.dma_start(out=w_in[:, k * PROJ:(k + 1) * PROJ],
                                in_=dwin[k * 128:(k + 1) * 128, :]), "GD", 16)
            inc(e.dma_start(out=m0b[:], in_=dm0), "GD", 16)
            inc(e.dma_start(out=identb[:], in_=dident), "GD", 16)
        blk.gpsimd(_)
        gd_w = cnt["GD"]

        def _(e):
            sdk(e, "hbm")
            for t_, d_ in ((grpb, dgrp), (dtwb, ddtw),
                           (dtbb, ddtb), (alogb, dalog), (dcolb, ddcol),
                           (nwcb, dnwc), (selAb, dselA), (seldb, dseld),
                           (invsdb, dinvsd)):
                inc(e.dma_start(out=t_[:], in_=d_), "SD", 16)
            inc(e.dma_start(out=rhsA[1:17, :], in_=dtind), "SD", 16)
        blk.sync(_)
        sd_w = cnt["SD"]

        def _(e):
            e.memset(onecol[:], 1.0)
            e.memset(epsc[:, 0:1], 1e-5)
            e.memset(dcoreb[:], 0.0)
            e.memset(e16[32:48, 16:32], 1.0)
            e.memset(epsc[:, 1:2], 1.0)
            e.memset(epsc[:, 2:3], 1e-6)
            e.memset(lhsA[:], 0.0)
            e.memset(lhsA[0:1, :], 1.0)
            e.memset(srun[:], 0.0)
            e.memset(hbuf[:], 0.0)
            inc(e.memset(pbig[:], 0.0), "VV")
        blk.vector(_)
        vv_init = cnt["VV"]

        xrotL2 = cwb[:, 0:2 * D].bitcast(F32)
        xnrot2 = cwb[:, 2 * D:3 * D]
        xrotsL = [xrotL, xrotL2]
        xnrots = [xnrot, xnrot2]
        CBL = [0, 16]
        xn_sd = []
        vv_xn_t = {}
        sd_tr_t = {}
        for t in range(NT):
            par = t % 2

            def _(e, t=t, par=par):
                sdk(e, "hbm")
                if t >= 2:
                    w(e, "VV", vv_xn_t[t - 2])
                inc(e.dma_start(out=xrotsL[par][:],
                                in_=dx[t * 128:(t + 1) * 128, :]), "SD", 16)
            blk.sync(_)
            sd_x = cnt["SD"]

            def _(e, t=t, par=par, sd_x=sd_x):
                w(e, "SD", sd_x)
                B = CBL[par]
                e.tensor_scalar(scr32[:, 0:D], xrotsL[par][:], 1.0, 0.0,
                                ALU.mult, ALU.add,
                                accum_out=colc[:, B:B + 1])
                e.scalar_tensor_tensor(scr32[:, 0:D], xrotsL[par][:], 1.0,
                                       xrotsL[par][:], ALU.mult, ALU.mult,
                                       accum_out=colc[:, B + 1:B + 2])
                e.tensor_scalar(colc[:, B + 2:B + 3], colc[:, B:B + 1],
                                1.0 / D, None, ALU.mult)
                e.tensor_scalar(colc[:, B + 3:B + 4], colc[:, B + 1:B + 2],
                                1.0 / D, None, ALU.mult)
                inc(e.scalar_tensor_tensor(colc[:, B + 4:B + 5],
                                           colc[:, B + 2:B + 3],
                                           colc[:, B + 2:B + 3],
                                           colc[:, B + 3:B + 4],
                                           ALU.mult, ALU.subtract), "VV")
            blk.vector(_)
            vv_st = cnt["VV"]

            def _(e, t=t, par=par, vv_st=vv_st):
                w(e, "VV", vv_st)
                B = CBL[par]
                e.activation(colc[:, B + 5:B + 6], colc[:, B + 4:B + 5],
                             AF.Ln, bias=epsc[:, 0:1], scale=-1.0)
                inc(e.activation(colc[:, B + 6:B + 7], colc[:, B + 5:B + 6],
                                 AF.Exp, bias=0.0, scale=-0.5), "CC")
            blk.scalar(_)
            cc_r = cnt["CC"]

            def _(e, t=t, par=par, cc_r=cc_r):
                w(e, "CC", cc_r)
                B = CBL[par]
                inc(e.tensor_scalar(colc[:, B + 7:B + 8],
                                    colc[:, B + 2:B + 3],
                                    colc[:, B + 6:B + 7], None, ALU.mult),
                    "VV")
            blk.vector(_)
            vv_mr = cnt["VV"]

            def _(e, t=t, par=par, vv_mr=vv_mr):
                w(e, "VV", vv_mr)
                if t >= 2:
                    w(e, "SD", sd_tr_t[t - 2])
                B = CBL[par]
                inc(e.scalar_tensor_tensor(
                    xnrots[par][:], xrotsL[par][:], colc[:, B + 6:B + 7],
                    colc[:, B + 7:B + 8].broadcast_to([128, D]),
                    ALU.mult, ALU.subtract), "VV")
            blk.vector(_)
            vv_xn_t[t] = cnt["VV"]

            def _(e, t=t, par=par):
                sdk(e, "tr")
                w(e, "VV", vv_xn_t[t])
                for k in range(8):
                    inc(e.dma_start(
                        out=fmb[:, k * TOK + t * 128:k * TOK + (t + 1) * 128],
                        in_=xnrots[par][:, k * 128:(k + 1) * 128],
                        transpose=True), "SD", 16)
            blk.sync(_)
            sd_tr_t[t] = cnt["SD"]
            xn_sd.append(cnt["SD"])

        fm_tt = []
        for mt in range(5):
            for half in range(2):
                pp = pfm if (mt * 2 + half) % 2 == 0 else pfm2
                M = 16 if mt == 4 else 128

                def _(e, mt=mt, half=half, pp=pp, M=M):
                    if mt == 0 and half == 0:
                        w(e, "SD", xn_sd[-1])
                        w(e, "GD", gd_w)
                    if mt * 2 + half >= 2:
                        w(e, "VV")
                        w(e, "CC")
                    for k in range(8):
                        lhs = w_in[:, k * PROJ + 2048 + mt * 128:
                                   k * PROJ + 2048 + mt * 128 + M]
                        for q in range(2):
                            ins = e.matmul(
                                pp[0:M, q * 512:(q + 1) * 512], lhs,
                                fmb[:, k * TOK + half * 1024 + q * 512:
                                    k * TOK + half * 1024 + (q + 1) * 512],
                                start=(k == 0), stop=(k == 7))
                    inc(ins, "TT")
                blk.tensor(_)
                fm_tt.append(cnt["TT"])

                if mt < 2:
                    def _(e, mt=mt, half=half, pp=pp):
                        w(e, "TT", fm_tt[-1])
                        inc(e.tensor_copy(
                            cwb[:, mt * 2048 + half * 1024:
                                mt * 2048 + (half + 1) * 1024], pp[:]), "VV")
                    blk.vector(_)
                elif mt < 4:
                    def _(e, mt=mt, half=half, pp=pp):
                        w(e, "TT", fm_tt[-1])
                        inc(e.copy(
                            cfm[:, (mt - 2) * 2048 + half * 1024:
                                (mt - 2) * 2048 + (half + 1) * 1024],
                            pp[:]), "CC")
                    blk.scalar(_)
                else:
                    def _(e, half=half, pp=pp):
                        w(e, "TT", fm_tt[-1])
                        inc(e.tensor_copy(
                            packA[0:16, half * 1024:(half + 1) * 1024],
                            pp[0:16, :]), "VV")
                    blk.vector(_)
        fm_vv = cnt["VV"]
        fm_cc = cnt["CC"]

        def _(e):
            inc(e.memset(bpad[:], 0.0), "VV")
        blk.vector(_)
        vv_bm = cnt["VV"]

        def _(e):
            sdk(e, "s2s")
            w(e, "VV", vv_bm)
            w(e, "VV", fm_vv)
            for h in range(0, H, 2):
                gp = h // 2
                dst_t = gp // 4
                base = 32 * (gp % 4)
                inc(e.dma_start(
                    out=bpad[base:base + 16, dst_t * TOK:(dst_t + 1) * TOK],
                    in_=cwb[16 * (h % 8):16 * (h % 8) + 16,
                            (h // 8) * 2048:(h // 8 + 1) * 2048]), "SD", 16)
        blk.sync(_)
        sd_f2 = cnt["SD"]


        def _(e):
            w(e, "VV", fm_vv)
            w(e, "SD", sd_w)
            w(e, "VV")
            for q in range(4):
                ins = e.matmul(pbig[0:16, q * 512:(q + 1) * 512], dtwb[:],
                               packA[0:16, q * 512:(q + 1) * 512],
                               start=True, stop=True)
            inc(ins, "TT")
        blk.tensor(_)
        tt_dt = cnt["TT"]

        def _(e):
            w(e, "TT", tt_dt)
            e.activation(scr32[32:48, :], pbig[0:16, :], AF.Abs,
                         bias=dtbb[:], scale=1.0)
            e.activation(scr32[64:80, :], scr32[32:48, :], AF.Exp,
                         bias=0.0, scale=-1.0)
            e.activation(scr32[96:112, :], scr32[64:80, :], AF.Ln,
                         bias=epsc[0:16, 1:2], scale=1.0)
            e.activation(packA[96:112, :], pbig[0:16, :], AF.Relu,
                         bias=dtbb[:], scale=1.0)
            inc(e.activation(acolb[:], alogb[:], AF.Exp, bias=0.0,
                             scale=1.0), "CC")
        blk.scalar(_)
        cc_d1 = cnt["CC"]

        def _(e):
            w(e, "CC", cc_d1)
            e.tensor_add(packA[0:16, :], scr32[96:112, :], packA[96:112, :])
            e.tensor_scalar(acolb[:], acolb[:], -1.0, None, ALU.mult)
            inc(e.tensor_scalar(scr32[32:48, :], packA[0:16, :], acolb[:],
                                None, ALU.mult), "VV")
        blk.vector(_)
        vv_da = cnt["VV"]

        def _(e):
            sdk(e, "hbm")
            w(e, "VV", vv_da)
            inc(e.dma_start(out=packA[96:112, :], in_=drst), "SD", 16)
        blk.sync(_)
        sd_rst = cnt["SD"]

        def _(e):
            w(e, "VV", vv_da)
            inc(e.activation(scr32[64:80, :], packA[0:16, :], AF.Ln,
                             bias=0.0, scale=1.0), "CC")
        blk.scalar(_)
        cc_lndt = cnt["CC"]

        def _(e):
            w(e, "CC", cc_lndt)
            w(e, "SD", sd_rst)
            e.tensor_copy(scr32[96:112, :], scr32[32:48, :])
            inc(e.tensor_tensor_scan(packA[32:48, :], packA[96:112, :],
                                     scr32[96:112, :], 0.0, ALU.mult,
                                     ALU.add), "VV")
        blk.vector(_)
        vv_a = cnt["VV"]

        def _(e):
            w(e, "VV", vv_a)
            e.tensor_copy(scr32[32:48, :], scr32[64:80, :])
            e.tensor_sub(packA[64:80, :], scr32[32:48, :], packA[32:48, :])
            av3 = packA[32:48, :].rearrange("h (c i) -> h c i", c=NT)
            e.tensor_copy(e16[32:48, 0:16].unsqueeze(2), av3[:, :, 127:128])
            bcv = e16[32:48, 0:16].unsqueeze(2).broadcast_to([16, NT, 128])
            e.tensor_tensor(
                scr32[96:112, :].rearrange("h (c i) -> h c i", c=NT), bcv,
                av3, ALU.subtract)
            inc(e.tensor_tensor_scan(e16[32:48, 32:48], e16[32:48, 16:32],
                                     e16[32:48, 0:16], 0.0, ALU.mult,
                                     ALU.add), "VV")
        blk.vector(_)
        vv_pib = cnt["VV"]

        def _(e):
            w(e, "VV", vv_pib)
            inc(e.tensor_sub(e16[32:48, 48:64], e16[32:48, 32:48],
                             e16[32:48, 0:16]), "VV")
        blk.vector(_)
        vv_d2 = cnt["VV"]

        def _(e):
            w(e, "VV", vv_d2)
            e.activation(fbf[:], scr32[96:112, :], AF.Exp, bias=0.0,
                         scale=1.0)
            inc(e.activation(dcoreb[:, 0:1], e16[32:48, 47:48], AF.Exp,
                             bias=0.0, scale=1.0), "CC")
        blk.scalar(_)
        cc_d3 = cnt["CC"]

        def _(e):
            w(e, "CC", cc_d3)
            inc(e.tensor_tensor(fbf[:], packA[0:16, :], fbf[:], ALU.mult),
                "VV")
        blk.vector(_)
        vv_f = cnt["VV"]

        def _(e):
            sdk(e, "tr")
            w(e, "VV", vv_f)
            for t in range(NT):
                inc(e.dma_start(out=ftm[:, t * 16:(t + 1) * 16],
                                in_=fbf[:, t * 128:(t + 1) * 128],
                                transpose=True), "SD", 16)
        blk.sync(_)
        sd_ftm = cnt["SD"]

        def _(e):
            sdk(e, "s2s")
            w(e, "VV", vv_d2)
            for gi in range(4):
                inc(e.dma_start(out=sp4[0:4, 16 * gi:16 * (gi + 1)],
                                in_=e16[32 + 4 * gi:32 + 4 * gi + 4, 0:16]),
                    "SD", 16)
        blk.sync(_)
        sd_ae = cnt["SD"]

        def _(e):
            w(e, "SD", sd_ae)
            w(e, "SD", sd_w)
            for gi in range(4):
                ins = e.matmul(pfm2[:, gi * 16:(gi + 1) * 16],
                               grpb[0:4, :],
                               sp4[0:4, 16 * gi:16 * (gi + 1)],
                               start=True, stop=True)
            inc(ins, "TT")
        blk.tensor(_)
        tt_dc = cnt["TT"]

        def _(e):
            w(e, "TT", tt_dc)
            inc(e.activation(dcolsb[:], pfm2[:, 0:64], AF.Exp, bias=0.0,
                             scale=1.0), "CC")
        blk.scalar(_)
        cc_dcol = cnt["CC"]

        def _(e):
            sdk(e, "s2s")
            w(e, "CC", cc_dcol)
            w(e, "VV", vv_d2)
            for gi in range(4):
                inc(e.dma_start(out=sp4[0:4, 16 * gi:16 * (gi + 1)],
                                in_=e16[32 + 4 * gi:32 + 4 * gi + 4, 48:64]),
                    "SD", 16)
        blk.sync(_)
        sd_pe = cnt["SD"]

        def _(e):
            w(e, "SD", sd_pe)
            for gi in range(4):
                ins = e.matmul(pfm2[:, gi * 16:(gi + 1) * 16],
                               grpb[0:4, :],
                               sp4[0:4, 16 * gi:16 * (gi + 1)],
                               start=True, stop=True)
            inc(ins, "TT")
        blk.tensor(_)
        tt_pc = cnt["TT"]

        def _(e):
            w(e, "TT", tt_pc)
            inc(e.activation(pcolsb[:], pfm2[:, 0:64], AF.Exp, bias=0.0,
                             scale=1.0), "CC")
        blk.scalar(_)
        cc_pcol = cnt["CC"]

        tt_projtm = []
        sd_xpfm = []
        for t in range(NT):
            def _(e, t=t):
                w(e, "SD", xn_sd[t])
                if t == 0:
                    w(e, "GD", gd_w)
                    w(e, "CC", cc_pcol)
                    w(e, "TT", fm_tt[-1])
                    w(e, "SD", sd_ftm)
                else:
                    w(e, "CC")
                for k in range(8):
                    st = fmb[:, k * TOK + t * 128:k * TOK + (t + 1) * 128]
                    for q in range(2):
                        e.matmul(pfm[:, q * 512:(q + 1) * 512], st,
                                 w_in[:, k * PROJ + q * 512:
                                      k * PROJ + (q + 1) * 512],
                                 start=(k == 0), stop=(k == 7))
                    for q in range(2):
                        ins = e.matmul(
                            pfm2[:, q * 512:(q + 1) * 512], st,
                            w_in[:, k * PROJ + 1024 + q * 512:
                                 k * PROJ + 1024 + (q + 1) * 512],
                            start=(k == 0), stop=(k == 7))
                inc(ins, "TT")
            blk.tensor(_)
            tt_projtm.append(cnt["TT"])
            tt_p = cnt["TT"]

            def _(e, t=t, tt_p=tt_p):
                w(e, "TT", tt_p)
                e.copy(xptm[:, t * D:(t + 1) * D], pfm[:])
                inc(e.activation(ztm[:, t * D:(t + 1) * D], pfm2[:], AF.Silu,
                                 bias=0.0, scale=1.0), "CC")
            blk.scalar(_)
            cc_ev = cnt["CC"]

            def _(e, t=t, cc_ev=cc_ev):
                sdk(e, "tr")
                w(e, "CC", cc_ev)
                if t >= 2:
                    w(e, "TT")
                for r in range(2):
                    inc(e.dma_start(
                        out=btm[:, (t % 2) * 256 + r * 128:
                                (t % 2) * 256 + (r + 1) * 128],
                        in_=cwb[:, r * 2048 + t * 128:
                                r * 2048 + (t + 1) * 128],
                        transpose=True), "SD", 16)
            blk.sync(_)
            sd_btm = cnt["SD"]

            def _(e, t=t):
                sdk(e, "tr")
                for k in range(8):
                    inc(e.dma_start(
                        out=fmb[:, k * TOK + t * 128:k * TOK + (t + 1) * 128],
                        in_=xptm[:, t * D + k * 128:t * D + (k + 1) * 128],
                        transpose=True), "SD", 16)
            blk.sync(_)
            sd_xpfm.append(cnt["SD"])

            def _(e, t=t, cc_ev=cc_ev):
                w(e, "CC", cc_ev)
                w(e, "SD", sd_ftm)
                fv = ftm[:, t * 16:(t + 1) * 16].unsqueeze(2).broadcast_to(
                    [128, 16, HD])
                xv = xptm[:, t * D:(t + 1) * D].rearrange(
                    "p (h d) -> p h d", h=16)
                inc(e.tensor_tensor(wxb[:].rearrange("p (h d) -> p h d",
                                                     h=16),
                                    xv, fv, ALU.mult), "VV")
            blk.vector(_)
            vv_wx = cnt["VV"]

            def _(e, t=t, vv_wx=vv_wx, sd_btm=sd_btm):
                w(e, "VV", vv_wx)
                w(e, "SD", sd_btm)
                if t == 0:
                    w(e, "SD", sd_f2)
                    w(e, "VV", vv_init)
                else:
                    w(e, "VV")
                for h in range(H):
                    gi, m = h // 4, h % 4
                    ins = e.matmul(
                        pbig[32 * m:32 * m + 16, gi * 512:gi * 512 + 64],
                        btm[:, (t % 2) * 256 + (h // 8) * 128 + 16 * (h % 8):
                            (t % 2) * 256 + (h // 8) * 128 + 16 * (h % 8)
                            + 16],
                        wxb[:, h * HD:(h + 1) * HD],
                        start=True, stop=True,
                        tile_position=(0, 32 * m))
                inc(ins, "TT")
            blk.tensor(_)
            tt_cs = cnt["TT"]

            def _(e, t=t, tt_cs=tt_cs):
                w(e, "TT", tt_cs)
                if t == 0:
                    w(e, "CC", cc_dcol)
                for gi in range(4):
                    e.tensor_copy(
                        snaps[:, (t * 4 + gi) * HD:(t * 4 + gi + 1) * HD],
                        srun[:, gi * HD:(gi + 1) * HD])
                    ins = e.scalar_tensor_tensor(
                        srun[:, gi * HD:(gi + 1) * HD],
                        srun[:, gi * HD:(gi + 1) * HD],
                        dcolsb[:, gi * 16 + t:gi * 16 + t + 1],
                        pbig[:, gi * 512:gi * 512 + 64],
                        ALU.mult, ALU.add)
                inc(ins, "VV")
            blk.vector(_)

        cc_evE = cnt["CC"]
        zfm3 = zfm[:, 0:8 * TOK].rearrange("p (k c) -> p k c", k=8)
        for t in range(NT):
            pz = pfm if t % 2 == 0 else pbig

            def _(e, t=t, pz=pz):
                if t == 0:
                    w(e, "CC", cc_evE)
                    w(e, "GD", gd_w)
                elif t >= 2:
                    w(e, "CC")
                for k in range(8):
                    ins = e.matmul(pz[:, k * 128:(k + 1) * 128],
                                   ztm[:, t * D + k * 128:t * D + (k + 1) * 128],
                                   identb[:], start=True, stop=True)
                inc(ins, "TT")
            blk.tensor(_)
            tt_zt = cnt["TT"]

            def _(e, t=t, pz=pz, tt_zt=tt_zt):
                w(e, "TT", tt_zt)
                inc(e.copy(zfm3[:, :, t * 128:(t + 1) * 128],
                           pz[:, 0:1024].rearrange("p (k c) -> p k c", k=8)),
                    "CC")
            blk.scalar(_)
        cc_zfm = cnt["CC"]

        def _(e):
            w(e, "VV")
            w(e, "CC", cc_d3)
            inc(e.dma_start(out=ccA_in.ap(), in_=srun[:]), "GD", 16)
            inc(e.dma_start(out=ccd_in.ap(), in_=dcoreb[:]), "GD", 16)
            w(e, "GD")
            inc(e.collective_compute(
                "AllGather", ALU.bypass,
                replica_groups=[list(range(NCORE))],
                ins=[ccA_in.ap().opt()], outs=[ccA_out.ap().opt()]), "CS")
            inc(e.collective_compute(
                "AllGather", ALU.bypass,
                replica_groups=[list(range(NCORE))],
                ins=[ccd_in.ap().opt()], outs=[ccd_out.ap().opt()]), "CT")
            e.wait_ge(sems["CS"], 1)
            e.wait_ge(sems["CT"], 1)
            inc(e.dma_start(out=ddall[:], in_=ccd_out.ap()[:, 0:1],
                            allow_slow_non_contiguous=True), "GD", 16)
        blk.gpsimd(_)
        gd_cc = cnt["GD"]

        def _(e):
            sdk(e, "s2s")
            w(e, "GD", gd_cc)
            for gi in range(4):
                for r in range(NCORE):
                    inc(e.dma_start(
                        out=sp4[0:4, 8 * gi + r:8 * gi + r + 1],
                        in_=ddall[16 * r + 4 * gi:16 * r + 4 * gi + 4, :],
                        allow_slow_non_contiguous=True), "SD", 16)
        blk.sync(_)
        sd_dd = cnt["SD"]

        def _(e):
            w(e, "SD", sd_dd)
            w(e, "CC")
            for gi in range(4):
                ins = e.matmul(pfm2[:, gi * 8:(gi + 1) * 8],
                               grpb[0:4, :],
                               sp4[0:4, 8 * gi:8 * (gi + 1)],
                               start=True, stop=True)
            inc(ins, "TT")
        blk.tensor(_)
        tt_dr = cnt["TT"]

        def _(e):
            w(e, "TT", tt_dr)
            for gi in range(4):
                ins = e.tensor_copy(drcols[:, gi * 8:(gi + 1) * 8],
                                    pfm2[:, gi * 8:(gi + 1) * 8])
            inc(ins, "VV")
        blk.vector(_)
        vv_dr = cnt["VV"]

        def _(e):
            w(e, "VV", vv_dr)
            w(e, "SD", sd_w)
            dv3 = dvb[:].rearrange("p (g r) -> p g r", g=4)
            dr3 = drcols[:].rearrange("p (g r) -> p g r", g=4)
            for r in range(NCORE):
                ins = e.scalar_tensor_tensor(
                    dv3[:, :, r], dr3[:, :, r], seldb[:, r:r + 1],
                    invsdb[:, r:r + 1].broadcast_to([128, 4]),
                    ALU.mult, ALU.add)
            inc(ins, "VV")
        blk.vector(_)
        vv_dv = cnt["VV"]

        for r in range(NCORE):
            def _(e, r=r):
                w(e, "GD", gd_cc)
                if r >= 1:
                    w(e, "VV")
                inc(e.dma_start(
                    out=avb[:],
                    in_=ccA_out.ap()[r * 128:(r + 1) * 128, :]), "GD", 16)
            blk.gpsimd(_)
            gd_av = cnt["GD"]

            def _(e, r=r, gd_av=gd_av):
                w(e, "GD", gd_av)
                if r == 0:
                    w(e, "VV", vv_dv)
                for gi in range(4):
                    e.tensor_scalar(avb[:, gi * HD:(gi + 1) * HD],
                                    avb[:, gi * HD:(gi + 1) * HD],
                                    selAb[:, r:r + 1], None, ALU.mult)
                for gi in range(4):
                    ins = e.scalar_tensor_tensor(
                        hbuf[:, gi * HD:(gi + 1) * HD],
                        hbuf[:, gi * HD:(gi + 1) * HD],
                        dvb[:, gi * 8 + r:gi * 8 + r + 1],
                        avb[:, gi * HD:(gi + 1) * HD],
                        ALU.mult, ALU.add)
                inc(ins, "VV")
            blk.vector(_)
        vv_h = cnt["VV"]

        def _(e):
            w(e, "VV", vv_h)
            w(e, "CC", cc_pcol)
            for t in range(NT):
                for gi in range(4):
                    e.tensor_scalar(avb[:, gi * HD:(gi + 1) * HD],
                                    hbuf[:, gi * HD:(gi + 1) * HD],
                                    pcolsb[:, gi * 16 + t:gi * 16 + t + 1],
                                    None, ALU.mult)
                for gi in range(4):
                    ins = e.tensor_add(
                        snaps[:, (t * 4 + gi) * HD:(t * 4 + gi + 1) * HD],
                        snaps[:, (t * 4 + gi) * HD:(t * 4 + gi + 1) * HD],
                        avb[:, gi * HD:(gi + 1) * HD])
            inc(ins, "VV")
        blk.vector(_)
        vv_snapc = cnt["VV"]

        def _(e):
            w(e, "TT", tt_projtm[-1])
            inc(e.memset(bpadB[:], 0.0), "VV")
        blk.vector(_)
        vv_bpB = cnt["VV"]

        def _(e):
            sdk(e, "s2s")
            w(e, "VV", vv_bpB)
            for h in range(1, H, 2):
                gp = h // 2
                dst_t = gp // 4
                base = 32 * (gp % 4) + 16
                inc(e.dma_start(
                    out=bpadB[base:base + 16, dst_t * TOK:(dst_t + 1) * TOK],
                    in_=cwb[16 * (h % 8):16 * (h % 8) + 16,
                            (h // 8) * 2048:(h // 8 + 1) * 2048]), "SD", 16)
        blk.sync(_)
        sd_bpB = cnt["SD"]

        def _(e):
            w(e, "SD", sd_f2)
            w(e, "SD", sd_bpB)
            w(e, "SD", xn_sd[-1])
            w(e, "CC")
            w(e, "VV", fm_vv)
            w(e, "VV")
            inc(e.memset(cwb[:], 0.0), "VV")
        blk.vector(_)
        vv_cm = cnt["VV"]

        def _(e):
            sdk(e, "s2s")
            w(e, "VV", vv_cm)
            w(e, "CC", fm_cc)
            for h in range(H):
                inc(e.dma_start(
                    out=cwb[32 * (h % 4):32 * (h % 4) + 16,
                            (h // 4) * 2048:(h // 4 + 1) * 2048],
                    in_=cfm[16 * (h % 8):16 * (h % 8) + 16,
                            (h // 8) * 2048:(h // 8 + 1) * 2048]), "SD", 16)
            w(e, "CC")
            for gi in range(4):
                inc(e.dma_start(out=scr32[32 * gi:32 * gi + 4, :],
                                in_=packA[32 + 4 * gi:32 + 4 * gi + 4, :]),
                    "SD", 16)
        blk.sync(_)
        sd_cp = cnt["SD"]

        for gi in range(4):
            def _(e, gi=gi):
                w(e, "SD", sd_cp)
                w(e, "CC", cc_zfm)
                if gi == 0:
                    w(e, "VV", vv_dr)
                    w(e, "CC")
                else:
                    w(e, "CC")
                for q in range(4):
                    pp = pfm2 if q < 2 else pfm
                    ins = e.matmul(pp[:, (q % 2) * 512:(q % 2 + 1) * 512],
                                   grpb[32 * gi:32 * gi + 4, :],
                                   scr32[32 * gi:32 * gi + 4,
                                         q * 512:(q + 1) * 512],
                                   start=True, stop=True,
                                   tile_position=(32 * gi, 0))
                inc(ins, "TT")
            blk.tensor(_)
            tt_ea = cnt["TT"]

            def _(e, gi=gi, tt_ea=tt_ea):
                w(e, "TT", tt_ea)
                e.activation(eag[:, 0:512], pfm2[:, 0:512], AF.Exp,
                             bias=0.0, scale=1.0)
                e.activation(eag[:, 512:1024], pfm2[:, 512:1024], AF.Exp,
                             bias=0.0, scale=1.0)
                e.activation(eag[:, 1024:1536], pfm[:, 0:512], AF.Exp,
                             bias=0.0, scale=1.0)
                inc(e.activation(eag[:, 1536:2048], pfm[:, 512:1024], AF.Exp,
                                 bias=0.0, scale=1.0), "CC")
            blk.scalar(_)
            cc_ea = cnt["CC"]

            def _(e, gi=gi, cc_ea=cc_ea):
                w(e, "CC", cc_ea)
                w(e, "SD", sd_cp)
                inc(e.tensor_tensor(cwb[:, gi * 2048:(gi + 1) * 2048],
                                    cwb[:, gi * 2048:(gi + 1) * 2048],
                                    eag[:], ALU.mult), "VV")
            blk.vector(_)
        vv_cpr = cnt["VV"]

        tt_lastA = [0]
        for h in range(H):
            pair, odd = h // 2, h % 2
            scob = scobO if odd else scobE

            def _(e, h=h):
                sdk(e, "s2s")
                if h >= 1:
                    w(e, "TT", tt_lastA[0])
                else:
                    w(e, "VV")
                inc(e.dma_start(out=lhsA[1:17, :],
                                in_=packA[64 + h:65 + h, :]), "SD", 16)
                inc(e.dma_start(out=rhsA[0:1, :],
                                in_=packA[32 + h:33 + h, :]), "SD", 16)
            blk.sync(_)
            sd_ab = cnt["SD"]

            for q in range(4):
                def _(e, h=h, q=q, sd_ab=sd_ab):
                    w(e, "SD", sd_ab)
                    if h == 0 and q == 0:
                        w(e, "CC", cc_zfm)
                    w(e, "VV")
                    inc(e.matmul(pfm[:, (q % 2) * 512:(q % 2 + 1) * 512],
                                 lhsA[0:17, :],
                                 rhsA[0:17, q * 512:(q + 1) * 512],
                                 start=True, stop=True), "TT")
                blk.tensor(_)
                tt_a = cnt["TT"]
                if q == 3:
                    tt_lastA[0] = tt_a

                def _(e, h=h, q=q, tt_a=tt_a):
                    w(e, "TT", tt_a)
                    if h == 0 and q == 0:
                        w(e, "GD", gd_w)
                    m0v = m0b[:].unsqueeze(1).broadcast_to([128, 4, 128])
                    inc(e.tensor_tensor(
                        scr32[:, q * 512:(q + 1) * 512].rearrange(
                            "p (a b) -> p a b", a=4),
                        pfm[:, (q % 2) * 512:(q % 2 + 1) * 512].rearrange(
                            "p (a b) -> p a b", a=4),
                        m0v, ALU.add), "VV")
                blk.vector(_)
            vv_arg = cnt["VV"]

            def _(e, h=h, vv_arg=vv_arg):
                w(e, "VV", vv_arg)
                e.activation(expb[:, 0:1024], scr32[:, 0:1024], AF.Exp,
                             bias=0.0, scale=1.0)
                inc(e.activation(expb[:, 1024:2048], scr32[:, 1024:2048],
                                 AF.Exp, bias=0.0, scale=1.0), "CC")
            blk.scalar(_)
            cc_exp = cnt["CC"]

            def _(e, h=h):
                if h == 0:
                    w(e, "SD", sd_cp)
                    w(e, "VV", vv_cpr)
                w(e, "VV")
                gp = h // 2
                base = 32 * (gp % 4)
                dst_t = gp // 4
                bp = bpad if h % 2 == 0 else bpadB
                for T in range(NT):
                    ins = e.matmul(
                        pbig[:, T * 128:(T + 1) * 128],
                        bp[base:base + 32,
                           dst_t * TOK + T * 128:dst_t * TOK +
                           (T + 1) * 128],
                        cfm[base:base + 32,
                            dst_t * TOK + T * 128:dst_t * TOK +
                            (T + 1) * 128],
                        start=True, stop=True, tile_position=(base, 0))
                inc(ins, "TT")
            blk.tensor(_)
            tt_raw = cnt["TT"]

            def _(e, h=h, tt_raw=tt_raw, cc_exp=cc_exp, scob=scob):
                w(e, "TT", tt_raw)
                w(e, "CC", cc_exp)
                e.tensor_tensor(scob[:, 0:1024], pbig[:, 0:1024],
                                expb[:, 0:1024], ALU.mult)
                inc(e.tensor_tensor(scob[:, 1024:2048], pbig[:, 1024:2048],
                                    expb[:, 1024:2048], ALU.mult), "VV")
            blk.vector(_)
            vv_sco = cnt["VV"]

            if _DEBUG_DUMP and h == 0:
                def _(e, vv_sco=vv_sco):
                    w(e, "VV", vv_sco)
                    inc(e.dma_start(out=dbgS.ap(), in_=scobE[:]), "SD", 16)
                    inc(e.dma_start(out=dbgE.ap(), in_=expb[:]), "SD", 16)
                blk.sync(_)
                def _(e, vv_sco=vv_sco):
                    w(e, "VV", vv_sco)
                    inc(e.tensor_copy(scr32[:], pbig[:]), "VV")
                blk.vector(_)
                vv_dbg = cnt["VV"]
                def _(e, vv_dbg=vv_dbg):
                    w(e, "VV", vv_dbg)
                    inc(e.dma_start(out=dbgR.ap(), in_=scr32[:]), "SD", 16)
                blk.sync(_)

            if odd:
                def _(e, pair=pair, vv_sco=vv_sco):
                    w(e, "VV", vv_sco)
                    w(e, "VV", vv_snapc)
                    for hh in (2 * pair, 2 * pair + 1):
                        par = hh % 2
                        sc = scobO if par else scobE
                        base = 32 * (hh % 4)
                        gtile = hh // 4
                        for T in range(NT):
                            ins = e.matmul(
                                pbig[64 * par:64 * par + 64,
                                     T * 128:(T + 1) * 128],
                                xptm[:, T * D + hh * HD:
                                     T * D + (hh + 1) * HD],
                                sc[:, T * 128:(T + 1) * 128],
                                start=(T % 4 == 0), stop=False,
                                tile_position=(0, 64 * par))
                        inc(ins, "TT")
                        w(e, "TT")
                        for T in range(NT):
                            ins = e.matmul(
                                pbig[64 * par:64 * par + 64,
                                     T * 128:(T + 1) * 128],
                                snaps[base:base + 32,
                                      (T * 4 + gtile) * HD:
                                      (T * 4 + gtile + 1) * HD],
                                cwb[base:base + 32,
                                    gtile * 2048 + T * 128:
                                    gtile * 2048 + (T + 1) * 128],
                                start=False, stop=(T % 4 == 3),
                                tile_position=(32 * (hh % 4), 64 * par))
                        inc(ins, "TT")
                        w(e, "TT")
                blk.tensor(_)
                tt_y = cnt["TT"]

                def _(e, pair=pair, tt_y=tt_y):
                    w(e, "TT", tt_y)
                    if pair == 0:
                        w(e, "SD", sd_xpfm[-1])
                    ins = e.scalar_tensor_tensor(
                        fmb[:, pair * TOK:(pair + 1) * TOK],
                        fmb[:, pair * TOK:(pair + 1) * TOK],
                        dcolb[:, pair:pair + 1], pbig[:],
                        ALU.mult, ALU.add)
                    inc(ins, "VV")
                blk.vector(_)
        vv_y = cnt["VV"]

        if _DEBUG_DUMP:
            def _(e):
                w(e, "VV")
                inc(e.dma_start(out=dbgY.ap(), in_=fmb[:]), "SD", 16)
            blk.sync(_)

        for gi in range(8):
            def _(e, gi=gi):
                if gi == 0:
                    w(e, "VV")
                    w(e, "CC", cc_zfm)
                inc(e.scalar_tensor_tensor(
                    fmb[:, gi * TOK:(gi + 1) * TOK],
                    fmb[:, gi * TOK:(gi + 1) * TOK],
                    nwcb[:, gi:gi + 1],
                    zfm[:, gi * TOK:(gi + 1) * TOK], ALU.mult, ALU.mult),
                    "VV")
            blk.vector(_)
        vv_gate = cnt["VV"]

        def _(e):
            w(e, "TT")
            w(e, "CC", cc_zfm)
            for k in range(8):
                inc(e.dma_start(out=cwb[:, k * D:(k + 1) * D],
                                in_=dwout[k * 128:(k + 1) * 128, :]),
                    "GD", 16)
            inc(e.dma_start(out=gb[:], in_=dgbc), "GD", 16)
            inc(e.dma_start(out=bb[:], in_=dbbc), "GD", 16)
        blk.gpsimd(_)
        gd_wo = cnt["GD"]

        xrot2 = ztm[:, 6 * TOK:7 * TOK].bitcast(F32)
        xrots = [xrotT, xrot2]
        mobs = [postbuf[:, 0:D], scr32[:, 0:D]]
        x1bs = [postbuf[:, D:TOK], scr32[:, D:TOK]]
        tAs = [ztm[:, 0:D], ztm[:, 2 * D:3 * D]]
        vbs = [ztm[:, D:2 * D], ztm[:, 3 * D:4 * D]]
        txs = [packA[:, 0:D], packA[:, D:2 * D]]
        CB = [0, 16]

        cd_t = {}
        tt_mo_t = {}
        s0_vv = {}
        od_t = {}
        for p in range(NT // 2):
            pair = (2 * p, 2 * p + 1)
            for t in pair:
                def _(e, t=t, p=p):
                    if p == 0:
                        w(e, "CC", cc_zfm)
                    else:
                        w(e, "VV")
                    inc(e.dma_start(out=xrots[t % 2][:],
                                    in_=dx[t * 128:(t + 1) * 128, :]),
                        "CD", 16)
                blk.scalar(_)
                cd_t[t] = cnt["CD"]

            for t in pair:
                pmo = pfm if t % 2 == 0 else pfm2

                def _(e, t=t, pmo=pmo):
                    w(e, "GD", gd_wo)
                    w(e, "VV", vv_gate)
                    if t >= 2:
                        w(e, "VV", s0_vv[t - 2])
                    for k in range(8):
                        st = fmb[:, k * TOK + t * 128:k * TOK + (t + 1) * 128]
                        for q in range(2):
                            ins = e.matmul(pmo[:, q * 512:(q + 1) * 512], st,
                                           cwb[:, k * D + q * 512:
                                               k * D + (q + 1) * 512],
                                           start=(k == 0), stop=(k == 7))
                    inc(ins, "TT")
                blk.tensor(_)
                tt_mo_t[t] = cnt["TT"]

            for t in pair:
                pmo = pfm if t % 2 == 0 else pfm2

                def _(e, t=t, pmo=pmo):
                    w(e, "TT", tt_mo_t[t])
                    B = CB[t % 2]
                    inc(e.tensor_scalar(mobs[t % 2][:], pmo[:], 1.0, 0.0,
                                        ALU.mult, ALU.add,
                                        accum_out=colc[:, B:B + 1]), "VV")
                blk.vector(_)
                s0_vv[t] = cnt["VV"]

            def _(e, p=p):
                for t in pair:
                    B = CB[t % 2]
                    e.scalar_tensor_tensor(
                        tAs[t % 2][:], mobs[t % 2][:], 1.0, mobs[t % 2][:],
                        ALU.mult, ALU.mult,
                        accum_out=colc[:, B + 1:B + 2])
                for t in pair:
                    B = CB[t % 2]
                    e.tensor_scalar(colc[:, B + 2:B + 3],
                                    colc[:, B:B + 1], 1.0 / D, None, ALU.mult)
                    e.tensor_scalar(colc[:, B + 3:B + 4],
                                    colc[:, B + 1:B + 2], 1.0 / D, None,
                                    ALU.mult)
                for t in pair:
                    B = CB[t % 2]
                    ins = e.scalar_tensor_tensor(
                        colc[:, B + 4:B + 5], colc[:, B + 2:B + 3],
                        colc[:, B + 2:B + 3], colc[:, B + 3:B + 4],
                        ALU.mult, ALU.subtract)
                inc(ins, "VV")
            blk.vector(_)
            vv_st1 = cnt["VV"]

            def _(e, p=p, vv_st1=vv_st1):
                w(e, "VV", vv_st1)
                for t in pair:
                    B = CB[t % 2]
                    e.activation(colc[:, B + 5:B + 6], colc[:, B + 4:B + 5],
                                 AF.Ln, bias=epsc[:, 0:1], scale=-1.0)
                    ins = e.activation(colc[:, B + 6:B + 7],
                                       colc[:, B + 5:B + 6], AF.Exp,
                                       bias=0.0, scale=-0.5)
                inc(ins, "CC")
            blk.scalar(_)
            cc_r2 = cnt["CC"]

            def _(e, p=p, cc_r2=cc_r2):
                w(e, "CC", cc_r2)
                w(e, "CD", cd_t[pair[1]])
                if p >= 1:
                    w(e, "OD", od_t[pair[0] - 2])
                    w(e, "OD", od_t[pair[1] - 2])
                for t in pair:
                    B = CB[t % 2]
                    e.tensor_scalar(colc[:, B + 7:B + 8],
                                    colc[:, B + 2:B + 3],
                                    colc[:, B + 6:B + 7], -1.0, ALU.mult,
                                    ALU.mult)
                for t in pair:
                    B = CB[t % 2]
                    e.tensor_scalar(txs[t % 2][:], xrots[t % 2][:],
                                    colc[:, B + 7:B + 8], None, ALU.add)
                for t in pair:
                    B = CB[t % 2]
                    e.scalar_tensor_tensor(
                        x1bs[t % 2][:], mobs[t % 2][:],
                        colc[:, B + 6:B + 7], txs[t % 2][:], ALU.mult,
                        ALU.add, accum_out=colc[:, B:B + 1])
                for t in pair:
                    B = CB[t % 2]
                    e.scalar_tensor_tensor(
                        tAs[t % 2][:], x1bs[t % 2][:], 1.0, x1bs[t % 2][:],
                        ALU.mult, ALU.mult,
                        accum_out=colc[:, B + 1:B + 2])
                for t in pair:
                    B = CB[t % 2]
                    e.tensor_scalar(colc[:, B + 2:B + 3],
                                    colc[:, B:B + 1], 1.0 / D, None, ALU.mult)
                    e.tensor_scalar(colc[:, B + 3:B + 4],
                                    colc[:, B + 1:B + 2], 1.0 / D, None,
                                    ALU.mult)
                for t in pair:
                    B = CB[t % 2]
                    ins = e.scalar_tensor_tensor(
                        colc[:, B + 4:B + 5], colc[:, B + 2:B + 3],
                        colc[:, B + 2:B + 3], colc[:, B + 3:B + 4],
                        ALU.mult, ALU.subtract)
                inc(ins, "VV")
            blk.vector(_)
            vv_st2 = cnt["VV"]

            def _(e, p=p, vv_st2=vv_st2):
                w(e, "VV", vv_st2)
                for t in pair:
                    B = CB[t % 2]
                    e.activation(colc[:, B + 5:B + 6], colc[:, B + 4:B + 5],
                                 AF.Ln, bias=epsc[:, 0:1], scale=-1.0)
                    e.activation(colc[:, B + 6:B + 7], colc[:, B + 5:B + 6],
                                 AF.Exp, bias=0.0, scale=-0.5)
                    ins = e.activation(colc[:, B + 8:B + 9],
                                       colc[:, B + 5:B + 6], AF.Exp,
                                       bias=0.0, scale=-1.0)
                inc(ins, "CC")
            blk.scalar(_)
            cc_r3 = cnt["CC"]

            def _(e, p=p, cc_r3=cc_r3):
                w(e, "CC", cc_r3)
                for t in pair:
                    B = CB[t % 2]
                    e.tensor_scalar(colc[:, B + 7:B + 8],
                                    colc[:, B + 2:B + 3],
                                    colc[:, B + 6:B + 7], -1.0, ALU.mult,
                                    ALU.mult)
                for t in pair:
                    B = CB[t % 2]
                    ins = e.tensor_scalar(colc[:, B + 9:B + 10],
                                          colc[:, B + 4:B + 5],
                                          colc[:, B + 8:B + 9], -1.0,
                                          ALU.mult, ALU.mult)
                inc(ins, "VV")
            blk.vector(_)
            vv_vu = cnt["VV"]

            def _(e, p=p, vv_vu=vv_vu):
                w(e, "VV", vv_vu)
                for t in pair:
                    B = CB[t % 2]
                    e.activation(colc[:, B + 10:B + 11],
                                 colc[:, B + 9:B + 10], AF.Ln,
                                 bias=epsc[:, 0:1], scale=1.0)
                    ins = e.activation(colc[:, B + 11:B + 12],
                                       colc[:, B + 10:B + 11], AF.Exp,
                                       bias=0.0, scale=-0.5)
                inc(ins, "CC")
            blk.scalar(_)
            cc_rs = cnt["CC"]

            def _(e, p=p, cc_rs=cc_rs):
                w(e, "CC", cc_rs)
                w(e, "GD", gd_wo)
                for t in pair:
                    B = CB[t % 2]
                    e.tensor_scalar(colc[:, B + 13:B + 14],
                                    colc[:, B + 7:B + 8],
                                    colc[:, B + 11:B + 12], None, ALU.mult)
                for t in pair:
                    B = CB[t % 2]
                    e.tensor_scalar(colc[:, B + 12:B + 13],
                                    colc[:, B + 6:B + 7],
                                    colc[:, B + 11:B + 12], None, ALU.mult)
                for t in pair:
                    B = CB[t % 2]
                    e.scalar_tensor_tensor(
                        tAs[t % 2][:], x1bs[t % 2][:],
                        colc[:, B + 12:B + 13],
                        colc[:, B + 13:B + 14].broadcast_to([128, D]),
                        ALU.mult, ALU.add)
                for t in pair:
                    e.tensor_tensor(vbs[t % 2][:], tAs[t % 2][:], gb[:],
                                    ALU.mult)
                for t in pair:
                    B = CB[t % 2]
                    e.scalar_tensor_tensor(
                        vbs[t % 2][:], vbs[t % 2][:], 1.0, bb[:], ALU.mult,
                        ALU.add, accum_out=colc[:, B:B + 1])
                for t in pair:
                    B = CB[t % 2]
                    e.scalar_tensor_tensor(
                        tAs[t % 2][:], vbs[t % 2][:], 1.0, vbs[t % 2][:],
                        ALU.mult, ALU.mult,
                        accum_out=colc[:, B + 1:B + 2])
                for t in pair:
                    B = CB[t % 2]
                    e.tensor_scalar(colc[:, B + 2:B + 3],
                                    colc[:, B:B + 1], 1.0 / D, None, ALU.mult)
                    e.tensor_scalar(colc[:, B + 3:B + 4],
                                    colc[:, B + 1:B + 2], 1.0 / D, None,
                                    ALU.mult)
                for t in pair:
                    B = CB[t % 2]
                    ins = e.scalar_tensor_tensor(
                        colc[:, B + 4:B + 5], colc[:, B + 2:B + 3],
                        colc[:, B + 2:B + 3], colc[:, B + 3:B + 4],
                        ALU.mult, ALU.subtract)
                inc(ins, "VV")
            blk.vector(_)
            vv_st3 = cnt["VV"]

            def _(e, p=p, vv_st3=vv_st3):
                w(e, "VV", vv_st3)
                for t in pair:
                    B = CB[t % 2]
                    e.activation(colc[:, B + 5:B + 6], colc[:, B + 4:B + 5],
                                 AF.Ln, bias=epsc[:, 0:1], scale=-1.0)
                    ins = e.activation(colc[:, B + 6:B + 7],
                                       colc[:, B + 5:B + 6], AF.Exp,
                                       bias=0.0, scale=-0.5)
                inc(ins, "CC")
            blk.scalar(_)
            cc_r4 = cnt["CC"]

            def _(e, p=p, cc_r4=cc_r4):
                w(e, "CC", cc_r4)
                for t in pair:
                    B = CB[t % 2]
                    e.tensor_scalar(colc[:, B + 7:B + 8],
                                    colc[:, B + 2:B + 3],
                                    colc[:, B + 6:B + 7], -1.0, ALU.mult,
                                    ALU.mult)
                for t in pair:
                    B = CB[t % 2]
                    e.scalar_tensor_tensor(
                        txs[t % 2][:], vbs[t % 2][:], colc[:, B + 6:B + 7],
                        colc[:, B + 7:B + 8].broadcast_to([128, D]),
                        ALU.mult, ALU.add)
                for t in pair:
                    inc(e.tensor_tensor(x1bs[t % 2][:], x1bs[t % 2][:],
                                        txs[t % 2][:], ALU.add), "VV")
            blk.vector(_)
            vv_o1 = cnt["VV"] - 1
            vv_o2 = cnt["VV"]

            for t, vvn in zip(pair, (vv_o1, vv_o2)):
                def _(e, t=t, vvn=vvn):
                    sdk(e, "hbm")
                    w(e, "VV", vvn)
                    inc(e.dma_start(out=dout[t * 128:(t + 1) * 128, :],
                                    in_=x1bs[t % 2][:]), "OD", 16)
                blk.sync(_)
                od_t[t] = cnt["OD"]

        def _(e):
            w(e, "OD")
        blk.sync(_)

    return nc


_CACHE = {}


def _host_prep(inputs):
    x = np.asarray(inputs["L0_attn_x"], dtype=np.float32).reshape(16384, D)
    w_inT = np.ascontiguousarray(
        np.asarray(inputs["L0_attn_W_in"], dtype=np.float32).T)
    w_outT = np.ascontiguousarray(
        np.asarray(inputs["L0_attn_W_out"], dtype=np.float32).T)
    dt_WT = np.ascontiguousarray(
        np.asarray(inputs["L0_attn_dt_W"], dtype=np.float32).T)
    dt_b = np.asarray(inputs["L0_attn_dt_b"],
                      dtype=np.float32).reshape(16, 1)
    A_log = np.asarray(inputs["L0_attn_A_log"],
                       dtype=np.float32).reshape(16, 1)
    Dp = np.asarray(inputs["L0_attn_D_param"], dtype=np.float32)
    nw = np.asarray(inputs["L0_attn_norm_w"], dtype=np.float32)
    ln_g = np.asarray(inputs["L0_ffn_ln_g"], dtype=np.float32)
    ln_b = np.asarray(inputs["L0_ffn_ln_b"], dtype=np.float32)

    Dcols = np.empty((128, 8), np.float32)
    for g in range(8):
        Dcols[0:64, g] = Dp[2 * g]
        Dcols[64:128, g] = Dp[2 * g + 1]
    nwcols = nw.reshape(8, 128).T.copy()
    gbc = np.broadcast_to(ln_g, (128, D)).copy()
    bbc = np.broadcast_to(ln_b, (128, D)).copy()

    resetm = np.ones((16, TOK), np.float32)
    resetm[:, ::CH] = 0.0
    jj, ii = np.meshgrid(np.arange(128), np.arange(128), indexing="ij")
    m0 = np.where(ii >= jj, 0.0, NEG).astype(np.float32)
    tileind = np.zeros((16, TOK), np.float32)
    for t in range(16):
        tileind[t, t * 128:(t + 1) * 128] = 1.0
    ident = np.eye(128, dtype=np.float32)
    grp4rep = np.zeros((128, 128), np.float32)
    for g in range(4):
        for m in range(4):
            grp4rep[32 * g + m, 32 * m:32 * (m + 1)] = 1.0

    common = dict(w_inT=w_inT, w_outT=w_outT, dt_WT=dt_WT, dt_b=dt_b,
                  A_log=A_log, Dcols=Dcols, nwcols=nwcols, gbc=gbc, bbc=bbc,
                  resetm=resetm, m0=m0, tileind=tileind, grp4rep=grp4rep,
                  ident=ident)
    in_maps = []
    for c in range(NCORE):
        selA = np.zeros((128, 8), np.float32)
        selA[:, :c] = 1.0
        seld = np.zeros((128, 8), np.float32)
        seld[:, :c] = 1.0
        invseld = 1.0 - seld
        m = dict(common)
        m["x"] = np.ascontiguousarray(x[c * TOK:(c + 1) * TOK])
        m["selA"] = selA
        m["seld"] = seld
        m["invseld"] = invseld
        in_maps.append(m)
    return in_maps


def kernel(**inputs):
    if "nc" not in _CACHE:
        _CACHE["nc"] = build_nc()
    nc = _CACHE["nc"]
    in_maps = _host_prep(inputs)
    res = run_bass_kernel_spmd(nc, in_maps, core_ids=list(range(NCORE)))
    out = np.concatenate([res.results[c]["out"] for c in range(NCORE)],
                         axis=0)
    return out.reshape(1, 16384, D).astype(np.float32)


if __name__ == "__main__":
    build_nc()
    print("build ok")


# version 16 (final)
# speedup vs baseline: 1.5926x; 1.1497x over previous
from contextlib import ExitStack

import numpy as np

import concourse.bass as bass
import concourse.mybir as mybir
from concourse.bass_utils import run_bass_kernel_spmd

F32 = mybir.dt.float32
BF16 = mybir.dt.bfloat16
AF = mybir.ActivationFunctionType
ALU = mybir.AluOpType

NCORE = 8
TOK = 2048
D = 1024
H = 16
HD = 64
CH = 128
NT = TOK // CH
PROJ = 2576
NEG = -1.0e30
_DETECT_RACES = False
_DEBUG_DUMP = False


def build_nc():
    nc = bass.Bass("TRN2", target_bir_lowering=False,
                   detect_race_conditions=_DETECT_RACES)

    dx = nc.dram_tensor("x", [TOK, D], F32, kind="ExternalInput").ap()
    dwin = nc.dram_tensor("w_inT", [D, PROJ], F32, kind="ExternalInput").ap()
    dwout = nc.dram_tensor("w_outT", [D, D], F32, kind="ExternalInput").ap()
    ddtw = nc.dram_tensor("dt_WT", [16, 16], F32, kind="ExternalInput").ap()
    ddtb = nc.dram_tensor("dt_b", [16, 1], F32, kind="ExternalInput").ap()
    dalog = nc.dram_tensor("A_log", [16, 1], F32, kind="ExternalInput").ap()
    ddcol = nc.dram_tensor("Dcols", [128, 8], F32, kind="ExternalInput").ap()
    dnwc = nc.dram_tensor("nwcols", [128, 8], F32, kind="ExternalInput").ap()
    dgbc = nc.dram_tensor("gbc", [128, D], F32, kind="ExternalInput").ap()
    dbbc = nc.dram_tensor("bbc", [128, D], F32, kind="ExternalInput").ap()
    drst = nc.dram_tensor("resetm", [16, TOK], F32, kind="ExternalInput").ap()
    dm0 = nc.dram_tensor("m0", [128, 128], F32, kind="ExternalInput").ap()
    dtind = nc.dram_tensor("tileind", [16, TOK], F32, kind="ExternalInput").ap()
    dgrp = nc.dram_tensor("grp4rep", [128, 128], F32, kind="ExternalInput").ap()
    dident = nc.dram_tensor("ident", [128, 128], F32, kind="ExternalInput").ap()
    dselA = nc.dram_tensor("selA", [128, 8], F32, kind="ExternalInput").ap()
    dseld = nc.dram_tensor("seld", [128, 8], F32, kind="ExternalInput").ap()
    dinvsd = nc.dram_tensor("invseld", [128, 8], F32,
                            kind="ExternalInput").ap()
    dout = nc.dram_tensor("out", [TOK, D], F32, kind="ExternalOutput").ap()

    ccA_in = nc.dram_tensor("ccA_in", [128, 4 * HD], F32)
    ccA_out = nc.dram_tensor("ccA_out", [NCORE * 128, 4 * HD], F32)
    ccd_in = nc.dram_tensor("ccd_in", [16, 8], F32)
    if _DEBUG_DUMP:
        dbgY = nc.dram_tensor("dbgY", [128, 8 * TOK], BF16)
        dbgS = nc.dram_tensor("dbgS", [128, TOK], BF16)
        dbgE = nc.dram_tensor("dbgE", [128, TOK], BF16)
        dbgR = nc.dram_tensor("dbgR", [128, TOK], F32)
    ccd_out = nc.dram_tensor("ccd_out", [NCORE * 16, 8], F32)

    es = ExitStack()
    sb = lambda n, s, d: es.enter_context(nc.sbuf_tensor(n, s, d))
    psa = lambda n, s, d: es.enter_context(nc.psum_tensor(n, s, d))

    w_in = sb("w_in_sb", [128, 8 * PROJ], BF16)
    fmb = sb("fmbuf", [128, 8 * TOK], BF16)
    xptm = sb("xp_tm", [128, NT * D], BF16)
    ztm = sb("siluz_tm", [128, NT * D], BF16)
    zfm = w_in
    bpadB = w_in[:, 8 * TOK:8 * TOK + 2 * TOK]
    cwb = sb("cwbuf", [128, 8 * D], BF16)
    bpad = sb("bpad2", [128, 2 * TOK], BF16)
    cfm = sb("cfm", [128, 2 * TOK], BF16)
    btm = sb("btm", [128, 2 * 256], BF16)
    packA = sb("packA", [128, TOK], F32)
    scr32 = sb("scr32", [128, TOK], F32)
    snaps = sb("snaps", [128, 16 * 4 * HD], BF16)
    srun = sb("srun", [128, 4 * HD], F32)
    wxb = cwb[:, 5 * D:6 * D]
    scobE = ztm[:, 0:TOK]
    scobO = ztm[:, TOK:2 * TOK]
    onerow = ztm[0:1, TOK:TOK + 128]
    expb = ztm[:, 2 * TOK:3 * TOK]
    eag = ztm[:, 3 * TOK:4 * TOK]
    ub = ztm[:, 4 * TOK:4 * TOK + D]
    vb = ztm[:, 4 * TOK + D:5 * TOK]
    gb = ztm[:, 5 * TOK:5 * TOK + D]
    bb = ztm[:, 5 * TOK + D:6 * TOK]
    fbfc = ztm[:, 6 * TOK:7 * TOK]
    postbuf = sb("postbuf", [128, TOK], F32)
    mob = postbuf[:, 0:D]
    x1b = postbuf[:, D:TOK]
    rhsA = postbuf[0:32, :]
    fbf = fbfc[0:16, :]
    lhsA = sb("lhsA", [32, 128], F32)
    m0b = sb("m0_sb", [128, 128], BF16)
    identb = sb("ident_sb", [128, 128], BF16)
    grpb = sb("grp_sb", [128, 128], F32)
    dtwb = sb("dtw_sb", [16, 16], F32)
    dtbb = sb("dtb_sb", [16, 1], F32)
    alogb = sb("alog_sb", [16, 1], F32)
    acolb = sb("acol_sb", [16, 1], F32)
    dcolb = sb("dcol_sb", [128, 8], F32)
    nwcb = sb("nwc_sb", [128, 8], F32)
    selAb = sb("selA_sb", [128, 8], F32)
    seldb = sb("seld_sb", [128, 8], F32)
    invsdb = sb("invsd_sb", [128, 8], F32)
    rcols = sb("rcols", [128, NT], F32)
    mrcols = sb("mrcols", [128, NT], F32)
    ftm = sb("ftm", [128, NT * 16], BF16)
    e16 = sb("e16", [128, 64], F32)
    sp4 = sb("sp4", [4, 64], F32)
    dcolsb = sb("dcolsb", [128, 64], F32)
    pcolsb = sb("pcolsb", [128, 64], F32)
    dcoreb = sb("dcoreb", [16, 8], F32)
    ddall = sb("ddall", [128, 1], F32)
    drcols = sb("drcols", [128, 32], F32)
    dvb = sb("dvb", [128, 32], F32)
    avb = sb("avb", [128, 4 * HD], F32)
    hbuf = sb("hbuf", [128, 4 * HD], BF16)
    colc = sb("colc", [128, 32], F32)
    onecol = sb("onecol", [128, 1], BF16)
    epsc = sb("epsc", [128, 3], F32)


    xnrot = cwb[:, 4 * D:5 * D]
    xrotL = cwb[:, 6 * D:8 * D].bitcast(F32)
    xrotT = ztm[:, 7 * TOK:8 * TOK].bitcast(F32)

    pfm = psa("pfm", [128, 1024], F32)
    pfm2 = psa("pfm2", [128, 1024], F32)
    pbig = psa("pbig", [128, 2048], F32)

    names = ["SD", "GD", "CD", "TT", "VV", "CC", "OD", "CS", "CT"]
    sems = {n: es.enter_context(nc.semaphore(n)) for n in names}
    cnt = {n: 0 for n in names}

    with es, nc.Block() as blk:
        def inc(ins, name, k=1):
            ins.then_inc(sems[name], k)
            cnt[name] += k

        def w(e, name, val=None):
            e.wait_ge(sems[name], cnt[name] if val is None else val)

        def barrier(e):
            for n in names:
                if cnt[n]:
                    e.wait_ge(sems[n], cnt[n])

        sd_kinds = set()

        def sdk(e, kind):
            if kind in ("tr", "s2s") and (sd_kinds - {kind, "hbm"}):
                e.wait_ge(sems["SD"], cnt["SD"])
                sd_kinds.clear()
            sd_kinds.add(kind)

        def _(e):
            inc(e.dma_start(out=identb[:], in_=dident), "GD", 16)
            for k in range(8):
                inc(e.dma_start(out=w_in[:, k * PROJ:(k + 1) * PROJ],
                                in_=dwin[k * 128:(k + 1) * 128, :]), "GD", 16)
            inc(e.dma_start(out=m0b[:], in_=dm0), "GD", 16)
        blk.gpsimd(_)
        gd_id = 16
        gd_w = cnt["GD"]

        def _(e):
            sdk(e, "hbm")
            for t_, d_ in ((grpb, dgrp), (dtwb, ddtw),
                           (dtbb, ddtb), (alogb, dalog), (dcolb, ddcol),
                           (nwcb, dnwc), (selAb, dselA), (seldb, dseld),
                           (invsdb, dinvsd)):
                inc(e.dma_start(out=t_[:], in_=d_), "SD", 16)
            inc(e.dma_start(out=rhsA[1:17, :], in_=dtind), "SD", 16)
        blk.sync(_)
        sd_w = cnt["SD"]

        def _(e):
            e.memset(onecol[:], 1.0)
            e.memset(epsc[:, 0:1], 1e-5)
            e.memset(dcoreb[:], 0.0)
            e.memset(e16[32:48, 16:32], 1.0)
            e.memset(epsc[:, 1:2], 1.0)
            e.memset(epsc[:, 2:3], 1e-6)
            e.memset(lhsA[:], 0.0)
            e.memset(lhsA[0:1, :], 1.0)
            e.memset(srun[:], 0.0)
            e.memset(hbuf[:], 0.0)
            inc(e.memset(pbig[:], 0.0), "VV")
        blk.vector(_)
        vv_init = cnt["VV"]

        xrotL2 = cwb[:, 0:2 * D].bitcast(F32)
        xnrot2 = cwb[:, 2 * D:3 * D]
        xrotsL = [xrotL, xrotL2]
        xnrots = [xnrot, xnrot2]
        CBL = [0, 16]
        vv_xn_t = {}
        tt_xn_t = {}
        cc_xn_t = {}
        fmb3 = fmb[:, 0:8 * TOK].rearrange("p (k c) -> p k c", k=8)
        for t in range(NT):
            par = t % 2

            def _(e, t=t, par=par):
                sdk(e, "hbm")
                if t >= 2:
                    w(e, "VV", vv_xn_t[t - 2])
                inc(e.dma_start(out=xrotsL[par][:],
                                in_=dx[t * 128:(t + 1) * 128, :]), "SD", 16)
            blk.sync(_)
            sd_x = cnt["SD"]

            def _(e, t=t, par=par, sd_x=sd_x):
                w(e, "SD", sd_x)
                B = CBL[par]
                e.tensor_scalar(scr32[:, 0:D], xrotsL[par][:], 1.0, 0.0,
                                ALU.mult, ALU.add,
                                accum_out=colc[:, B:B + 1])
                e.scalar_tensor_tensor(scr32[:, 0:D], xrotsL[par][:], 1.0,
                                       xrotsL[par][:], ALU.mult, ALU.mult,
                                       accum_out=colc[:, B + 1:B + 2])
                e.tensor_scalar(colc[:, B + 2:B + 3], colc[:, B:B + 1],
                                1.0 / D, None, ALU.mult)
                e.tensor_scalar(colc[:, B + 3:B + 4], colc[:, B + 1:B + 2],
                                1.0 / D, None, ALU.mult)
                inc(e.scalar_tensor_tensor(colc[:, B + 4:B + 5],
                                           colc[:, B + 2:B + 3],
                                           colc[:, B + 2:B + 3],
                                           colc[:, B + 3:B + 4],
                                           ALU.mult, ALU.subtract), "VV")
            blk.vector(_)
            vv_st = cnt["VV"]

            def _(e, t=t, par=par, vv_st=vv_st):
                w(e, "VV", vv_st)
                B = CBL[par]
                e.activation(colc[:, B + 5:B + 6], colc[:, B + 4:B + 5],
                             AF.Ln, bias=epsc[:, 0:1], scale=-1.0)
                inc(e.activation(colc[:, B + 6:B + 7], colc[:, B + 5:B + 6],
                                 AF.Exp, bias=0.0, scale=-0.5), "CC")
            blk.scalar(_)
            cc_r = cnt["CC"]

            def _(e, t=t, par=par, cc_r=cc_r):
                w(e, "CC", cc_r)
                B = CBL[par]
                inc(e.tensor_scalar(colc[:, B + 7:B + 8],
                                    colc[:, B + 2:B + 3],
                                    colc[:, B + 6:B + 7], None, ALU.mult),
                    "VV")
            blk.vector(_)
            vv_mr = cnt["VV"]

            def _(e, t=t, par=par, vv_mr=vv_mr):
                w(e, "VV", vv_mr)
                if t >= 2:
                    w(e, "TT", tt_xn_t[t - 2])
                B = CBL[par]
                inc(e.scalar_tensor_tensor(
                    xnrots[par][:], xrotsL[par][:], colc[:, B + 6:B + 7],
                    colc[:, B + 7:B + 8].broadcast_to([128, D]),
                    ALU.mult, ALU.subtract), "VV")
            blk.vector(_)
            vv_xn_t[t] = cnt["VV"]

            pz = pfm if par == 0 else pfm2

            def _(e, t=t, par=par, pz=pz):
                w(e, "VV", vv_xn_t[t])
                if t == 0:
                    w(e, "GD", gd_w)
                if t >= 2:
                    w(e, "CC", cc_xn_t[t - 2])
                for k in range(8):
                    ins = e.matmul(pz[:, k * 128:(k + 1) * 128],
                                   xnrots[par][:, k * 128:(k + 1) * 128],
                                   identb[:], start=True, stop=True)
                inc(ins, "TT")
            blk.tensor(_)
            tt_xn_t[t] = cnt["TT"]

            def _(e, t=t, pz=pz):
                w(e, "TT", tt_xn_t[t])
                inc(e.copy(fmb3[:, :, t * 128:(t + 1) * 128],
                           pz[:, 0:1024].rearrange("p (k c) -> p k c", k=8)),
                    "CC")
            blk.scalar(_)
            cc_xn_t[t] = cnt["CC"]

        fm_tt = []
        for mt in range(5):
            for half in range(2):
                pp = pfm if (mt * 2 + half) % 2 == 0 else pfm2
                M = 16 if mt == 4 else 128

                def _(e, mt=mt, half=half, pp=pp, M=M):
                    if mt == 0 and half == 0:
                        w(e, "CC", cc_xn_t[NT - 1])
                        w(e, "GD", gd_w)
                    if mt * 2 + half >= 2:
                        w(e, "VV")
                        w(e, "CC")
                    for k in range(8):
                        lhs = w_in[:, k * PROJ + 2048 + mt * 128:
                                   k * PROJ + 2048 + mt * 128 + M]
                        for q in range(2):
                            ins = e.matmul(
                                pp[0:M, q * 512:(q + 1) * 512], lhs,
                                fmb[:, k * TOK + half * 1024 + q * 512:
                                    k * TOK + half * 1024 + (q + 1) * 512],
                                start=(k == 0), stop=(k == 7))
                    inc(ins, "TT")
                blk.tensor(_)
                fm_tt.append(cnt["TT"])

                if mt < 2:
                    def _(e, mt=mt, half=half, pp=pp):
                        w(e, "TT", fm_tt[-1])
                        inc(e.tensor_copy(
                            cwb[:, mt * 2048 + half * 1024:
                                mt * 2048 + (half + 1) * 1024], pp[:]), "VV")
                    blk.vector(_)
                elif mt < 4:
                    def _(e, mt=mt, half=half, pp=pp):
                        w(e, "TT", fm_tt[-1])
                        inc(e.copy(
                            cfm[:, (mt - 2) * 2048 + half * 1024:
                                (mt - 2) * 2048 + (half + 1) * 1024],
                            pp[:]), "CC")
                    blk.scalar(_)
                else:
                    def _(e, half=half, pp=pp):
                        w(e, "TT", fm_tt[-1])
                        inc(e.tensor_copy(
                            packA[0:16, half * 1024:(half + 1) * 1024],
                            pp[0:16, :]), "VV")
                    blk.vector(_)
        fm_vv = cnt["VV"]
        fm_cc = cnt["CC"]

        def _(e):
            inc(e.memset(bpad[:], 0.0), "VV")
        blk.vector(_)
        vv_bm = cnt["VV"]

        def _(e):
            sdk(e, "s2s")
            w(e, "VV", vv_bm)
            w(e, "VV", fm_vv)
            for h in range(0, H, 2):
                gp = h // 2
                dst_t = gp // 4
                base = 32 * (gp % 4)
                inc(e.dma_start(
                    out=bpad[base:base + 16, dst_t * TOK:(dst_t + 1) * TOK],
                    in_=cwb[16 * (h % 8):16 * (h % 8) + 16,
                            (h // 8) * 2048:(h // 8 + 1) * 2048]), "SD", 16)
        blk.sync(_)
        sd_f2 = cnt["SD"]


        def _(e):
            w(e, "VV", fm_vv)
            w(e, "SD", sd_w)
            w(e, "VV")
            for q in range(4):
                ins = e.matmul(pbig[0:16, q * 512:(q + 1) * 512], dtwb[:],
                               packA[0:16, q * 512:(q + 1) * 512],
                               start=True, stop=True)
            inc(ins, "TT")
        blk.tensor(_)
        tt_dt = cnt["TT"]

        def _(e):
            w(e, "TT", tt_dt)
            e.activation(scr32[32:48, :], pbig[0:16, :], AF.Abs,
                         bias=dtbb[:], scale=1.0)
            e.activation(scr32[64:80, :], scr32[32:48, :], AF.Exp,
                         bias=0.0, scale=-1.0)
            e.activation(scr32[96:112, :], scr32[64:80, :], AF.Ln,
                         bias=epsc[0:16, 1:2], scale=1.0)
            e.activation(packA[96:112, :], pbig[0:16, :], AF.Relu,
                         bias=dtbb[:], scale=1.0)
            inc(e.activation(acolb[:], alogb[:], AF.Exp, bias=0.0,
                             scale=1.0), "CC")
        blk.scalar(_)
        cc_d1 = cnt["CC"]

        def _(e):
            w(e, "CC", cc_d1)
            e.tensor_add(packA[0:16, :], scr32[96:112, :], packA[96:112, :])
            e.tensor_scalar(acolb[:], acolb[:], -1.0, None, ALU.mult)
            inc(e.tensor_scalar(scr32[32:48, :], packA[0:16, :], acolb[:],
                                None, ALU.mult), "VV")
        blk.vector(_)
        vv_da = cnt["VV"]

        def _(e):
            sdk(e, "hbm")
            w(e, "VV", vv_da)
            inc(e.dma_start(out=packA[96:112, :], in_=drst), "SD", 16)
        blk.sync(_)
        sd_rst = cnt["SD"]

        def _(e):
            w(e, "VV", vv_da)
            inc(e.activation(scr32[64:80, :], packA[0:16, :], AF.Ln,
                             bias=0.0, scale=1.0), "CC")
        blk.scalar(_)
        cc_lndt = cnt["CC"]

        def _(e):
            w(e, "CC", cc_lndt)
            w(e, "SD", sd_rst)
            e.tensor_copy(scr32[96:112, :], scr32[32:48, :])
            inc(e.tensor_tensor_scan(packA[32:48, :], packA[96:112, :],
                                     scr32[96:112, :], 0.0, ALU.mult,
                                     ALU.add), "VV")
        blk.vector(_)
        vv_a = cnt["VV"]

        def _(e):
            w(e, "VV", vv_a)
            e.tensor_copy(scr32[32:48, :], scr32[64:80, :])
            e.tensor_sub(packA[64:80, :], scr32[32:48, :], packA[32:48, :])
            av3 = packA[32:48, :].rearrange("h (c i) -> h c i", c=NT)
            e.tensor_copy(e16[32:48, 0:16].unsqueeze(2), av3[:, :, 127:128])
            bcv = e16[32:48, 0:16].unsqueeze(2).broadcast_to([16, NT, 128])
            e.tensor_tensor(
                scr32[96:112, :].rearrange("h (c i) -> h c i", c=NT), bcv,
                av3, ALU.subtract)
            inc(e.tensor_tensor_scan(e16[32:48, 32:48], e16[32:48, 16:32],
                                     e16[32:48, 0:16], 0.0, ALU.mult,
                                     ALU.add), "VV")
        blk.vector(_)
        vv_pib = cnt["VV"]

        def _(e):
            w(e, "VV", vv_pib)
            inc(e.tensor_sub(e16[32:48, 48:64], e16[32:48, 32:48],
                             e16[32:48, 0:16]), "VV")
        blk.vector(_)
        vv_d2 = cnt["VV"]

        def _(e):
            w(e, "VV", vv_d2)
            e.activation(fbf[:], scr32[96:112, :], AF.Exp, bias=0.0,
                         scale=1.0)
            inc(e.activation(dcoreb[:, 0:1], e16[32:48, 47:48], AF.Exp,
                             bias=0.0, scale=1.0), "CC")
        blk.scalar(_)
        cc_d3 = cnt["CC"]

        def _(e):
            w(e, "CC", cc_d3)
            inc(e.tensor_tensor(fbf[:], packA[0:16, :], fbf[:], ALU.mult),
                "VV")
        blk.vector(_)
        vv_f = cnt["VV"]

        def _(e):
            sdk(e, "tr")
            w(e, "VV", vv_f)
            for t in range(NT):
                inc(e.dma_start(out=ftm[:, t * 16:(t + 1) * 16],
                                in_=fbf[:, t * 128:(t + 1) * 128],
                                transpose=True), "SD", 16)
        blk.sync(_)
        sd_ftm = cnt["SD"]

        def _(e):
            sdk(e, "s2s")
            w(e, "VV", vv_d2)
            for gi in range(4):
                inc(e.dma_start(out=sp4[0:4, 16 * gi:16 * (gi + 1)],
                                in_=e16[32 + 4 * gi:32 + 4 * gi + 4, 0:16]),
                    "SD", 16)
        blk.sync(_)
        sd_ae = cnt["SD"]

        def _(e):
            w(e, "SD", sd_ae)
            w(e, "SD", sd_w)
            for gi in range(4):
                ins = e.matmul(pfm2[:, gi * 16:(gi + 1) * 16],
                               grpb[0:4, :],
                               sp4[0:4, 16 * gi:16 * (gi + 1)],
                               start=True, stop=True)
            inc(ins, "TT")
        blk.tensor(_)
        tt_dc = cnt["TT"]

        def _(e):
            w(e, "TT", tt_dc)
            inc(e.activation(dcolsb[:], pfm2[:, 0:64], AF.Exp, bias=0.0,
                             scale=1.0), "CC")
        blk.scalar(_)
        cc_dcol = cnt["CC"]

        def _(e):
            sdk(e, "s2s")
            w(e, "CC", cc_dcol)
            w(e, "VV", vv_d2)
            for gi in range(4):
                inc(e.dma_start(out=sp4[0:4, 16 * gi:16 * (gi + 1)],
                                in_=e16[32 + 4 * gi:32 + 4 * gi + 4, 48:64]),
                    "SD", 16)
        blk.sync(_)
        sd_pe = cnt["SD"]

        def _(e):
            w(e, "SD", sd_pe)
            for gi in range(4):
                ins = e.matmul(pfm2[:, gi * 16:(gi + 1) * 16],
                               grpb[0:4, :],
                               sp4[0:4, 16 * gi:16 * (gi + 1)],
                               start=True, stop=True)
            inc(ins, "TT")
        blk.tensor(_)
        tt_pc = cnt["TT"]

        def _(e):
            w(e, "TT", tt_pc)
            inc(e.activation(pcolsb[:], pfm2[:, 0:64], AF.Exp, bias=0.0,
                             scale=1.0), "CC")
        blk.scalar(_)
        cc_pcol = cnt["CC"]

        tt_projtm = []
        sd_xpfm = []
        for t in range(NT):
            def _(e, t=t):
                w(e, "CC", cc_xn_t[t])
                if t == 0:
                    w(e, "GD", gd_w)
                    w(e, "CC", cc_pcol)
                    w(e, "TT", fm_tt[-1])
                    w(e, "SD", sd_ftm)
                else:
                    w(e, "CC")
                for k in range(8):
                    st = fmb[:, k * TOK + t * 128:k * TOK + (t + 1) * 128]
                    for q in range(2):
                        e.matmul(pfm[:, q * 512:(q + 1) * 512], st,
                                 w_in[:, k * PROJ + q * 512:
                                      k * PROJ + (q + 1) * 512],
                                 start=(k == 0), stop=(k == 7))
                    for q in range(2):
                        ins = e.matmul(
                            pfm2[:, q * 512:(q + 1) * 512], st,
                            w_in[:, k * PROJ + 1024 + q * 512:
                                 k * PROJ + 1024 + (q + 1) * 512],
                            start=(k == 0), stop=(k == 7))
                inc(ins, "TT")
            blk.tensor(_)
            tt_projtm.append(cnt["TT"])
            tt_p = cnt["TT"]

            def _(e, t=t, tt_p=tt_p):
                w(e, "TT", tt_p)
                e.copy(xptm[:, t * D:(t + 1) * D], pfm[:])
                inc(e.activation(ztm[:, t * D:(t + 1) * D], pfm2[:], AF.Silu,
                                 bias=0.0, scale=1.0), "CC")
            blk.scalar(_)
            cc_ev = cnt["CC"]

            def _(e, t=t, cc_ev=cc_ev):
                sdk(e, "tr")
                w(e, "CC", cc_ev)
                if t >= 2:
                    w(e, "TT")
                for r in range(2):
                    inc(e.dma_start(
                        out=btm[:, (t % 2) * 256 + r * 128:
                                (t % 2) * 256 + (r + 1) * 128],
                        in_=cwb[:, r * 2048 + t * 128:
                                r * 2048 + (t + 1) * 128],
                        transpose=True), "SD", 16)
            blk.sync(_)
            sd_btm = cnt["SD"]

            def _(e, t=t):
                sdk(e, "tr")
                for k in range(8):
                    inc(e.dma_start(
                        out=fmb[:, k * TOK + t * 128:k * TOK + (t + 1) * 128],
                        in_=xptm[:, t * D + k * 128:t * D + (k + 1) * 128],
                        transpose=True), "SD", 16)
            blk.sync(_)
            sd_xpfm.append(cnt["SD"])

            def _(e, t=t, cc_ev=cc_ev):
                w(e, "CC", cc_ev)
                w(e, "SD", sd_ftm)
                fv = ftm[:, t * 16:(t + 1) * 16].unsqueeze(2).broadcast_to(
                    [128, 16, HD])
                xv = xptm[:, t * D:(t + 1) * D].rearrange(
                    "p (h d) -> p h d", h=16)
                inc(e.tensor_tensor(wxb[:].rearrange("p (h d) -> p h d",
                                                     h=16),
                                    xv, fv, ALU.mult), "VV")
            blk.vector(_)
            vv_wx = cnt["VV"]

            def _(e, t=t, vv_wx=vv_wx, sd_btm=sd_btm):
                w(e, "VV", vv_wx)
                w(e, "SD", sd_btm)
                if t == 0:
                    w(e, "SD", sd_f2)
                    w(e, "VV", vv_init)
                else:
                    w(e, "VV")
                for h in range(H):
                    gi, m = h // 4, h % 4
                    ins = e.matmul(
                        pbig[32 * m:32 * m + 16, gi * 512:gi * 512 + 64],
                        btm[:, (t % 2) * 256 + (h // 8) * 128 + 16 * (h % 8):
                            (t % 2) * 256 + (h // 8) * 128 + 16 * (h % 8)
                            + 16],
                        wxb[:, h * HD:(h + 1) * HD],
                        start=True, stop=True,
                        tile_position=(0, 32 * m))
                inc(ins, "TT")
            blk.tensor(_)
            tt_cs = cnt["TT"]

            def _(e, t=t, tt_cs=tt_cs):
                w(e, "TT", tt_cs)
                if t == 0:
                    w(e, "CC", cc_dcol)
                for gi in range(4):
                    e.tensor_copy(
                        snaps[:, (t * 4 + gi) * HD:(t * 4 + gi + 1) * HD],
                        srun[:, gi * HD:(gi + 1) * HD])
                    ins = e.scalar_tensor_tensor(
                        srun[:, gi * HD:(gi + 1) * HD],
                        srun[:, gi * HD:(gi + 1) * HD],
                        dcolsb[:, gi * 16 + t:gi * 16 + t + 1],
                        pbig[:, gi * 512:gi * 512 + 64],
                        ALU.mult, ALU.add)
                inc(ins, "VV")
            blk.vector(_)

        cc_evE = cnt["CC"]
        zfm3 = zfm[:, 0:8 * TOK].rearrange("p (k c) -> p k c", k=8)
        for t in range(NT):
            pz = pfm if t % 2 == 0 else pbig

            def _(e, t=t, pz=pz):
                if t == 0:
                    w(e, "CC", cc_evE)
                    w(e, "GD", gd_w)
                elif t >= 2:
                    w(e, "CC")
                for k in range(8):
                    ins = e.matmul(pz[:, k * 128:(k + 1) * 128],
                                   ztm[:, t * D + k * 128:t * D + (k + 1) * 128],
                                   identb[:], start=True, stop=True)
                inc(ins, "TT")
            blk.tensor(_)
            tt_zt = cnt["TT"]

            def _(e, t=t, pz=pz, tt_zt=tt_zt):
                w(e, "TT", tt_zt)
                inc(e.copy(zfm3[:, :, t * 128:(t + 1) * 128],
                           pz[:, 0:1024].rearrange("p (k c) -> p k c", k=8)),
                    "CC")
            blk.scalar(_)
        cc_zfm = cnt["CC"]

        def _(e):
            w(e, "VV")
            w(e, "CC", cc_d3)
            inc(e.dma_start(out=ccA_in.ap(), in_=srun[:]), "GD", 16)
            inc(e.dma_start(out=ccd_in.ap(), in_=dcoreb[:]), "GD", 16)
            w(e, "GD")
            inc(e.collective_compute(
                "AllGather", ALU.bypass,
                replica_groups=[list(range(NCORE))],
                ins=[ccA_in.ap().opt()], outs=[ccA_out.ap().opt()]), "CS")
            inc(e.collective_compute(
                "AllGather", ALU.bypass,
                replica_groups=[list(range(NCORE))],
                ins=[ccd_in.ap().opt()], outs=[ccd_out.ap().opt()]), "CT")
            e.wait_ge(sems["CS"], 1)
            e.wait_ge(sems["CT"], 1)
            inc(e.dma_start(out=ddall[:], in_=ccd_out.ap()[:, 0:1],
                            allow_slow_non_contiguous=True), "GD", 16)
        blk.gpsimd(_)
        gd_cc = cnt["GD"]

        def _(e):
            sdk(e, "s2s")
            w(e, "GD", gd_cc)
            for gi in range(4):
                for r in range(NCORE):
                    inc(e.dma_start(
                        out=sp4[0:4, 8 * gi + r:8 * gi + r + 1],
                        in_=ddall[16 * r + 4 * gi:16 * r + 4 * gi + 4, :],
                        allow_slow_non_contiguous=True), "SD", 16)
        blk.sync(_)
        sd_dd = cnt["SD"]

        def _(e):
            w(e, "SD", sd_dd)
            w(e, "CC")
            for gi in range(4):
                ins = e.matmul(pfm2[:, gi * 8:(gi + 1) * 8],
                               grpb[0:4, :],
                               sp4[0:4, 8 * gi:8 * (gi + 1)],
                               start=True, stop=True)
            inc(ins, "TT")
        blk.tensor(_)
        tt_dr = cnt["TT"]

        def _(e):
            w(e, "TT", tt_dr)
            for gi in range(4):
                ins = e.tensor_copy(drcols[:, gi * 8:(gi + 1) * 8],
                                    pfm2[:, gi * 8:(gi + 1) * 8])
            inc(ins, "VV")
        blk.vector(_)
        vv_dr = cnt["VV"]

        def _(e):
            w(e, "VV", vv_dr)
            w(e, "SD", sd_w)
            dv3 = dvb[:].rearrange("p (g r) -> p g r", g=4)
            dr3 = drcols[:].rearrange("p (g r) -> p g r", g=4)
            for r in range(NCORE):
                ins = e.scalar_tensor_tensor(
                    dv3[:, :, r], dr3[:, :, r], seldb[:, r:r + 1],
                    invsdb[:, r:r + 1].broadcast_to([128, 4]),
                    ALU.mult, ALU.add)
            inc(ins, "VV")
        blk.vector(_)
        vv_dv = cnt["VV"]

        for r in range(NCORE):
            def _(e, r=r):
                w(e, "GD", gd_cc)
                if r >= 1:
                    w(e, "VV")
                inc(e.dma_start(
                    out=avb[:],
                    in_=ccA_out.ap()[r * 128:(r + 1) * 128, :]), "GD", 16)
            blk.gpsimd(_)
            gd_av = cnt["GD"]

            def _(e, r=r, gd_av=gd_av):
                w(e, "GD", gd_av)
                if r == 0:
                    w(e, "VV", vv_dv)
                for gi in range(4):
                    e.tensor_scalar(avb[:, gi * HD:(gi + 1) * HD],
                                    avb[:, gi * HD:(gi + 1) * HD],
                                    selAb[:, r:r + 1], None, ALU.mult)
                for gi in range(4):
                    ins = e.scalar_tensor_tensor(
                        hbuf[:, gi * HD:(gi + 1) * HD],
                        hbuf[:, gi * HD:(gi + 1) * HD],
                        dvb[:, gi * 8 + r:gi * 8 + r + 1],
                        avb[:, gi * HD:(gi + 1) * HD],
                        ALU.mult, ALU.add)
                inc(ins, "VV")
            blk.vector(_)
        vv_h = cnt["VV"]

        def _(e):
            w(e, "VV", vv_h)
            w(e, "CC", cc_pcol)
            for t in range(NT):
                for gi in range(4):
                    e.tensor_scalar(avb[:, gi * HD:(gi + 1) * HD],
                                    hbuf[:, gi * HD:(gi + 1) * HD],
                                    pcolsb[:, gi * 16 + t:gi * 16 + t + 1],
                                    None, ALU.mult)
                for gi in range(4):
                    ins = e.tensor_add(
                        snaps[:, (t * 4 + gi) * HD:(t * 4 + gi + 1) * HD],
                        snaps[:, (t * 4 + gi) * HD:(t * 4 + gi + 1) * HD],
                        avb[:, gi * HD:(gi + 1) * HD])
            inc(ins, "VV")
        blk.vector(_)
        vv_snapc = cnt["VV"]

        def _(e):
            w(e, "TT", tt_projtm[-1])
            inc(e.memset(bpadB[:], 0.0), "VV")
        blk.vector(_)
        vv_bpB = cnt["VV"]

        def _(e):
            sdk(e, "s2s")
            w(e, "VV", vv_bpB)
            for h in range(1, H, 2):
                gp = h // 2
                dst_t = gp // 4
                base = 32 * (gp % 4) + 16
                inc(e.dma_start(
                    out=bpadB[base:base + 16, dst_t * TOK:(dst_t + 1) * TOK],
                    in_=cwb[16 * (h % 8):16 * (h % 8) + 16,
                            (h // 8) * 2048:(h // 8 + 1) * 2048]), "SD", 16)
        blk.sync(_)
        sd_bpB = cnt["SD"]

        def _(e):
            w(e, "SD", sd_f2)
            w(e, "SD", sd_bpB)
            w(e, "TT", tt_xn_t[NT - 1])
            w(e, "CC")
            w(e, "VV", fm_vv)
            w(e, "VV")
            inc(e.memset(cwb[:], 0.0), "VV")
        blk.vector(_)
        vv_cm = cnt["VV"]

        def _(e):
            sdk(e, "s2s")
            w(e, "VV", vv_cm)
            w(e, "CC", fm_cc)
            for h in range(H):
                inc(e.dma_start(
                    out=cwb[32 * (h % 4):32 * (h % 4) + 16,
                            (h // 4) * 2048:(h // 4 + 1) * 2048],
                    in_=cfm[16 * (h % 8):16 * (h % 8) + 16,
                            (h // 8) * 2048:(h // 8 + 1) * 2048]), "SD", 16)
            w(e, "CC")
            for gi in range(4):
                inc(e.dma_start(out=scr32[32 * gi:32 * gi + 4, :],
                                in_=packA[32 + 4 * gi:32 + 4 * gi + 4, :]),
                    "SD", 16)
        blk.sync(_)
        sd_cp = cnt["SD"]

        for gi in range(4):
            def _(e, gi=gi):
                w(e, "SD", sd_cp)
                w(e, "CC", cc_zfm)
                if gi == 0:
                    w(e, "VV", vv_dr)
                    w(e, "CC")
                else:
                    w(e, "CC")
                for q in range(4):
                    pp = pfm2 if q < 2 else pfm
                    ins = e.matmul(pp[:, (q % 2) * 512:(q % 2 + 1) * 512],
                                   grpb[32 * gi:32 * gi + 4, :],
                                   scr32[32 * gi:32 * gi + 4,
                                         q * 512:(q + 1) * 512],
                                   start=True, stop=True,
                                   tile_position=(32 * gi, 0))
                inc(ins, "TT")
            blk.tensor(_)
            tt_ea = cnt["TT"]

            def _(e, gi=gi, tt_ea=tt_ea):
                w(e, "TT", tt_ea)
                e.activation(eag[:, 0:512], pfm2[:, 0:512], AF.Exp,
                             bias=0.0, scale=1.0)
                e.activation(eag[:, 512:1024], pfm2[:, 512:1024], AF.Exp,
                             bias=0.0, scale=1.0)
                e.activation(eag[:, 1024:1536], pfm[:, 0:512], AF.Exp,
                             bias=0.0, scale=1.0)
                inc(e.activation(eag[:, 1536:2048], pfm[:, 512:1024], AF.Exp,
                                 bias=0.0, scale=1.0), "CC")
            blk.scalar(_)
            cc_ea = cnt["CC"]

            def _(e, gi=gi, cc_ea=cc_ea):
                w(e, "CC", cc_ea)
                w(e, "SD", sd_cp)
                inc(e.tensor_tensor(cwb[:, gi * 2048:(gi + 1) * 2048],
                                    cwb[:, gi * 2048:(gi + 1) * 2048],
                                    eag[:], ALU.mult), "VV")
            blk.vector(_)
        vv_cpr = cnt["VV"]

        tt_lastA = [0]
        for h in range(H):
            pair, odd = h // 2, h % 2
            scob = scobO if odd else scobE

            def _(e, h=h):
                sdk(e, "s2s")
                if h >= 1:
                    w(e, "TT", tt_lastA[0])
                else:
                    w(e, "VV")
                inc(e.dma_start(out=lhsA[1:17, :],
                                in_=packA[64 + h:65 + h, :]), "SD", 16)
                inc(e.dma_start(out=rhsA[0:1, :],
                                in_=packA[32 + h:33 + h, :]), "SD", 16)
            blk.sync(_)
            sd_ab = cnt["SD"]

            for q in range(4):
                def _(e, h=h, q=q, sd_ab=sd_ab):
                    w(e, "SD", sd_ab)
                    if h == 0 and q == 0:
                        w(e, "CC", cc_zfm)
                    w(e, "VV")
                    inc(e.matmul(pfm[:, (q % 2) * 512:(q % 2 + 1) * 512],
                                 lhsA[0:17, :],
                                 rhsA[0:17, q * 512:(q + 1) * 512],
                                 start=True, stop=True), "TT")
                blk.tensor(_)
                tt_a = cnt["TT"]
                if q == 3:
                    tt_lastA[0] = tt_a

                def _(e, h=h, q=q, tt_a=tt_a):
                    w(e, "TT", tt_a)
                    if h == 0 and q == 0:
                        w(e, "GD", gd_w)
                    m0v = m0b[:].unsqueeze(1).broadcast_to([128, 4, 128])
                    inc(e.tensor_tensor(
                        scr32[:, q * 512:(q + 1) * 512].rearrange(
                            "p (a b) -> p a b", a=4),
                        pfm[:, (q % 2) * 512:(q % 2 + 1) * 512].rearrange(
                            "p (a b) -> p a b", a=4),
                        m0v, ALU.add), "VV")
                blk.vector(_)
            vv_arg = cnt["VV"]

            def _(e, h=h, vv_arg=vv_arg):
                w(e, "VV", vv_arg)
                e.activation(expb[:, 0:1024], scr32[:, 0:1024], AF.Exp,
                             bias=0.0, scale=1.0)
                inc(e.activation(expb[:, 1024:2048], scr32[:, 1024:2048],
                                 AF.Exp, bias=0.0, scale=1.0), "CC")
            blk.scalar(_)
            cc_exp = cnt["CC"]

            def _(e, h=h):
                if h == 0:
                    w(e, "SD", sd_cp)
                    w(e, "VV", vv_cpr)
                w(e, "VV")
                gp = h // 2
                base = 32 * (gp % 4)
                dst_t = gp // 4
                bp = bpad if h % 2 == 0 else bpadB
                for T in range(NT):
                    ins = e.matmul(
                        pbig[:, T * 128:(T + 1) * 128],
                        bp[base:base + 32,
                           dst_t * TOK + T * 128:dst_t * TOK +
                           (T + 1) * 128],
                        cfm[base:base + 32,
                            dst_t * TOK + T * 128:dst_t * TOK +
                            (T + 1) * 128],
                        start=True, stop=True, tile_position=(base, 0))
                inc(ins, "TT")
            blk.tensor(_)
            tt_raw = cnt["TT"]

            def _(e, h=h, tt_raw=tt_raw, cc_exp=cc_exp, scob=scob):
                w(e, "TT", tt_raw)
                w(e, "CC", cc_exp)
                e.tensor_tensor(scob[:, 0:1024], pbig[:, 0:1024],
                                expb[:, 0:1024], ALU.mult)
                inc(e.tensor_tensor(scob[:, 1024:2048], pbig[:, 1024:2048],
                                    expb[:, 1024:2048], ALU.mult), "VV")
            blk.vector(_)
            vv_sco = cnt["VV"]

            if _DEBUG_DUMP and h == 0:
                def _(e, vv_sco=vv_sco):
                    w(e, "VV", vv_sco)
                    inc(e.dma_start(out=dbgS.ap(), in_=scobE[:]), "SD", 16)
                    inc(e.dma_start(out=dbgE.ap(), in_=expb[:]), "SD", 16)
                blk.sync(_)
                def _(e, vv_sco=vv_sco):
                    w(e, "VV", vv_sco)
                    inc(e.tensor_copy(scr32[:], pbig[:]), "VV")
                blk.vector(_)
                vv_dbg = cnt["VV"]
                def _(e, vv_dbg=vv_dbg):
                    w(e, "VV", vv_dbg)
                    inc(e.dma_start(out=dbgR.ap(), in_=scr32[:]), "SD", 16)
                blk.sync(_)

            if odd:
                def _(e, pair=pair, vv_sco=vv_sco):
                    w(e, "VV", vv_sco)
                    w(e, "VV", vv_snapc)
                    for hh in (2 * pair, 2 * pair + 1):
                        par = hh % 2
                        sc = scobO if par else scobE
                        base = 32 * (hh % 4)
                        gtile = hh // 4
                        for T in range(NT):
                            ins = e.matmul(
                                pbig[64 * par:64 * par + 64,
                                     T * 128:(T + 1) * 128],
                                xptm[:, T * D + hh * HD:
                                     T * D + (hh + 1) * HD],
                                sc[:, T * 128:(T + 1) * 128],
                                start=(T % 4 == 0), stop=False,
                                tile_position=(0, 64 * par))
                        inc(ins, "TT")
                        w(e, "TT")
                        for T in range(NT):
                            ins = e.matmul(
                                pbig[64 * par:64 * par + 64,
                                     T * 128:(T + 1) * 128],
                                snaps[base:base + 32,
                                      (T * 4 + gtile) * HD:
                                      (T * 4 + gtile + 1) * HD],
                                cwb[base:base + 32,
                                    gtile * 2048 + T * 128:
                                    gtile * 2048 + (T + 1) * 128],
                                start=False, stop=(T % 4 == 3),
                                tile_position=(32 * (hh % 4), 64 * par))
                        inc(ins, "TT")
                        w(e, "TT")
                blk.tensor(_)
                tt_y = cnt["TT"]

                def _(e, pair=pair, tt_y=tt_y):
                    w(e, "TT", tt_y)
                    if pair == 0:
                        w(e, "SD", sd_xpfm[-1])
                    ins = e.scalar_tensor_tensor(
                        fmb[:, pair * TOK:(pair + 1) * TOK],
                        fmb[:, pair * TOK:(pair + 1) * TOK],
                        dcolb[:, pair:pair + 1], pbig[:],
                        ALU.mult, ALU.add)
                    inc(ins, "VV")
                blk.vector(_)
        vv_y = cnt["VV"]

        if _DEBUG_DUMP:
            def _(e):
                w(e, "VV")
                inc(e.dma_start(out=dbgY.ap(), in_=fmb[:]), "SD", 16)
            blk.sync(_)

        for gi in range(8):
            def _(e, gi=gi):
                if gi == 0:
                    w(e, "VV")
                    w(e, "CC", cc_zfm)
                inc(e.scalar_tensor_tensor(
                    fmb[:, gi * TOK:(gi + 1) * TOK],
                    fmb[:, gi * TOK:(gi + 1) * TOK],
                    nwcb[:, gi:gi + 1],
                    zfm[:, gi * TOK:(gi + 1) * TOK], ALU.mult, ALU.mult),
                    "VV")
            blk.vector(_)
        vv_gate = cnt["VV"]

        def _(e):
            w(e, "TT")
            w(e, "CC", cc_zfm)
            for k in range(8):
                inc(e.dma_start(out=cwb[:, k * D:(k + 1) * D],
                                in_=dwout[k * 128:(k + 1) * 128, :]),
                    "GD", 16)
            inc(e.dma_start(out=gb[:], in_=dgbc), "GD", 16)
            inc(e.dma_start(out=bb[:], in_=dbbc), "GD", 16)
        blk.gpsimd(_)
        gd_wo = cnt["GD"]

        xrot2 = ztm[:, 6 * TOK:7 * TOK].bitcast(F32)
        xrots = [xrotT, xrot2]
        mobs = [postbuf[:, 0:D], scr32[:, 0:D]]
        x1bs = [postbuf[:, D:TOK], scr32[:, D:TOK]]
        tAs = [ztm[:, 0:D], ztm[:, 2 * D:3 * D]]
        vbs = [ztm[:, D:2 * D], ztm[:, 3 * D:4 * D]]
        txs = [packA[:, 0:D], packA[:, D:2 * D]]
        CB = [0, 16]

        cd_t = {}
        tt_mo_t = {}
        s0_vv = {}
        od_t = {}
        for p in range(NT // 2):
            pair = (2 * p, 2 * p + 1)
            for t in pair:
                def _(e, t=t, p=p):
                    if p == 0:
                        w(e, "CC", cc_zfm)
                    else:
                        w(e, "VV")
                    inc(e.dma_start(out=xrots[t % 2][:],
                                    in_=dx[t * 128:(t + 1) * 128, :]),
                        "CD", 16)
                blk.scalar(_)
                cd_t[t] = cnt["CD"]

            for t in pair:
                pmo = pfm if t % 2 == 0 else pfm2

                def _(e, t=t, pmo=pmo):
                    w(e, "GD", gd_wo)
                    w(e, "VV", vv_gate)
                    if t >= 2:
                        w(e, "VV", s0_vv[t - 2])
                    for k in range(8):
                        st = fmb[:, k * TOK + t * 128:k * TOK + (t + 1) * 128]
                        for q in range(2):
                            ins = e.matmul(pmo[:, q * 512:(q + 1) * 512], st,
                                           cwb[:, k * D + q * 512:
                                               k * D + (q + 1) * 512],
                                           start=(k == 0), stop=(k == 7))
                    inc(ins, "TT")
                blk.tensor(_)
                tt_mo_t[t] = cnt["TT"]

            for t in pair:
                pmo = pfm if t % 2 == 0 else pfm2

                def _(e, t=t, pmo=pmo):
                    w(e, "TT", tt_mo_t[t])
                    B = CB[t % 2]
                    inc(e.tensor_scalar(mobs[t % 2][:], pmo[:], 1.0, 0.0,
                                        ALU.mult, ALU.add,
                                        accum_out=colc[:, B:B + 1]), "VV")
                blk.vector(_)
                s0_vv[t] = cnt["VV"]

            def _(e, p=p):
                for t in pair:
                    B = CB[t % 2]
                    e.scalar_tensor_tensor(
                        tAs[t % 2][:], mobs[t % 2][:], 1.0, mobs[t % 2][:],
                        ALU.mult, ALU.mult,
                        accum_out=colc[:, B + 1:B + 2])
                for t in pair:
                    B = CB[t % 2]
                    e.tensor_scalar(colc[:, B + 2:B + 3],
                                    colc[:, B:B + 1], 1.0 / D, None, ALU.mult)
                    e.tensor_scalar(colc[:, B + 3:B + 4],
                                    colc[:, B + 1:B + 2], 1.0 / D, None,
                                    ALU.mult)
                for t in pair:
                    B = CB[t % 2]
                    ins = e.scalar_tensor_tensor(
                        colc[:, B + 4:B + 5], colc[:, B + 2:B + 3],
                        colc[:, B + 2:B + 3], colc[:, B + 3:B + 4],
                        ALU.mult, ALU.subtract)
                inc(ins, "VV")
            blk.vector(_)
            vv_st1 = cnt["VV"]

            def _(e, p=p, vv_st1=vv_st1):
                w(e, "VV", vv_st1)
                for t in pair:
                    B = CB[t % 2]
                    e.activation(colc[:, B + 5:B + 6], colc[:, B + 4:B + 5],
                                 AF.Ln, bias=epsc[:, 0:1], scale=-1.0)
                    ins = e.activation(colc[:, B + 6:B + 7],
                                       colc[:, B + 5:B + 6], AF.Exp,
                                       bias=0.0, scale=-0.5)
                inc(ins, "CC")
            blk.scalar(_)
            cc_r2 = cnt["CC"]

            def _(e, p=p, cc_r2=cc_r2):
                w(e, "CC", cc_r2)
                w(e, "CD", cd_t[pair[1]])
                if p >= 1:
                    w(e, "OD", od_t[pair[0] - 2])
                    w(e, "OD", od_t[pair[1] - 2])
                for t in pair:
                    B = CB[t % 2]
                    e.tensor_scalar(colc[:, B + 7:B + 8],
                                    colc[:, B + 2:B + 3],
                                    colc[:, B + 6:B + 7], -1.0, ALU.mult,
                                    ALU.mult)
                for t in pair:
                    B = CB[t % 2]
                    e.tensor_scalar(txs[t % 2][:], xrots[t % 2][:],
                                    colc[:, B + 7:B + 8], None, ALU.add)
                for t in pair:
                    B = CB[t % 2]
                    e.scalar_tensor_tensor(
                        x1bs[t % 2][:], mobs[t % 2][:],
                        colc[:, B + 6:B + 7], txs[t % 2][:], ALU.mult,
                        ALU.add, accum_out=colc[:, B:B + 1])
                for t in pair:
                    B = CB[t % 2]
                    e.scalar_tensor_tensor(
                        tAs[t % 2][:], x1bs[t % 2][:], 1.0, x1bs[t % 2][:],
                        ALU.mult, ALU.mult,
                        accum_out=colc[:, B + 1:B + 2])
                for t in pair:
                    B = CB[t % 2]
                    e.tensor_scalar(colc[:, B + 2:B + 3],
                                    colc[:, B:B + 1], 1.0 / D, None, ALU.mult)
                    e.tensor_scalar(colc[:, B + 3:B + 4],
                                    colc[:, B + 1:B + 2], 1.0 / D, None,
                                    ALU.mult)
                for t in pair:
                    B = CB[t % 2]
                    ins = e.scalar_tensor_tensor(
                        colc[:, B + 4:B + 5], colc[:, B + 2:B + 3],
                        colc[:, B + 2:B + 3], colc[:, B + 3:B + 4],
                        ALU.mult, ALU.subtract)
                inc(ins, "VV")
            blk.vector(_)
            vv_st2 = cnt["VV"]

            def _(e, p=p, vv_st2=vv_st2):
                w(e, "VV", vv_st2)
                for t in pair:
                    B = CB[t % 2]
                    e.activation(colc[:, B + 5:B + 6], colc[:, B + 4:B + 5],
                                 AF.Ln, bias=epsc[:, 0:1], scale=-1.0)
                    e.activation(colc[:, B + 6:B + 7], colc[:, B + 5:B + 6],
                                 AF.Exp, bias=0.0, scale=-0.5)
                    ins = e.activation(colc[:, B + 8:B + 9],
                                       colc[:, B + 5:B + 6], AF.Exp,
                                       bias=0.0, scale=-1.0)
                inc(ins, "CC")
            blk.scalar(_)
            cc_r3 = cnt["CC"]

            def _(e, p=p, cc_r3=cc_r3):
                w(e, "CC", cc_r3)
                for t in pair:
                    B = CB[t % 2]
                    e.tensor_scalar(colc[:, B + 7:B + 8],
                                    colc[:, B + 2:B + 3],
                                    colc[:, B + 6:B + 7], -1.0, ALU.mult,
                                    ALU.mult)
                for t in pair:
                    B = CB[t % 2]
                    ins = e.tensor_scalar(colc[:, B + 9:B + 10],
                                          colc[:, B + 4:B + 5],
                                          colc[:, B + 8:B + 9], -1.0,
                                          ALU.mult, ALU.mult)
                inc(ins, "VV")
            blk.vector(_)
            vv_vu = cnt["VV"]

            def _(e, p=p, vv_vu=vv_vu):
                w(e, "VV", vv_vu)
                for t in pair:
                    B = CB[t % 2]
                    e.activation(colc[:, B + 10:B + 11],
                                 colc[:, B + 9:B + 10], AF.Ln,
                                 bias=epsc[:, 0:1], scale=1.0)
                    ins = e.activation(colc[:, B + 11:B + 12],
                                       colc[:, B + 10:B + 11], AF.Exp,
                                       bias=0.0, scale=-0.5)
                inc(ins, "CC")
            blk.scalar(_)
            cc_rs = cnt["CC"]

            def _(e, p=p, cc_rs=cc_rs):
                w(e, "CC", cc_rs)
                w(e, "GD", gd_wo)
                for t in pair:
                    B = CB[t % 2]
                    e.tensor_scalar(colc[:, B + 13:B + 14],
                                    colc[:, B + 7:B + 8],
                                    colc[:, B + 11:B + 12], None, ALU.mult)
                for t in pair:
                    B = CB[t % 2]
                    e.tensor_scalar(colc[:, B + 12:B + 13],
                                    colc[:, B + 6:B + 7],
                                    colc[:, B + 11:B + 12], None, ALU.mult)
                for t in pair:
                    B = CB[t % 2]
                    e.scalar_tensor_tensor(
                        tAs[t % 2][:], x1bs[t % 2][:],
                        colc[:, B + 12:B + 13],
                        colc[:, B + 13:B + 14].broadcast_to([128, D]),
                        ALU.mult, ALU.add)
                for t in pair:
                    e.tensor_tensor(vbs[t % 2][:], tAs[t % 2][:], gb[:],
                                    ALU.mult)
                for t in pair:
                    B = CB[t % 2]
                    e.scalar_tensor_tensor(
                        vbs[t % 2][:], vbs[t % 2][:], 1.0, bb[:], ALU.mult,
                        ALU.add, accum_out=colc[:, B:B + 1])
                for t in pair:
                    B = CB[t % 2]
                    e.scalar_tensor_tensor(
                        tAs[t % 2][:], vbs[t % 2][:], 1.0, vbs[t % 2][:],
                        ALU.mult, ALU.mult,
                        accum_out=colc[:, B + 1:B + 2])
                for t in pair:
                    B = CB[t % 2]
                    e.tensor_scalar(colc[:, B + 2:B + 3],
                                    colc[:, B:B + 1], 1.0 / D, None, ALU.mult)
                    e.tensor_scalar(colc[:, B + 3:B + 4],
                                    colc[:, B + 1:B + 2], 1.0 / D, None,
                                    ALU.mult)
                for t in pair:
                    B = CB[t % 2]
                    ins = e.scalar_tensor_tensor(
                        colc[:, B + 4:B + 5], colc[:, B + 2:B + 3],
                        colc[:, B + 2:B + 3], colc[:, B + 3:B + 4],
                        ALU.mult, ALU.subtract)
                inc(ins, "VV")
            blk.vector(_)
            vv_st3 = cnt["VV"]

            def _(e, p=p, vv_st3=vv_st3):
                w(e, "VV", vv_st3)
                for t in pair:
                    B = CB[t % 2]
                    e.activation(colc[:, B + 5:B + 6], colc[:, B + 4:B + 5],
                                 AF.Ln, bias=epsc[:, 0:1], scale=-1.0)
                    ins = e.activation(colc[:, B + 6:B + 7],
                                       colc[:, B + 5:B + 6], AF.Exp,
                                       bias=0.0, scale=-0.5)
                inc(ins, "CC")
            blk.scalar(_)
            cc_r4 = cnt["CC"]

            def _(e, p=p, cc_r4=cc_r4):
                w(e, "CC", cc_r4)
                for t in pair:
                    B = CB[t % 2]
                    e.tensor_scalar(colc[:, B + 7:B + 8],
                                    colc[:, B + 2:B + 3],
                                    colc[:, B + 6:B + 7], -1.0, ALU.mult,
                                    ALU.mult)
                for t in pair:
                    B = CB[t % 2]
                    e.scalar_tensor_tensor(
                        txs[t % 2][:], vbs[t % 2][:], colc[:, B + 6:B + 7],
                        colc[:, B + 7:B + 8].broadcast_to([128, D]),
                        ALU.mult, ALU.add)
                for t in pair:
                    inc(e.tensor_tensor(x1bs[t % 2][:], x1bs[t % 2][:],
                                        txs[t % 2][:], ALU.add), "VV")
            blk.vector(_)
            vv_o1 = cnt["VV"] - 1
            vv_o2 = cnt["VV"]

            for t, vvn in zip(pair, (vv_o1, vv_o2)):
                def _(e, t=t, vvn=vvn):
                    sdk(e, "hbm")
                    w(e, "VV", vvn)
                    inc(e.dma_start(out=dout[t * 128:(t + 1) * 128, :],
                                    in_=x1bs[t % 2][:]), "OD", 16)
                blk.sync(_)
                od_t[t] = cnt["OD"]

        def _(e):
            w(e, "OD")
        blk.sync(_)

    return nc


_CACHE = {}


def _host_prep(inputs):
    x = np.asarray(inputs["L0_attn_x"], dtype=np.float32).reshape(16384, D)
    w_inT = np.ascontiguousarray(
        np.asarray(inputs["L0_attn_W_in"], dtype=np.float32).T)
    w_outT = np.ascontiguousarray(
        np.asarray(inputs["L0_attn_W_out"], dtype=np.float32).T)
    dt_WT = np.ascontiguousarray(
        np.asarray(inputs["L0_attn_dt_W"], dtype=np.float32).T)
    dt_b = np.asarray(inputs["L0_attn_dt_b"],
                      dtype=np.float32).reshape(16, 1)
    A_log = np.asarray(inputs["L0_attn_A_log"],
                       dtype=np.float32).reshape(16, 1)
    Dp = np.asarray(inputs["L0_attn_D_param"], dtype=np.float32)
    nw = np.asarray(inputs["L0_attn_norm_w"], dtype=np.float32)
    ln_g = np.asarray(inputs["L0_ffn_ln_g"], dtype=np.float32)
    ln_b = np.asarray(inputs["L0_ffn_ln_b"], dtype=np.float32)

    Dcols = np.empty((128, 8), np.float32)
    for g in range(8):
        Dcols[0:64, g] = Dp[2 * g]
        Dcols[64:128, g] = Dp[2 * g + 1]
    nwcols = nw.reshape(8, 128).T.copy()
    gbc = np.broadcast_to(ln_g, (128, D)).copy()
    bbc = np.broadcast_to(ln_b, (128, D)).copy()

    resetm = np.ones((16, TOK), np.float32)
    resetm[:, ::CH] = 0.0
    jj, ii = np.meshgrid(np.arange(128), np.arange(128), indexing="ij")
    m0 = np.where(ii >= jj, 0.0, NEG).astype(np.float32)
    tileind = np.zeros((16, TOK), np.float32)
    for t in range(16):
        tileind[t, t * 128:(t + 1) * 128] = 1.0
    ident = np.eye(128, dtype=np.float32)
    grp4rep = np.zeros((128, 128), np.float32)
    for g in range(4):
        for m in range(4):
            grp4rep[32 * g + m, 32 * m:32 * (m + 1)] = 1.0

    common = dict(w_inT=w_inT, w_outT=w_outT, dt_WT=dt_WT, dt_b=dt_b,
                  A_log=A_log, Dcols=Dcols, nwcols=nwcols, gbc=gbc, bbc=bbc,
                  resetm=resetm, m0=m0, tileind=tileind, grp4rep=grp4rep,
                  ident=ident)
    in_maps = []
    for c in range(NCORE):
        selA = np.zeros((128, 8), np.float32)
        selA[:, :c] = 1.0
        seld = np.zeros((128, 8), np.float32)
        seld[:, :c] = 1.0
        invseld = 1.0 - seld
        m = dict(common)
        m["x"] = np.ascontiguousarray(x[c * TOK:(c + 1) * TOK])
        m["selA"] = selA
        m["seld"] = seld
        m["invseld"] = invseld
        in_maps.append(m)
    return in_maps


def kernel(**inputs):
    if "nc" not in _CACHE:
        _CACHE["nc"] = build_nc()
    nc = _CACHE["nc"]
    in_maps = _host_prep(inputs)
    res = run_bass_kernel_spmd(nc, in_maps, core_ids=list(range(NCORE)))
    out = np.concatenate([res.results[c]["out"] for c in range(NCORE)],
                         axis=0)
    return out.reshape(1, 16384, D).astype(np.float32)


if __name__ == "__main__":
    build_nc()
    print("build ok")
